# version 02
# baseline (speedup 1.0000x reference)
import numpy as np
import ml_dtypes
from contextlib import ExitStack
import concourse.bass as bass
import concourse.mybir as mybir
from concourse.bass_utils import run_bass_kernel_spmd

F32 = mybir.dt.float32
BF16 = mybir.dt.bfloat16
AF = mybir.ActivationFunctionType
ALU = mybir.AluOpType
AX = mybir.AxisListType

NCORE = 8
TOK = 2048
NT = TOK // 128
D = 1024
DC = D // 128
SLOTS = 2048
NEXP = 64
PSL = 32
EL = NEXP // NCORE
NHEAD = 16
DH = 64
L4_DEBUG = False
SEQ = 8192
EPS = 1e-5


class Sched:

    def __init__(self):
        self.ops = []
        self.counts = {}

    AUTO_SIG = {"vector": "dve", "scalar": "act", "gpsimd": "pool"}

    def tok(self, sem):
        return (sem, self.counts.get(sem, 0))

    def do(self, eng, fn, after=(), sig=None, inc=1, chain=True):
        auto = self.AUTO_SIG.get(eng)
        if sig is None and auto is not None:
            sig = auto
        after = [t for t in after if t is not None]
        if chain and auto is not None and self.counts.get(auto, 0) > 0:
            after.append((auto, self.counts[auto]))
        after = tuple(after)
        for s, v in after:
            assert v <= self.counts.get(s, 0), (s, v, self.counts.get(s, 0))
        self.ops.append((eng, after, fn, sig or None, inc if sig else 0))
        if sig:
            self.counts[sig] = self.counts.get(sig, 0) + inc
            return (sig, self.counts[sig])
        return None

    def dma(self, eng, out, in_, after=(), sig="dma", **kw):
        return self.do(eng, lambda e: e.dma_start(out=out, in_=in_, **kw),
                       after=after, sig=sig, inc=16, chain=False)

    def wait(self, eng, after):
        self.do(eng, lambda e: None, after=after, sig=False, chain=False)

    def build(self, nc, stack):
        sems = {n: stack.enter_context(nc.semaphore("s_" + n)) for n in self.counts}
        per_eng = {}
        for eng, after, fn, sig, inc in self.ops:
            per_eng.setdefault(eng, []).append((after, fn, sig, inc))
        block = stack.enter_context(nc.Block())

        def mk(items):
            def body(e):
                waited = {}
                for after, fn, sig, inc in items:
                    for s, v in after:
                        if waited.get(s, 0) < v:
                            e.wait_ge(sems[s], v)
                            waited[s] = v
                    ins = fn(e)
                    if sig and ins is not None:
                        ins.then_inc(sems[sig], inc)
            return body
        for eng_name, items in per_eng.items():
            getattr(block, eng_name)(mk(items))


def emit_ln(S, x_ap, h_out_ap, scr, guard, out_dve_sig="dve"):
    mu, var, s2, rstd, xsq = scr["mu"], scr["var"], scr["s2"], scr["rstd"], scr["xsq"]
    S.do("vector", lambda e: e.reduce_sum(out=mu, in_=x_ap, axis=AX.X), after=guard)
    S.do("vector", lambda e: e.tensor_scalar_mul(out=mu, in0=mu, scalar1=1.0 / x_ap.shape[-1]))
    S.do("vector", lambda e: e.tensor_mul(out=xsq, in0=x_ap, in1=x_ap))
    S.do("vector", lambda e: e.reduce_sum(out=s2, in_=xsq, axis=AX.X))
    S.do("vector", lambda e: e.tensor_mul(out=var, in0=mu, in1=mu))
    t_v = S.do("vector", lambda e: e.scalar_tensor_tensor(
        out=var, in0=s2, scalar=1.0 / x_ap.shape[-1], in1=var,
        op0=ALU.mult, op1=ALU.subtract), sig="dve")
    S.do("scalar", lambda e: e.activation(out=s2, in_=var, func=AF.Ln, bias=scr["eps"]),
         after=[t_v])
    t_act = S.do("scalar", lambda e: e.activation(out=rstd, in_=s2, func=AF.Exp,
                                                  scale=-0.5), sig="act")
    t_h = S.do("vector", lambda e: e.tensor_scalar(
        out=h_out_ap, in0=x_ap, scalar1=mu, scalar2=rstd,
        op0=ALU.subtract, op1=ALU.mult), after=[t_act], sig=out_dve_sig)
    return t_h, t_act


def emit_ffn_A_tile(S, m, b, x_tile_ap, after_x):
    hm = b["h_all"][:, m * D:(m + 1) * D]
    t_h, t_ln_act = emit_ln(S, x_tile_ap, hm, b["lnscr"],
                            list(after_x) + [b["ln_free"]])
    b["ln_free"] = t_ln_act
    hT = b["hT"]
    ev_toks = []
    tr_toks = []
    for k in range(DC):
        ps = b["tr_ps"][k % 2]
        after = [t_h] + ([ev_toks[k - 2]] if k >= 2 else [b["trps_free"][k % 2]])
        t_tr = S.do("tensor", lambda e, k=k, ps=ps: e.transpose(
            ps[:, 0:128], b["h_all"][:, m * D + k * 128: m * D + (k + 1) * 128],
            b["eye"]), after=after, sig="pe")
        tr_toks.append(t_tr)
        ev = S.do("vector", lambda e, k=k, ps=ps: e.tensor_copy(
            out=hT[:, k * 128:(k + 1) * 128], in_=ps[:, 0:128]),
            after=[t_tr], sig="dve")
        ev_toks.append(ev)
    b["trps_free"] = [ev_toks[DC - 2], ev_toks[DC - 1]]
    lp = b["logits_ps"]
    for k in range(DC):
        for j in range(4):
            after = []
            if k == 0:
                after = [ev_toks[k], b["logits_free"]] + ([b["t_ones"]] if m == 0 else [])
            elif j == 0:
                after = [ev_toks[k]]
            S.do("tensor", lambda e, k=k, j=j: e.matmul(
                lp[:, j * 512:(j + 1) * 512],
                hT[:, k * 128:(k + 1) * 128],
                b["phiT"][:, k * SLOTS + j * 512: k * SLOTS + (j + 1) * 512],
                start=(k == 0), stop=False), after=after)
    t_mm = None
    for j in range(4):
        t_mm = S.do("tensor", lambda e, j=j: e.matmul(
            lp[:, j * 512:(j + 1) * 512], b["ones1"],
            b["phib"][:, j * 512:(j + 1) * 512], start=False, stop=True), sig="pe")
    t_negm = S.do("vector", lambda e: e.reduce_max(
        out=b["negm"], in_=lp[:], axis=AX.X, negate=True), after=[t_mm], sig="dve")
    E = b["E_t"][m % 2]
    t_E = S.do("scalar", lambda e: e.activation(
        out=E[:], in_=lp[:], func=AF.Exp, bias=b["negm"]),
        after=[t_negm] + b["E_free"][m % 2], sig="act")
    b["logits_free"] = t_E
    Ev = E[:].rearrange("p (n q) -> p n q", n=NEXP)
    Evt = E[:].rearrange("p (n q) -> p q n", n=NEXP)
    S.do("vector", lambda e: e.reduce_sum(out=b["sp"], in_=Ev, axis=AX.X), after=[t_E])
    S.do("vector", lambda e: e.reciprocal(out=b["rp"], in_=b["sp"]))
    S.do("vector", lambda e: e.reduce_sum(out=b["sn"], in_=Evt, axis=AX.X))
    t_rn = S.do("vector", lambda e: e.reciprocal(out=b["rn"], in_=b["sn"]), sig="dve")
    Dv = b["D_all"][:, m * SLOTS:(m + 1) * SLOTS].rearrange("p (n q) -> p n q", n=NEXP)
    t_D = S.do("vector", lambda e: e.tensor_tensor(
        out=Dv, in0=Ev, in1=b["rp"].unsqueeze(2).broadcast_to([128, NEXP, PSL]),
        op=ALU.mult), after=[t_rn], sig="dve")
    C = b["C_t"][m % 2]
    Cv = C[:].rearrange("p (n q) -> p n q", n=NEXP)
    t_C = S.do("gpsimd", lambda e: e.tensor_tensor(
        out=Cv, in0=Ev, in1=b["rn"].unsqueeze(1).broadcast_to([128, NEXP, PSL]),
        op=ALU.mult), after=[t_rn, b["C_free"][m % 2]], sig="pool")
    t_Cout = S.dma("sync", b["C_out"][m * 128:(m + 1) * 128, :], C[:],
                   after=[t_C], sig="dmaout")
    b["C_free"][m % 2] = t_Cout
    b["E_free"][m % 2] = [t_D, t_C]
    return t_h


def emit_ffn_AB(S, nc, st, b, get_x_tile, C_out, Xp_out):
    sb = lambda name, shape, dt: st.enter_context(nc.sbuf_tensor(name, shape, dt))
    uid = b["uid"]
    b["h_all"] = sb(f"h_all{uid}", [128, NT * D], BF16)[:]
    b["hT"] = sb(f"hT{uid}", [128, D], BF16)[:]
    b["D_all"] = sb(f"D_all{uid}", [128, NT * SLOTS], BF16)[:]
    b["E_t"] = [sb(f"E{uid}_{i}", [128, SLOTS], BF16) for i in range(2)]
    b["C_t"] = [sb(f"Ct{uid}_{i}", [128, SLOTS], BF16) for i in range(2)]
    b["negm"] = sb(f"negm{uid}", [128, 1], F32)[:]
    b["sp"] = sb(f"sp{uid}", [128, NEXP], F32)[:]
    b["rp"] = sb(f"rp{uid}", [128, NEXP], F32)[:]
    b["sn"] = sb(f"sn{uid}", [128, PSL], F32)[:]
    b["rn"] = sb(f"rn{uid}", [128, PSL], F32)[:]
    b["C_out"] = C_out
    b["C_free"] = [None, None]
    b["E_free"] = [[], []]
    b["logits_free"] = None
    b["ln_free"] = None
    b["trps_free"] = [None, None]

    ps_stack = ExitStack()
    b["tr_ps"] = [ps_stack.enter_context(nc.psum_tensor(f"tr{uid}_{i}", [128, 1024], BF16))
                  for i in range(2)]
    b["logits_ps"] = ps_stack.enter_context(nc.psum_tensor(f"lg{uid}", [128, SLOTS], F32))[:]

    for m in range(NT):
        x_ap, after_x = get_x_tile(m)
        t_h = emit_ffn_A_tile(S, m, b, x_ap, after_x)
        if "x_done" in b:
            b["x_done"](m, t_h)

    t_lastA = [S.tok("dve"), S.tok("pool")]
    ps_stack.close()

    Xps = [st.enter_context(nc.psum_tensor(f"Xps{uid}_{j}", [128, D], F32))
           for j in range(4)]
    Xev = [sb(f"Xev{uid}_{j}", [128, D], F32) for j in range(2)]
    bank_free = [None] * 4
    xev_dma = [None, None]
    for s in range(4):
        for m in range(NT):
            for j in range(4):
                slot0 = s * 512 + j * 128
                after = []
                if m == 0:
                    after = (t_lastA if s == 0 else []) + [bank_free[j]]
                for half in range(2):
                    S.do("tensor", lambda e, m=m, j=j, half=half, slot0=slot0: e.matmul(
                        Xps[j][:, half * 512:(half + 1) * 512],
                        b["D_all"][:, m * SLOTS + slot0: m * SLOTS + slot0 + 128],
                        b["h_all"][:, m * D + half * 512: m * D + (half + 1) * 512],
                        start=(m == 0), stop=(m == NT - 1)),
                        after=after if half == 0 else (),
                        sig="pe" if (m == NT - 1 and half == 1) else None)
        t_s = S.tok("pe")
        for j in range(4):
            t_ev = S.do("vector", lambda e, j=j: e.tensor_copy(
                out=Xev[j % 2][:], in_=Xps[j][:]), after=[t_s, xev_dma[j % 2]],
                sig="dve")
            bank_free[j] = t_ev
            xev_dma[j % 2] = S.dma(
                "sync", Xp_out[s * 512 + j * 128: s * 512 + (j + 1) * 128, :],
                Xev[j % 2][:], after=[t_ev], sig="dmaout")
    return S.tok("dmaout")


def build_L1():
    nc = bass.Bass("TRN2", target_bir_lowering=False)
    x_d = nc.dram_tensor("x", [TOK, D], F32, kind="ExternalInput").ap()
    phiT_d = nc.dram_tensor("phiT", [D, SLOTS], F32, kind="ExternalInput").ap()
    phib_d = nc.dram_tensor("phib", [1, SLOTS], F32, kind="ExternalInput").ap()
    eye_d = nc.dram_tensor("eye", [128, 128], BF16, kind="ExternalInput").ap()
    Xp_d = nc.dram_tensor("Xp", [SLOTS, D], F32, kind="ExternalOutput").ap()
    C_d = nc.dram_tensor("C", [TOK, SLOTS], BF16, kind="ExternalOutput").ap()

    S = Sched()
    with ExitStack() as st:
        sb = lambda name, shape, dt: st.enter_context(nc.sbuf_tensor(name, shape, dt))
        xt = [sb(f"xt{i}", [128, D], F32) for i in range(2)]
        phiT = sb("phiT_s", [128, DC * SLOTS], BF16)
        phib = sb("phib_s", [1, SLOTS], BF16)
        ones1 = sb("ones1", [1, 128], BF16)
        eye = sb("eye_s", [128, 128], BF16)
        lnscr = {k: sb("ln_" + k, [128, 1], F32)[:] for k in ("mu", "var", "s2", "rstd")}
        lnscr["xsq"] = sb("ln_xsq", [128, D], F32)[:]
        lnscr["eps"] = sb("ln_eps", [128, 1], F32)[:]
        S.do("vector", lambda e: e.memset(lnscr["eps"], EPS))

        b = dict(uid="a", phiT=phiT[:], phib=phib[:], ones1=ones1[:], eye=eye[:],
                 lnscr=lnscr)
        b["t_ones"] = S.do("vector", lambda e: e.memset(ones1[:], 1.0), sig="dve")
        t_eye = S.dma("sync", eye[:], eye_d, sig="ld")
        for k in range(DC):
            S.dma("gpsimd", phiT[:, k * SLOTS:(k + 1) * SLOTS],
                  phiT_d[k * 128:(k + 1) * 128, :], sig="ldw")
        S.dma("gpsimd", phib[:], phib_d, sig="ldw")
        t_w = S.tok("ldw")

        x_free = [None, None]

        def get_x_tile(m):
            t_x = S.dma("sync", xt[m % 2][:], x_d[m * 128:(m + 1) * 128, :],
                        after=[x_free[m % 2]], sig="ldx")
            return xt[m % 2][:], [t_x, t_w, t_eye]

        def x_done(m, t_h):
            x_free[m % 2] = t_h
        b["x_done"] = x_done

        t_out = emit_ffn_AB(S, nc, st, b, get_x_tile, C_d, Xp_d)
        S.wait("sync", [t_out, S.tok("dmaout")])
        S.build(nc, st)
    return nc


def build_L2():
    nc = bass.Bass("TRN2", target_bir_lowering=False)
    Xp8_d = nc.dram_tensor("Xp8", [NCORE, EL * PSL, D], F32, kind="ExternalInput").ap()
    ew_d = nc.dram_tensor("ew", [EL, D, D], F32, kind="ExternalInput").ap()
    eb_d = nc.dram_tensor("eb", [EL, D], F32, kind="ExternalInput").ap()
    eye_d = nc.dram_tensor("eye", [128, 128], BF16, kind="ExternalInput").ap()
    Y_d = nc.dram_tensor("Ytl", [EL * PSL, D], F32, kind="ExternalOutput").ap()

    S = Sched()
    with ExitStack() as st:
        sb = lambda name, shape, dt: st.enter_context(nc.sbuf_tensor(name, shape, dt))
        xp = sb("xp_s", [128, NCORE * 2 * D], F32)
        xs = sb("xs", [128, 2 * D], F32)
        xn = sb("xn", [128, 2 * D], F32)
        xb = sb("xb", [128, 2 * D], BF16)
        XT = sb("XT", [128, 2 * DC * 128], BF16)
        ew = [sb(f"ew{i}", [128, DC * D], BF16) for i in range(2)]
        ebr_s = sb("ebr_s", [1, EL * D], BF16)
        ebr = ebr_s
        ones1 = sb("ones1", [1, 128], BF16)
        eye = sb("eye_s", [128, 128], BF16)
        Yev = sb("Yev", [32, EL * D], F32)
        lnscr = {k: sb("ln_" + k, [128, 1], F32)[:] for k in ("mu", "var", "s2", "rstd")}
        lnscr["xsq"] = sb("ln_xsq", [128, D], F32)[:]
        lnscr["eps"] = sb("ln_eps", [128, 1], F32)[:]
        S.do("vector", lambda e: e.memset(lnscr["eps"], EPS))

        tr_ps = [st.enter_context(nc.psum_tensor(f"tr{i}", [128, 1024], BF16))
                 for i in range(2)]
        Y_ps = [st.enter_context(nc.psum_tensor(f"Y{i}", [32, D], F32))
                for i in range(2)]

        t_ones = S.do("vector", lambda e: e.memset(ones1[:], 1.0), sig="dve")
        t_eye = S.dma("sync", eye[:], eye_d, sig="ld")
        for src in range(NCORE):
            for half in range(2):
                S.dma("sync", xp[:, (src * 2 + half) * D:(src * 2 + half + 1) * D],
                      Xp8_d[src, half * 128:(half + 1) * 128, :], sig="ldx")
        t_x = S.tok("ldx")
        for n in range(EL):
            S.dma("gpsimd", ebr[:, n * D:(n + 1) * D], eb_d[n:n + 1, :], sig="ldb")
        t_eb = S.tok("ldb")
        ew_ready = [None] * EL
        for n in range(2):
            for k in range(DC):
                S.dma("gpsimd", ew[n][:, k * D:(k + 1) * D],
                      ew_d[n, k * 128:(k + 1) * 128, :], sig="ldw")
            ew_ready[n] = S.tok("ldw")

        ln_free = None
        t_tanh = [None, None]
        ev_done = []
        for half in range(2):
            hs = slice(half * D, (half + 1) * D)
            S.do("vector", lambda e, hs=hs: e.tensor_add(
                out=xs[:, hs], in0=xp[:, hs],
                in1=xp[:, 2 * D + hs.start: 2 * D + hs.stop]), after=[t_x])
            for src in range(2, NCORE):
                S.do("vector", lambda e, hs=hs, src=src: e.tensor_add(
                    out=xs[:, hs], in0=xs[:, hs],
                    in1=xp[:, src * 2 * D + hs.start: src * 2 * D + hs.stop]))
            S.do("vector", lambda e, hs=hs: e.tensor_scalar(
                out=xs[:, hs], in0=xs[:, hs], scalar1=-33000.0, scalar2=65000.0,
                op0=ALU.max, op1=ALU.min))
            t_h, t_act = emit_ln(S, xs[:, hs], xn[:, hs], lnscr,
                                 [ln_free])
            t_t = S.do("scalar", lambda e, hs=hs: e.activation(
                out=xb[:, hs], in_=xn[:, hs], func=AF.Tanh), after=[t_h], sig="act")
            ln_free = t_t
            t_tanh[half] = t_t
            for k in range(DC):
                ps = tr_ps[k % 2]
                after = [t_t, t_eye]
                if len(ev_done) >= 2:
                    after.append(ev_done[-2])
                t_tr = S.do("tensor", lambda e, hs=hs, k=k, ps=ps: e.transpose(
                    ps[:, 0:128], xb[:, hs.start + k * 128: hs.start + (k + 1) * 128],
                    eye[:]), after=after, sig="pe")
                ev = S.do("vector", lambda e, half=half, k=k, ps=ps: e.tensor_copy(
                    out=XT[:, (half * DC + k) * 128:(half * DC + k + 1) * 128],
                    in_=ps[:, 0:128]), after=[t_tr], sig="dve")
                ev_done.append(ev)
        t_XT = S.tok("dve")

        y_free = [None, None]
        for n in range(EL):
            half, j = divmod(n, 4)
            ps = Y_ps[n % 2]
            base = (half * DC) * 128
            for k in range(DC):
                lhs = XT[:, (half * DC + k) * 128 + j * 32:
                         (half * DC + k) * 128 + (j + 1) * 32]
                for hh in range(2):
                    after = ()
                    if k == 0 and hh == 0:
                        after = [t_XT, ew_ready[n], y_free[n % 2], t_ones, t_eb]
                    S.do("tensor", lambda e, k=k, hh=hh, lhs=lhs, ps=ps, n=n: e.matmul(
                        ps[:, hh * 512:(hh + 1) * 512], lhs,
                        ew[n % 2][:, k * D + hh * 512: k * D + (hh + 1) * 512],
                        start=(k == 0), stop=False), after=after)
            t_pe = None
            for hh in range(2):
                t_pe = S.do("tensor", lambda e, n=n, hh=hh, ps=ps: e.matmul(
                    ps[:, hh * 512:(hh + 1) * 512], ones1[:, 0:32],
                    ebr[:, n * D + hh * 512: n * D + (hh + 1) * 512],
                    start=False, stop=True), sig="pe")
            t_ev = S.do("vector", lambda e, n=n, ps=ps: e.tensor_copy(
                out=Yev[:, n * D:(n + 1) * D], in_=ps[:]), after=[t_pe], sig="dve")
            y_free[n % 2] = t_ev
            if n + 2 < EL:
                for k in range(DC):
                    S.dma("gpsimd", ew[n % 2][:, k * D:(k + 1) * D],
                          ew_d[n + 2, k * 128:(k + 1) * 128, :],
                          after=[t_ev] if k == 0 else (), sig="ldw")
                ew_ready[n + 2] = S.tok("ldw")
            S.dma("sync", Y_d[n * PSL:(n + 1) * PSL, :],
                  Yev[0:PSL, n * D:(n + 1) * D], after=[t_ev], sig="dmaout")
        S.wait("sync", [S.tok("dmaout")])
        S.build(nc, st)
    return nc



def emit_ffn_D(S, nc, st, uid, C_d, Ytl_d, x_d, xout_d, tile_cb=None, psum_pp=2):
    sb = lambda name, shape, dt: st.enter_context(nc.sbuf_tensor(name, shape, dt))
    CT = sb(f"CT{uid}", [128, 16 * SLOTS], BF16)
    Ytl = sb(f"Ytl{uid}", [128, 16 * D], BF16)
    xt = [sb(f"xt{uid}{i}", [128, D], F32) for i in range(2)]
    x1t = [sb(f"x1t{uid}{i}", [128, D], F32) for i in range(2)]
    comb = [st.enter_context(nc.psum_tensor(f"comb{uid}{i}", [128, D], F32))
            for i in range(psum_pp)]
    for sc in range(16):
        S.dma("sync", CT[:, sc * SLOTS:(sc + 1) * SLOTS],
              C_d[0:TOK, sc * 128:(sc + 1) * 128], transpose=True, sig="ldct")
        S.dma("gpsimd", Ytl[:, sc * D:(sc + 1) * D],
              Ytl_d[sc * 128:(sc + 1) * 128, :], sig="ldyt")
    t_ct = S.tok("ldct")
    t_yt = S.tok("ldyt")
    x_free = [None, None]
    x1_free = [None, None]
    comb_free = [None] * psum_pp
    for m in range(NT):
        t_x = S.dma("sync", xt[m % 2][:], x_d[m * 128:(m + 1) * 128, :],
                    after=[x_free[m % 2]], sig="ldx" + uid)
        ps = comb[m % psum_pp]
        for sc in range(16):
            for hh in range(2):
                after = ()
                if sc == 0 and hh == 0:
                    after = [t_ct, t_yt, comb_free[m % psum_pp]]
                S.do("tensor", lambda e, m=m, sc=sc, hh=hh, ps=ps: e.matmul(
                    ps[:, hh * 512:(hh + 1) * 512],
                    CT[:, sc * SLOTS + m * 128: sc * SLOTS + (m + 1) * 128],
                    Ytl[:, sc * D + hh * 512: sc * D + (hh + 1) * 512],
                    start=(sc == 0), stop=(sc == 15)),
                    after=after, sig="pe" if (sc == 15 and hh == 1) else None)
        t_mm = S.tok("pe")
        t_x1 = S.do("vector", lambda e, m=m, ps=ps: e.scalar_tensor_tensor(
            out=x1t[m % 2][:], in0=ps[:], scalar=0.5, in1=xt[m % 2][:],
            op0=ALU.mult, op1=ALU.add),
            after=[t_mm, t_x, x_free[m % 2], x1_free[m % 2]])
        comb_free[m % psum_pp] = t_x1
        x_free[m % 2] = t_x1
        t_o = S.dma("sync", xout_d[m * 128:(m + 1) * 128, :], x1t[m % 2][:],
                    after=[t_x1], sig="dmaout")
        x1_free[m % 2] = t_o
        if tile_cb is not None:
            tile_cb(m, x1t[m % 2][:], t_x1)
    return S.tok("dmaout")


def build_L3():
    nc = bass.Bass("TRN2", target_bir_lowering=False)
    x_d = nc.dram_tensor("x", [TOK, D], F32, kind="ExternalInput").ap()
    C_d = nc.dram_tensor("C", [TOK, SLOTS], BF16, kind="ExternalInput").ap()
    Ytl_d = nc.dram_tensor("Ytl", [SLOTS, D], F32, kind="ExternalInput").ap()
    W_d = {w: nc.dram_tensor(w, [D, D], F32, kind="ExternalInput").ap()
           for w in ("WqT", "WkT", "WvT")}
    b_d = {w: nc.dram_tensor(w, [1, D], F32, kind="ExternalInput").ap()
           for w in ("bq", "bk", "bv")}
    cos_d = nc.dram_tensor("cos", [TOK, 32], F32, kind="ExternalInput").ap()
    sin_d = nc.dram_tensor("sin2", [TOK, 64], F32, kind="ExternalInput").ap()
    eye_d = nc.dram_tensor("eye", [128, 128], BF16, kind="ExternalInput").ap()
    eyef_d = nc.dram_tensor("eyef", [128, 128], F32, kind="ExternalInput").ap()
    x1_d = nc.dram_tensor("x1", [TOK, D], F32, kind="ExternalOutput").ap()
    q_d = nc.dram_tensor("q", [TOK, D], BF16, kind="ExternalOutput").ap()
    k_d = nc.dram_tensor("k", [TOK, D], BF16, kind="ExternalOutput").ap()
    v_d = nc.dram_tensor("v", [8, TOK, 130], BF16, kind="ExternalOutput").ap()
    nrm_d = nc.dram_tensor("norms", [NHEAD, 2], F32, kind="ExternalOutput").ap()
    bvscr_d = nc.dram_tensor("bvscr", [1, D], F32, kind="Internal").ap()

    S = Sched()
    with ExitStack() as st:
        sb = lambda name, shape, dt: st.enter_context(nc.sbuf_tensor(name, shape, dt))
        W = {w: sb(w + "_s", [128, DC * D], BF16) for w in ("WqT", "WkT", "WvT")}
        br = {w: sb(w + "_s", [1, D], BF16) for w in ("bq", "bk", "bv")}
        bvb = sb("bvb", [128, D], F32)
        ones1 = sb("ones1", [1, 128], BF16)
        eye = sb("eye_s", [128, 128], BF16)
        eyef = sb("eyef_s", [128, 128], F32)
        cos_sb = sb("cos_sb", [128, NT * 32], F32)
        sin_sb = sb("sin_sb", [128, NT * 64], F32)
        g_t = sb("g_t", [128, D], BF16)
        gT = sb("gT", [128, D], BF16)
        t1f = sb("t1f", [128, D], F32)
        t2f = sb("t2f", [128, D], F32)
        qk_bf = [sb(f"qkbf{i}", [128, D], BF16) for i in range(2)]
        v_t = [sb(f"vt{i}", [128, 16 * 65], BF16) for i in range(2)]
        sqs = sb("sqs", [128, D], F32)
        red16 = sb("red16", [128, NHEAD], F32)
        qn_run = sb("qn_run", [128, NHEAD], F32)
        kn_run = sb("kn_run", [128, NHEAD], F32)
        nrm_sb = sb("nrm_sb", [16, 2], F32)
        lnscr = {k2: sb("ln_" + k2, [128, 1], F32)[:] for k2 in ("mu", "var", "s2", "rstd")}
        lnscr["xsq"] = sb("ln_xsq", [128, D], F32)[:]
        lnscr["eps"] = sb("ln_eps", [128, 1], F32)[:]
        S.do("vector", lambda e: e.memset(lnscr["eps"], EPS))
        t_ones = S.do("vector", lambda e: e.memset(ones1[:], 1.0))
        S.do("vector", lambda e: e.memset(qn_run[:], 0.0))
        t_nrm0 = S.do("vector", lambda e: e.memset(kn_run[:], 0.0))

        t_eye = S.dma("sync", eye[:], eye_d, sig="ld")
        S.dma("sync", eyef[:], eyef_d, sig="ld")
        for m in range(NT):
            S.dma("sync", cos_sb[:, m * 32:(m + 1) * 32],
                  cos_d[m * 128:(m + 1) * 128, :], sig="ld")
            S.dma("sync", sin_sb[:, m * 64:(m + 1) * 64],
                  sin_d[m * 128:(m + 1) * 128, :], sig="ld")
        BIAS = {"WqT": "bq", "WkT": "bk", "WvT": "bv"}
        for w in ("WqT", "WkT", "WvT"):
            for kk in range(DC):
                S.dma("gpsimd", W[w][:, kk * D:(kk + 1) * D],
                      W_d[w][kk * 128:(kk + 1) * 128, :], sig="ldw")
            S.dma("gpsimd", br[BIAS[w]][:], b_d[BIAS[w]], sig="ldw")
        t_bv = S.dma("sync", bvscr_d, b_d["bv"], sig="ld")
        S.dma("sync", bvb[:], bvscr_d.broadcast_to([128, D]), after=[t_bv], sig="ld")
        t_ld = S.tok("ld")
        t_w = S.tok("ldw")

        tr_ps = [st.enter_context(nc.psum_tensor(f"trq{i}", [128, 1024], BF16))
                 for i in range(2)]
        qkv_ps = [st.enter_context(nc.psum_tensor(f"qkv{i}", [128, D], F32))
                  for i in range(2)]

        state = dict(trps_free=[None, None], qkv_free=[None, None],
                     qk_free=[None, None], vt_free=[None, None], ln_free=None,
                     gT_free=None)

        def qkv_tile(m, x1_ap, t_x1):
            t_g, t_act = emit_ln(S, x1_ap, g_t[:], lnscr,
                                 [t_x1, state["ln_free"]])
            state["ln_free"] = t_act
            ev_toks = []
            for kk in range(DC):
                ps = tr_ps[kk % 2]
                after = [t_g, t_eye] +                     ([ev_toks[kk - 2]] if kk >= 2 else [state["trps_free"][kk % 2]])
                if kk == 0:
                    after.append(state["gT_free"])
                t_tr = S.do("tensor", lambda e, kk=kk, ps=ps: e.transpose(
                    ps[:, 0:128], g_t[:, kk * 128:(kk + 1) * 128], eye[:]),
                    after=after, sig="pe")
                ev = S.do("vector", lambda e, kk=kk, ps=ps: e.tensor_copy(
                    out=gT[:, kk * 128:(kk + 1) * 128], in_=ps[:, 0:128]),
                    after=[t_tr])
                ev_toks.append(ev)
            state["trps_free"] = [ev_toks[DC - 2], ev_toks[DC - 1]]
            t_gT = ev_toks[-1]
            for pi, w in enumerate(("WqT", "WkT", "WvT")):
                ps = qkv_ps[pi % 2]
                for kk in range(DC):
                    for hh in range(2):
                        after = ()
                        if kk == 0 and hh == 0:
                            after = [t_gT, state["qkv_free"][pi % 2],
                                     t_w if m == 0 else None,
                                     t_ones if m == 0 else None]
                        S.do("tensor", lambda e, w=w, kk=kk, hh=hh, ps=ps: e.matmul(
                            ps[:, hh * 512:(hh + 1) * 512],
                            gT[:, kk * 128:(kk + 1) * 128],
                            W[w][:, kk * D + hh * 512: kk * D + (hh + 1) * 512],
                            start=(kk == 0), stop=False), after=after)
                t_p = None
                for hh in range(2):
                    t_p = S.do("tensor", lambda e, w=w, hh=hh, ps=ps: e.matmul(
                        ps[:, hh * 512:(hh + 1) * 512], ones1[:],
                        br[{"WqT": "bq", "WkT": "bk", "WvT": "bv"}[w]][:, hh * 512:(hh + 1) * 512],
                        start=False, stop=True), sig="pe")
                if w == "WvT":
                    vt = v_t[m % 2]
                    t_ms = S.do("vector", lambda e, vt=vt: e.memset(vt[:], 1.0),
                                after=[state["vt_free"][m % 2]])
                    t_v = S.do("vector", lambda e, vt=vt, ps=ps: e.tensor_copy(
                        out=vt[:].rearrange("p (h g) -> p h g", g=65)[:, :, 0:64],
                        in_=ps[:].rearrange("p (h g) -> p h g", g=64)),
                        after=[t_p, t_ms])
                    for hg in range(8):
                        S.dma("sync", v_d[hg, m * 128:(m + 1) * 128, :],
                              vt[:, hg * 130:(hg + 1) * 130], after=[t_v],
                              sig="dmaout")
                    state["vt_free"][m % 2] = t_v
                    state["qkv_free"][pi % 2] = t_v
                else:
                    qb = qk_bf[pi]
                    Pv = ps[:].rearrange("p (h s f) -> p h s f", h=NHEAD, s=2)
                    cosv = cos_sb[:, m * 32:(m + 1) * 32].unsqueeze(1).unsqueeze(2) \
                        .broadcast_to([128, NHEAD, 2, 32])
                    sinv = sin_sb[:, m * 64:(m + 1) * 64] \
                        .rearrange("p (s f) -> p s f", s=2).unsqueeze(1) \
                        .broadcast_to([128, NHEAD, 2, 32])
                    t1v = t1f[:].rearrange("p (h s f) -> p h s f", h=NHEAD, s=2)
                    t2v = t2f[:].rearrange("p (h s f) -> p h s f", h=NHEAD, s=2)
                    S.do("vector", lambda e, Pv=Pv, cosv=cosv, t1v=t1v: e.tensor_tensor(
                        out=t1v, in0=Pv, in1=cosv, op=ALU.mult), after=[t_p, t_ld])
                    S.do("vector", lambda e, Pv=Pv, sinv=sinv, t2v=t2v: e.tensor_tensor(
                        out=t2v[:, :, 0, :], in0=Pv[:, :, 1, :],
                        in1=sinv[:, :, 0, :], op=ALU.mult))
                    S.do("vector", lambda e, Pv=Pv, sinv=sinv, t2v=t2v: e.tensor_tensor(
                        out=t2v[:, :, 1, :], in0=Pv[:, :, 0, :],
                        in1=sinv[:, :, 1, :], op=ALU.mult))
                    t_qb = S.do("vector", lambda e, qb=qb: e.tensor_add(
                        out=qb[:], in0=t1f[:], in1=t2f[:]),
                        after=[state["qk_free"][pi]])
                    state["qkv_free"][pi % 2] = t_qb
                    t_sq = S.do("gpsimd", lambda e, qb=qb: e.tensor_tensor(
                        out=sqs[:], in0=qb[:], in1=qb[:], op=ALU.mult), after=[t_qb])
                    S.do("vector", lambda e: e.reduce_sum(
                        out=red16[:], in_=sqs[:].rearrange("p (h f) -> p h f", h=NHEAD),
                        axis=AX.X), after=[t_sq])
                    run = qn_run if w == "WqT" else kn_run
                    S.do("vector", lambda e, run=run: e.tensor_max(
                        out=run[:], in0=run[:], in1=red16[:]),
                        after=[t_nrm0] if m == 0 else ())
                    out_d = q_d if w == "WqT" else k_d
                    t_dq = S.dma("sync", out_d[m * 128:(m + 1) * 128, :], qb[:],
                                 after=[t_qb], sig="dmaout")
                    state["qk_free"][pi] = t_dq

        t_dout = emit_ffn_D(S, nc, st, "d", C_d, Ytl_d, x_d, x1_d,
                            tile_cb=qkv_tile, psum_pp=1)
        t_last = [S.tok("dve"), S.tok("pool"), S.tok("pe")]
        for i, run in enumerate((qn_run, kn_run)):
            t_tr = S.do("tensor", lambda e, run=run: e.transpose(
                qkv_ps[1][0:16, 0:128], run[:], eyef[:]),
                after=t_last + [S.tok("dve")], sig="pe")
            S.do("vector", lambda e, i=i: e.reduce_max(
                out=nrm_sb[:, i:i + 1], in_=qkv_ps[1][0:16, 0:128], axis=AX.X),
                after=[t_tr])
        t_n = S.tok("dve")
        S.dma("sync", nrm_d, nrm_sb[:], after=[t_n], sig="dmaout")
        S.wait("sync", [S.tok("dmaout")])
        S.build(nc, st)
    return nc


def build_L6():
    nc = bass.Bass("TRN2", target_bir_lowering=False)
    x_d = nc.dram_tensor("x", [TOK, D], F32, kind="ExternalInput").ap()
    C_d = nc.dram_tensor("C", [TOK, SLOTS], BF16, kind="ExternalInput").ap()
    Ytl_d = nc.dram_tensor("Ytl", [SLOTS, D], F32, kind="ExternalInput").ap()
    x3_d = nc.dram_tensor("x3", [TOK, D], F32, kind="ExternalOutput").ap()
    S = Sched()
    with ExitStack() as st:
        t_out = emit_ffn_D(S, nc, st, "f", C_d, Ytl_d, x_d, x3_d, psum_pp=2)
        S.wait("sync", [S.tok("dmaout")])
        S.build(nc, st)
    return nc



def build_L4():
    nc = bass.Bass("TRN2", target_bir_lowering=False)
    x1_d = nc.dram_tensor("x1", [TOK, D], F32, kind="ExternalInput").ap()
    q_d = nc.dram_tensor("q", [TOK, D], BF16, kind="ExternalInput").ap()
    K_d = nc.dram_tensor("K", [SEQ, D], BF16, kind="ExternalInput").ap()
    V_d = nc.dram_tensor("V", [8, SEQ, 130], BF16, kind="ExternalInput").ap()
    qn_d = nc.dram_tensor("qn", [1, NHEAD], F32, kind="ExternalInput").ap()
    kn4_d = nc.dram_tensor("kn4", [1, 4 * NHEAD], F32, kind="ExternalInput").ap()
    WoT_d = nc.dram_tensor("WoT", [D, D], F32, kind="ExternalInput").ap()
    bo_d = nc.dram_tensor("bo", [1, D], F32, kind="ExternalInput").ap()
    phiT_d = nc.dram_tensor("phiT", [D, SLOTS], F32, kind="ExternalInput").ap()
    phib_d = nc.dram_tensor("phib", [1, SLOTS], F32, kind="ExternalInput").ap()
    eye_d = nc.dram_tensor("eye", [128, 128], BF16, kind="ExternalInput").ap()
    x2_d = nc.dram_tensor("x2", [TOK, D], F32, kind="ExternalOutput").ap()
    Xp2_d = nc.dram_tensor("Xp2", [SLOTS, D], F32, kind="ExternalOutput").ap()
    C2_d = nc.dram_tensor("C2", [TOK, SLOTS], BF16, kind="ExternalOutput").ap()
    nscr_d = nc.dram_tensor("nscr", [1, NHEAD], F32, kind="Internal").ap()
    aTdbg_d = (nc.dram_tensor("aTdbg", [128, DC * TOK], BF16,
                              kind="ExternalOutput").ap() if L4_DEBUG else None)
    nmdbg_d = (nc.dram_tensor("nmdbg", [128, NHEAD], F32,
                              kind="ExternalOutput").ap() if L4_DEBUG else None)
    rscr_d = [nc.dram_tensor(f"rscr{i}", [1, TOK // 2], F32, kind="Internal").ap()
              for i in range(2)]

    S = Sched()
    with ExitStack() as st:
        sb = lambda name, shape, dt: st.enter_context(nc.sbuf_tensor(name, shape, dt))
        negmb = sb("negmb", [128, NHEAD], F32)
        nrow = sb("nrow", [1, 5 * NHEAD], F32)
        aT = sb("aT", [128, DC * TOK], BF16)

        t_qn = S.dma("sync", nrow[:, 0:NHEAD], qn_d, sig="ld")
        t_kn = S.dma("sync", nrow[:, NHEAD:5 * NHEAD], kn4_d, sig="ld")
        knv = nrow[:, NHEAD:5 * NHEAD].rearrange("p (c h) -> p h c", c=4)
        t0 = S.do("vector", lambda e: e.reduce_max(
            out=negmb[0:1, 0:NHEAD], in_=knv, axis=AX.X), after=[t_qn, t_kn])
        t_mul = S.do("vector", lambda e: e.tensor_mul(
            out=negmb[0:1, 0:NHEAD], in0=negmb[0:1, 0:NHEAD], in1=nrow[:, 0:NHEAD]))
        S.do("scalar", lambda e: e.activation(
            out=negmb[0:1, 0:NHEAD], in_=negmb[0:1, 0:NHEAD], func=AF.Ln),
            after=[t_mul])
        S.do("scalar", lambda e: e.activation(
            out=negmb[0:1, 0:NHEAD], in_=negmb[0:1, 0:NHEAD], func=AF.Exp, scale=0.5))
        t_nm = S.do("vector", lambda e: e.tensor_scalar_mul(
            out=negmb[0:1, 0:NHEAD], in0=negmb[0:1, 0:NHEAD], scalar1=-0.125),
            after=[S.tok("act")])
        t_ns = S.dma("sync", nscr_d, negmb[0:1, 0:NHEAD], after=[t_nm], sig="ld")
        t_nb = S.dma("sync", negmb[:], nscr_d.broadcast_to([128, NHEAD]),
                     after=[t_ns], sig="ld")

        with ExitStack() as st2:
            sb2 = lambda name, shape, dt: st2.enter_context(nc.sbuf_tensor(name, shape, dt))
            kT = [sb2(f"kT{i}", [128, SEQ], BF16) for i in range(2)]
            qT = [sb2(f"qT{i}", [128, TOK // 2], BF16) for i in range(2)]
            vsb = [sb2(f"vsb{i}", [128, 64 * 130], BF16) for i in range(2)]
            E = [sb2(f"E{i}", [128, TOK // 2], BF16) for i in range(2)]
            rrow = [sb2(f"rrow{i}", [1, TOK // 2], F32) for i in range(2)]
            rb = [sb2(f"rb{i}", [64, TOK // 2], F32) for i in range(2)]
            s_ps = [st2.enter_context(nc.psum_tensor(f"sps{i}", [128, TOK // 2], F32))
                    for i in range(2)]
            av_ps = [st2.enter_context(nc.psum_tensor(f"av{i}", [65, TOK // 2], F32))
                     for i in range(2)]
            QW = TOK // 2

            buf_free = [None, None]
            av_free = [None, None]
            exp_done = [None, None]
            di = 0
            for qt in range(2):
                for hg in range(8):
                    bi = (qt * 8 + hg) % 2
                    t_kT = S.dma("sync", kT[bi][:],
                                 K_d[0:SEQ, hg * 128:(hg + 1) * 128],
                                 after=[buf_free[bi]], transpose=True, sig="ldk")
                    t_qT = S.dma("sync", qT[bi][:],
                                 q_d[qt * QW:(qt + 1) * QW, hg * 128:(hg + 1) * 128],
                                 after=[buf_free[bi]], transpose=True, sig="ldk")
                    t_v = S.dma("gpsimd",
                                vsb[bi][:].rearrange("p (kb f) -> p kb f", f=130),
                                V_d[hg].rearrange("(kb p) f -> p kb f", p=128),
                                after=[buf_free[bi]], sig="ldv")
                    t_loads = [t_kT, t_qT, t_v, t_nb]
                    for h2 in range(2):
                        h = 2 * hg + h2
                        av = av_ps[(qt * 16 + hg * 2 + h2) % 2]
                        avp = (qt * 16 + hg * 2 + h2) % 2
                        t_sc = [None, None]
                        for kb in range(64):
                            sp = s_ps[kb % 2]
                            after0 = [exp_done[kb % 2]]
                            if kb < 2:
                                after0 += t_loads
                            S.do("tensor", lambda e, kb=kb, sp=sp, h2=h2, bi=bi: e.matmul(
                                sp[:, 0:512],
                                kT[bi][h2 * 64:(h2 + 1) * 64, kb * 128:(kb + 1) * 128],
                                qT[bi][h2 * 64:(h2 + 1) * 64, 0:512],
                                start=True, stop=True), after=after0)
                            t_s = S.do("tensor", lambda e, kb=kb, sp=sp, h2=h2, bi=bi: e.matmul(
                                sp[:, 512:1024],
                                kT[bi][h2 * 64:(h2 + 1) * 64, kb * 128:(kb + 1) * 128],
                                qT[bi][h2 * 64:(h2 + 1) * 64, 512:1024],
                                start=True, stop=True), sig="pe")
                            t_e = S.do("scalar", lambda e, kb=kb, sp=sp, h=h: e.activation(
                                out=E[kb % 2][:], in_=sp[:], func=AF.Exp,
                                scale=0.125, bias=negmb[:, h:h + 1]),
                                after=[t_s], chain=False)
                            exp_done[kb % 2] = t_e
                            for hh in range(2):
                                after1 = []
                                if kb == 0 and hh == 0:
                                    after1 = [t_e, av_free[avp]]
                                elif hh == 0:
                                    after1 = [t_e]
                                S.do("tensor", lambda e, kb=kb, hh=hh, av=av, h2=h2, bi=bi: e.matmul(
                                    av[:, hh * 512:(hh + 1) * 512],
                                    vsb[bi][:].rearrange("p (kb f) -> p kb f", f=130)
                                    [:, kb, h2 * 65:(h2 + 1) * 65],
                                    E[kb % 2][:, hh * 512:(hh + 1) * 512],
                                    start=(kb == 0), stop=(kb == 63)),
                                    after=after1,
                                    sig="pe" if (kb == 63 and hh == 1) else None)
                        t_av = S.tok("pe")
                        rp = di % 2
                        t_rc = S.do("vector", lambda e, av=av, rp=rp: e.reciprocal(
                            out=rrow[rp][:], in_=av[64:65, :]), after=[t_av])
                        t_r1 = S.dma("sync", rscr_d[rp], rrow[rp][:], after=[t_rc],
                                     sig="ldr")
                        t_r2 = S.dma("sync", rb[rp][:],
                                     rscr_d[rp].broadcast_to([64, QW]),
                                     after=[t_r1], sig="ldr")
                        t_aT = S.do("vector", lambda e, av=av, rp=rp, h2=h2, hg=hg, qt=qt:
                                    e.tensor_tensor(
                                        out=aT[h2 * 64:(h2 + 1) * 64,
                                               hg * TOK + qt * QW: hg * TOK + (qt + 1) * QW],
                                        in0=av[0:64, :], in1=rb[rp][:], op=ALU.mult),
                                    after=[t_r2])
                        av_free[avp] = t_aT
                        di += 1
                    buf_free[bi] = S.tok("pe")
            t_attn = [S.tok("dve"), S.tok("pe")]
            if L4_DEBUG:
                S.dma("sync", aTdbg_d, aT[:], after=t_attn, sig="dmaout")
                S.dma("sync", nmdbg_d, negmb[:], after=t_attn, sig="dmaout")

        with ExitStack() as st3:
            sb3 = lambda name, shape, dt: st3.enter_context(nc.sbuf_tensor(name, shape, dt))
            WoT = sb3("WoT_s", [128, DC * D], BF16)
            bor = sb3("bor", [1, D], BF16)
            ones1o = sb3("ones1o", [1, 128], BF16)
            x1t = [sb3(f"x1t{i}", [128, D], F32) for i in range(2)]
            x2t = [sb3(f"x2t{i}", [128, D], F32) for i in range(2)]
            o_ps = [st3.enter_context(nc.psum_tensor(f"ops{i}", [128, D], F32))
                    for i in range(2)]
            attn_done = t_attn + [S.tok("act")]
            t_ones = S.do("vector", lambda e: e.memset(ones1o[:], 1.0),
                          after=attn_done)
            for kk in range(DC):
                S.dma("gpsimd", WoT[:, kk * D:(kk + 1) * D],
                      WoT_d[kk * 128:(kk + 1) * 128, :], after=attn_done, sig="ldw")
            S.dma("gpsimd", bor[:], bo_d, after=attn_done, sig="ldw")
            t_wo = S.tok("ldw")
            x1_free = [None, None]
            x2_free = [None, None]
            o_free = [None, None]
            x2_out = [None] * NT
            for m in range(NT):
                t_x1 = S.dma("sync", x1t[m % 2][:], x1_d[m * 128:(m + 1) * 128, :],
                             after=[x1_free[m % 2]], sig="ldx")
                ps = o_ps[m % 2]
                for kk in range(DC):
                    for hh in range(2):
                        after = ()
                        if kk == 0 and hh == 0:
                            after = t_attn + [t_wo, o_free[m % 2]] + \
                                ([t_ones] if m == 0 else [])
                        S.do("tensor", lambda e, m=m, kk=kk, hh=hh, ps=ps: e.matmul(
                            ps[:, hh * 512:(hh + 1) * 512],
                            aT[:, kk * TOK + m * 128: kk * TOK + (m + 1) * 128],
                            WoT[:, kk * D + hh * 512: kk * D + (hh + 1) * 512],
                            start=(kk == 0), stop=False), after=after)
                t_p = None
                for hh in range(2):
                    t_p = S.do("tensor", lambda e, hh=hh, ps=ps: e.matmul(
                        ps[:, hh * 512:(hh + 1) * 512], ones1o[:],
                        bor[:, hh * 512:(hh + 1) * 512], start=False, stop=True),
                        sig="pe")
                t_x2 = S.do("vector", lambda e, m=m, ps=ps: e.tensor_add(
                    out=x2t[m % 2][:], in0=ps[:], in1=x1t[m % 2][:]),
                    after=[t_p, t_x1, x2_free[m % 2]])
                o_free[m % 2] = t_x2
                x1_free[m % 2] = t_x2
                t_o = S.dma("sync", x2_d[m * 128:(m + 1) * 128, :], x2t[m % 2][:],
                            after=[t_x2], sig="dmaout")
                x2_free[m % 2] = t_o
                x2_out[m] = t_o

        oproj_done = [S.tok("pe"), S.tok("dve")]
        sb4 = lambda name, shape, dt: st.enter_context(nc.sbuf_tensor(name, shape, dt))
        xt = [sb4(f"xtf{i}", [128, D], F32) for i in range(2)]
        phiT = sb4("phiT_s", [128, DC * SLOTS], BF16)
        phib = sb4("phib_s", [1, SLOTS], BF16)
        ones1 = sb4("ones1f", [1, 128], BF16)
        eye = sb4("eye_s", [128, 128], BF16)
        lnscr = {k2: sb4("lnf_" + k2, [128, 1], F32)[:] for k2 in ("mu", "var", "s2", "rstd")}
        lnscr["xsq"] = sb4("lnf_xsq", [128, D], F32)[:]
        lnscr["eps"] = sb4("lnf_eps", [128, 1], F32)[:]
        S.do("vector", lambda e: e.memset(lnscr["eps"], EPS))
        b = dict(uid="f4", phiT=phiT[:], phib=phib[:], ones1=ones1[:], eye=eye[:],
                 lnscr=lnscr)
        b["t_ones"] = S.do("vector", lambda e: e.memset(ones1[:], 1.0))
        t_eye = S.dma("sync", eye[:], eye_d, after=oproj_done, sig="ld")
        for kk in range(DC):
            S.dma("gpsimd", phiT[:, kk * SLOTS:(kk + 1) * SLOTS],
                  phiT_d[kk * 128:(kk + 1) * 128, :], after=oproj_done, sig="ldw")
        S.dma("gpsimd", phib[:], phib_d, after=oproj_done, sig="ldw")
        t_w = S.tok("ldw")
        xf_free = [None, None]

        def get_x_tile(m):
            t_x = S.dma("sync", xt[m % 2][:], x2_d[m * 128:(m + 1) * 128, :],
                        after=[xf_free[m % 2], x2_out[m]] +
                        (oproj_done if m < 2 else []), sig="ldxf")
            return xt[m % 2][:], [t_x, t_w, t_eye]

        def x_done(m, t_h):
            xf_free[m % 2] = t_h
        b["x_done"] = x_done
        emit_ffn_AB(S, nc, st, b, get_x_tile, C2_d, Xp2_d)
        S.wait("sync", [S.tok("dmaout")])
        S.build(nc, st)
    return nc


def _prep_common(Wq, bq, Wk, bk, Wv, bv, Wo, bo, phi_w, phi_b, expert_w, expert_b):
    c = {}
    c["phiT"] = np.ascontiguousarray(phi_w.T)
    c["phib"] = np.ascontiguousarray(phi_b[None, :])
    c["WqT"] = np.ascontiguousarray(Wq.T)
    c["WkT"] = np.ascontiguousarray(Wk.T)
    c["WvT"] = np.ascontiguousarray(Wv.T)
    c["WoT"] = np.ascontiguousarray(Wo.T)
    c["bq"] = np.ascontiguousarray(bq[None, :])
    c["bk"] = np.ascontiguousarray(bk[None, :])
    c["bv"] = np.ascontiguousarray(bv[None, :])
    c["bo"] = np.ascontiguousarray(bo[None, :])
    c["ew"] = np.ascontiguousarray(expert_w)
    c["eb"] = np.ascontiguousarray(expert_b)
    c["eye"] = np.eye(128, dtype=ml_dtypes.bfloat16)
    inv = 10000.0 ** (-np.arange(0, DH, 2, dtype=np.float64) / DH)
    cos_l, sin_l = [], []
    for core in range(NCORE):
        pos = (core % 4) * TOK + np.arange(TOK, dtype=np.float64)
        fr = np.outer(pos, inv)
        cos_l.append(np.cos(fr).astype(np.float32))
        s = np.sin(fr).astype(np.float32)
        sin_l.append(np.ascontiguousarray(np.concatenate([-s, s], axis=1)))
    c["cos"] = cos_l
    c["sin2"] = sin_l
    return c


def run_expert_launch(Xp_list, c):
    nc2 = build_L2()
    in2 = []
    for cc in range(NCORE):
        xp8 = np.stack([Xp_list[src][cc * EL * PSL:(cc + 1) * EL * PSL]
                        for src in range(NCORE)])
        in2.append({"Xp8": np.ascontiguousarray(xp8),
                    "ew": np.ascontiguousarray(c["ew"][cc * EL:(cc + 1) * EL]),
                    "eb": np.ascontiguousarray(c["eb"][cc * EL:(cc + 1) * EL]),
                    "eye": c["eye"]})
    r2 = run_bass_kernel_spmd(nc2, in2, core_ids=list(range(NCORE)))
    return np.ascontiguousarray(
        np.concatenate([r2.results[cc]["Ytl"] for cc in range(NCORE)], axis=0))


def kernel(L0_attn_x, L0_attn_Wq, L0_attn_bq, L0_attn_Wk, L0_attn_bk, L0_attn_Wv,
           L0_attn_bv, L0_attn_Wo, L0_attn_bo, L0_ffn_phi_w, L0_ffn_phi_b,
           L0_ffn_expert_w, L0_ffn_expert_b):
    c = _prep_common(L0_attn_Wq, L0_attn_bq, L0_attn_Wk, L0_attn_bk, L0_attn_Wv,
                     L0_attn_bv, L0_attn_Wo, L0_attn_bo, L0_ffn_phi_w,
                     L0_ffn_phi_b, L0_ffn_expert_w, L0_ffn_expert_b)
    x0 = np.ascontiguousarray(np.asarray(L0_attn_x, dtype=np.float32)
                              .reshape(NCORE * TOK, D))
    eyef = np.eye(128, dtype=np.float32)
    cores = list(range(NCORE))

    nc1 = build_L1()
    in1 = [{"x": np.ascontiguousarray(x0[cc * TOK:(cc + 1) * TOK]),
            "phiT": c["phiT"], "phib": c["phib"], "eye": c["eye"]}
           for cc in cores]
    r1 = run_bass_kernel_spmd(nc1, in1, core_ids=cores)
    Xp1 = [r1.results[cc]["Xp"] for cc in cores]
    C1 = [r1.results[cc]["C"] for cc in cores]

    Ytl1 = run_expert_launch(Xp1, c)

    nc3 = build_L3()
    in3 = [{"x": in1[cc]["x"], "C": C1[cc], "Ytl": Ytl1,
            "WqT": c["WqT"], "WkT": c["WkT"], "WvT": c["WvT"],
            "bq": c["bq"], "bk": c["bk"], "bv": c["bv"],
            "cos": c["cos"][cc], "sin2": c["sin2"][cc],
            "eye": c["eye"], "eyef": eyef} for cc in cores]
    r3 = run_bass_kernel_spmd(nc3, in3, core_ids=cores)

    nc4 = build_L4()
    in4 = []
    for cc in cores:
        grp = range(4 * (cc // 4), 4 * (cc // 4) + 4)
        Kf = np.ascontiguousarray(
            np.concatenate([r3.results[g]["k"] for g in grp], axis=0))
        Vf = np.ascontiguousarray(
            np.concatenate([r3.results[g]["v"] for g in grp], axis=1))
        kn4 = np.ascontiguousarray(
            np.concatenate([r3.results[g]["norms"][:, 1] for g in grp])[None, :])
        qn = np.ascontiguousarray(r3.results[cc]["norms"][:, 0][None, :])
        in4.append({"x1": r3.results[cc]["x1"], "q": r3.results[cc]["q"],
                    "K": Kf, "V": Vf, "qn": qn, "kn4": kn4,
                    "WoT": c["WoT"], "bo": c["bo"],
                    "phiT": c["phiT"], "phib": c["phib"], "eye": c["eye"]})
    r4 = run_bass_kernel_spmd(nc4, in4, core_ids=cores)
    Xp2 = [r4.results[cc]["Xp2"] for cc in cores]
    C2 = [r4.results[cc]["C2"] for cc in cores]

    Ytl2 = run_expert_launch(Xp2, c)

    nc6 = build_L6()
    in6 = [{"x": r4.results[cc]["x2"], "C": C2[cc], "Ytl": Ytl2} for cc in cores]
    r6 = run_bass_kernel_spmd(nc6, in6, core_ids=cores)
    out = np.concatenate([r6.results[cc]["x3"] for cc in cores], axis=0)
    return out.reshape(2, SEQ, D)


# version 03
# speedup vs baseline: 1.3689x; 1.3689x over previous
import numpy as np
import ml_dtypes
from contextlib import ExitStack
import concourse.bass as bass
import concourse.mybir as mybir
from concourse.bass_utils import run_bass_kernel_spmd

F32 = mybir.dt.float32
BF16 = mybir.dt.bfloat16
AF = mybir.ActivationFunctionType
ALU = mybir.AluOpType
AX = mybir.AxisListType

NCORE = 8
TOK = 2048
NT = TOK // 128
D = 1024
DC = D // 128
SLOTS = 2048
NEXP = 64
PSL = 32
EL = NEXP // NCORE
NHEAD = 16
DH = 64
L4_DEBUG = False
SEQ = 8192
EPS = 1e-5


class Sched:

    def __init__(self):
        self.ops = []
        self.counts = {}

    AUTO_SIG = {"vector": "dve", "scalar": "act", "gpsimd": "pool"}

    def tok(self, sem):
        return (sem, self.counts.get(sem, 0))

    def do(self, eng, fn, after=(), sig=None, inc=1, chain=True):
        auto = self.AUTO_SIG.get(eng)
        if sig is None and auto is not None:
            sig = auto
        after = [t for t in after if t is not None]
        if chain and auto is not None and self.counts.get(auto, 0) > 0:
            after.append((auto, self.counts[auto]))
        after = tuple(after)
        for s, v in after:
            assert v <= self.counts.get(s, 0), (s, v, self.counts.get(s, 0))
        self.ops.append((eng, after, fn, sig or None, inc if sig else 0))
        if sig:
            self.counts[sig] = self.counts.get(sig, 0) + inc
            return (sig, self.counts[sig])
        return None

    def dma(self, eng, out, in_, after=(), sig="dma", **kw):
        return self.do(eng, lambda e: e.dma_start(out=out, in_=in_, **kw),
                       after=after, sig=sig, inc=16, chain=False)

    def wait(self, eng, after):
        self.do(eng, lambda e: None, after=after, sig=False, chain=False)

    def build(self, nc, stack):
        sems = {n: stack.enter_context(nc.semaphore("s_" + n)) for n in self.counts}
        per_eng = {}
        for eng, after, fn, sig, inc in self.ops:
            per_eng.setdefault(eng, []).append((after, fn, sig, inc))
        block = stack.enter_context(nc.Block())

        def mk(items):
            def body(e):
                waited = {}
                for after, fn, sig, inc in items:
                    for s, v in after:
                        if waited.get(s, 0) < v:
                            e.wait_ge(sems[s], v)
                            waited[s] = v
                    ins = fn(e)
                    if sig and ins is not None:
                        ins.then_inc(sems[sig], inc)
            return body
        for eng_name, items in per_eng.items():
            getattr(block, eng_name)(mk(items))


def emit_ln(S, x_ap, h_out_ap, scr, guard, out_dve_sig="dve"):
    mu, var, s2, rstd, xsq = scr["mu"], scr["var"], scr["s2"], scr["rstd"], scr["xsq"]
    S.do("vector", lambda e: e.reduce_sum(out=mu, in_=x_ap, axis=AX.X), after=guard)
    S.do("vector", lambda e: e.tensor_scalar_mul(out=mu, in0=mu, scalar1=1.0 / x_ap.shape[-1]))
    S.do("vector", lambda e: e.tensor_mul(out=xsq, in0=x_ap, in1=x_ap))
    S.do("vector", lambda e: e.reduce_sum(out=s2, in_=xsq, axis=AX.X))
    S.do("vector", lambda e: e.tensor_mul(out=var, in0=mu, in1=mu))
    t_v = S.do("vector", lambda e: e.scalar_tensor_tensor(
        out=var, in0=s2, scalar=1.0 / x_ap.shape[-1], in1=var,
        op0=ALU.mult, op1=ALU.subtract), sig="dve")
    S.do("scalar", lambda e: e.activation(out=s2, in_=var, func=AF.Ln, bias=scr["eps"]),
         after=[t_v])
    t_act = S.do("scalar", lambda e: e.activation(out=rstd, in_=s2, func=AF.Exp,
                                                  scale=-0.5), sig="act")
    t_h = S.do("vector", lambda e: e.tensor_scalar(
        out=h_out_ap, in0=x_ap, scalar1=mu, scalar2=rstd,
        op0=ALU.subtract, op1=ALU.mult), after=[t_act], sig=out_dve_sig)
    return t_h, t_act


def emit_ffn_A_tile(S, m, b, x_tile_ap, after_x):
    hm = b["h_all"][:, m * D:(m + 1) * D]
    t_h, t_ln_act = emit_ln(S, x_tile_ap, hm, b["lnscr"],
                            list(after_x) + [b["ln_free"]])
    b["ln_free"] = t_ln_act
    hT = b["hT"]
    ev_toks = []
    tr_toks = []
    for k in range(DC):
        ps = b["tr_ps"][k % 2]
        after = [t_h] + ([ev_toks[k - 2]] if k >= 2 else [b["trps_free"][k % 2]])
        t_tr = S.do("tensor", lambda e, k=k, ps=ps: e.transpose(
            ps[:, 0:128], b["h_all"][:, m * D + k * 128: m * D + (k + 1) * 128],
            b["eye"]), after=after, sig="pe")
        tr_toks.append(t_tr)
        ev = S.do("vector", lambda e, k=k, ps=ps: e.tensor_copy(
            out=hT[:, k * 128:(k + 1) * 128], in_=ps[:, 0:128]),
            after=[t_tr], sig="dve")
        ev_toks.append(ev)
    b["trps_free"] = [ev_toks[DC - 2], ev_toks[DC - 1]]
    lp = b["logits_ps"]
    for k in range(DC):
        for j in range(4):
            after = []
            if k == 0:
                after = [ev_toks[k], b["logits_free"]] + ([b["t_ones"]] if m == 0 else [])
            elif j == 0:
                after = [ev_toks[k]]
            S.do("tensor", lambda e, k=k, j=j: e.matmul(
                lp[:, j * 512:(j + 1) * 512],
                hT[:, k * 128:(k + 1) * 128],
                b["phiT"][:, k * SLOTS + j * 512: k * SLOTS + (j + 1) * 512],
                start=(k == 0), stop=False), after=after)
    t_mm = None
    for j in range(4):
        t_mm = S.do("tensor", lambda e, j=j: e.matmul(
            lp[:, j * 512:(j + 1) * 512], b["ones1"],
            b["phib"][:, j * 512:(j + 1) * 512], start=False, stop=True), sig="pe")
    t_negm = S.do("vector", lambda e: e.reduce_max(
        out=b["negm"], in_=lp[:], axis=AX.X, negate=True), after=[t_mm], sig="dve")
    E = b["E_t"][m % 2]
    t_E = S.do("scalar", lambda e: e.activation(
        out=E[:], in_=lp[:], func=AF.Exp, bias=b["negm"]),
        after=[t_negm] + b["E_free"][m % 2], sig="act")
    b["logits_free"] = t_E
    Ev = E[:].rearrange("p (n q) -> p n q", n=NEXP)
    Evt = E[:].rearrange("p (n q) -> p q n", n=NEXP)
    S.do("vector", lambda e: e.reduce_sum(out=b["sp"], in_=Ev, axis=AX.X), after=[t_E])
    S.do("vector", lambda e: e.reciprocal(out=b["rp"], in_=b["sp"]))
    S.do("vector", lambda e: e.reduce_sum(out=b["sn"], in_=Evt, axis=AX.X))
    t_rn = S.do("vector", lambda e: e.reciprocal(out=b["rn"], in_=b["sn"]), sig="dve")
    Dv = b["D_all"][:, m * SLOTS:(m + 1) * SLOTS].rearrange("p (n q) -> p n q", n=NEXP)
    t_D = S.do("vector", lambda e: e.tensor_tensor(
        out=Dv, in0=Ev, in1=b["rp"].unsqueeze(2).broadcast_to([128, NEXP, PSL]),
        op=ALU.mult), after=[t_rn], sig="dve")
    C = b["C_t"][m % 2]
    Cv = C[:].rearrange("p (n q) -> p n q", n=NEXP)
    t_C = S.do("gpsimd", lambda e: e.tensor_tensor(
        out=Cv, in0=Ev, in1=b["rn"].unsqueeze(1).broadcast_to([128, NEXP, PSL]),
        op=ALU.mult), after=[t_rn, b["C_free"][m % 2]], sig="pool")
    t_Cout = S.dma("sync", b["C_out"][m * 128:(m + 1) * 128, :], C[:],
                   after=[t_C], sig="dmaout")
    b["C_free"][m % 2] = t_Cout
    b["E_free"][m % 2] = [t_D, t_C]
    return t_h


def emit_ffn_AB(S, nc, st, b, get_x_tile, C_out, Xp_out):
    sb = lambda name, shape, dt: st.enter_context(nc.sbuf_tensor(name, shape, dt))
    uid = b["uid"]
    b["h_all"] = sb(f"h_all{uid}", [128, NT * D], BF16)[:]
    b["hT"] = sb(f"hT{uid}", [128, D], BF16)[:]
    b["D_all"] = sb(f"D_all{uid}", [128, NT * SLOTS], BF16)[:]
    b["E_t"] = [sb(f"E{uid}_{i}", [128, SLOTS], BF16) for i in range(2)]
    b["C_t"] = [sb(f"Ct{uid}_{i}", [128, SLOTS], BF16) for i in range(2)]
    b["negm"] = sb(f"negm{uid}", [128, 1], F32)[:]
    b["sp"] = sb(f"sp{uid}", [128, NEXP], F32)[:]
    b["rp"] = sb(f"rp{uid}", [128, NEXP], F32)[:]
    b["sn"] = sb(f"sn{uid}", [128, PSL], F32)[:]
    b["rn"] = sb(f"rn{uid}", [128, PSL], F32)[:]
    b["C_out"] = C_out
    b["C_free"] = [None, None]
    b["E_free"] = [[], []]
    b["logits_free"] = None
    b["ln_free"] = None
    b["trps_free"] = [None, None]

    ps_stack = ExitStack()
    b["tr_ps"] = [ps_stack.enter_context(nc.psum_tensor(f"tr{uid}_{i}", [128, 1024], BF16))
                  for i in range(2)]
    b["logits_ps"] = ps_stack.enter_context(nc.psum_tensor(f"lg{uid}", [128, SLOTS], F32))[:]

    for m in range(NT):
        x_ap, after_x = get_x_tile(m)
        t_h = emit_ffn_A_tile(S, m, b, x_ap, after_x)
        if "x_done" in b:
            b["x_done"](m, t_h)

    t_lastA = [S.tok("dve"), S.tok("pool")]
    ps_stack.close()

    Xps = [st.enter_context(nc.psum_tensor(f"Xps{uid}_{j}", [128, D], F32))
           for j in range(4)]
    Xev = [sb(f"Xev{uid}_{j}", [128, D], F32) for j in range(2)]
    bank_free = [None] * 4
    xev_dma = [None, None]
    for s in range(4):
        for m in range(NT):
            for j in range(4):
                slot0 = s * 512 + j * 128
                after = []
                if m == 0:
                    after = (t_lastA if s == 0 else []) + [bank_free[j]]
                for half in range(2):
                    S.do("tensor", lambda e, m=m, j=j, half=half, slot0=slot0: e.matmul(
                        Xps[j][:, half * 512:(half + 1) * 512],
                        b["D_all"][:, m * SLOTS + slot0: m * SLOTS + slot0 + 128],
                        b["h_all"][:, m * D + half * 512: m * D + (half + 1) * 512],
                        start=(m == 0), stop=(m == NT - 1)),
                        after=after if half == 0 else (),
                        sig="pe" if (m == NT - 1 and half == 1) else None)
        t_s = S.tok("pe")
        for j in range(4):
            t_ev = S.do("vector", lambda e, j=j: e.tensor_copy(
                out=Xev[j % 2][:], in_=Xps[j][:]), after=[t_s, xev_dma[j % 2]],
                sig="dve")
            bank_free[j] = t_ev
            xev_dma[j % 2] = S.dma(
                "sync", Xp_out[s * 512 + j * 128: s * 512 + (j + 1) * 128, :],
                Xev[j % 2][:], after=[t_ev], sig="dmaout")
    return S.tok("dmaout")


def build_L1():
    nc = bass.Bass("TRN2", target_bir_lowering=False)
    x_d = nc.dram_tensor("x", [TOK, D], F32, kind="ExternalInput").ap()
    phiT_d = nc.dram_tensor("phiT", [D, SLOTS], F32, kind="ExternalInput").ap()
    phib_d = nc.dram_tensor("phib", [1, SLOTS], F32, kind="ExternalInput").ap()
    eye_d = nc.dram_tensor("eye", [128, 128], BF16, kind="ExternalInput").ap()
    Xp_d = nc.dram_tensor("Xp", [SLOTS, D], F32, kind="ExternalOutput").ap()
    C_d = nc.dram_tensor("C", [TOK, SLOTS], BF16, kind="ExternalOutput").ap()

    S = Sched()
    with ExitStack() as st:
        sb = lambda name, shape, dt: st.enter_context(nc.sbuf_tensor(name, shape, dt))
        xt = [sb(f"xt{i}", [128, D], F32) for i in range(2)]
        phiT = sb("phiT_s", [128, DC * SLOTS], BF16)
        phib = sb("phib_s", [1, SLOTS], BF16)
        ones1 = sb("ones1", [1, 128], BF16)
        eye = sb("eye_s", [128, 128], BF16)
        lnscr = {k: sb("ln_" + k, [128, 1], F32)[:] for k in ("mu", "var", "s2", "rstd")}
        lnscr["xsq"] = sb("ln_xsq", [128, D], F32)[:]
        lnscr["eps"] = sb("ln_eps", [128, 1], F32)[:]
        S.do("vector", lambda e: e.memset(lnscr["eps"], EPS))

        b = dict(uid="a", phiT=phiT[:], phib=phib[:], ones1=ones1[:], eye=eye[:],
                 lnscr=lnscr)
        b["t_ones"] = S.do("vector", lambda e: e.memset(ones1[:], 1.0), sig="dve")
        t_eye = S.dma("sync", eye[:], eye_d, sig="ld")
        for k in range(DC):
            S.dma("gpsimd", phiT[:, k * SLOTS:(k + 1) * SLOTS],
                  phiT_d[k * 128:(k + 1) * 128, :], sig="ldw")
        S.dma("gpsimd", phib[:], phib_d, sig="ldw")
        t_w = S.tok("ldw")

        x_free = [None, None]

        def get_x_tile(m):
            t_x = S.dma("sync", xt[m % 2][:], x_d[m * 128:(m + 1) * 128, :],
                        after=[x_free[m % 2]], sig="ldx")
            return xt[m % 2][:], [t_x, t_w, t_eye]

        def x_done(m, t_h):
            x_free[m % 2] = t_h
        b["x_done"] = x_done

        t_out = emit_ffn_AB(S, nc, st, b, get_x_tile, C_d, Xp_d)
        S.wait("sync", [t_out, S.tok("dmaout")])
        S.build(nc, st)
    return nc


def build_L2():
    nc = bass.Bass("TRN2", target_bir_lowering=False)
    Xp8_d = nc.dram_tensor("Xp8", [NCORE, EL * PSL, D], F32, kind="ExternalInput").ap()
    ew_d = nc.dram_tensor("ew", [EL, D, D], F32, kind="ExternalInput").ap()
    eb_d = nc.dram_tensor("eb", [EL, D], F32, kind="ExternalInput").ap()
    eye_d = nc.dram_tensor("eye", [128, 128], BF16, kind="ExternalInput").ap()
    Y_d = nc.dram_tensor("Ytl", [EL * PSL, D], F32, kind="ExternalOutput").ap()

    S = Sched()
    with ExitStack() as st:
        sb = lambda name, shape, dt: st.enter_context(nc.sbuf_tensor(name, shape, dt))
        xp = sb("xp_s", [128, NCORE * 2 * D], F32)
        xs = sb("xs", [128, 2 * D], F32)
        xn = sb("xn", [128, 2 * D], F32)
        xb = sb("xb", [128, 2 * D], BF16)
        XT = sb("XT", [128, 2 * DC * 128], BF16)
        ew = [sb(f"ew{i}", [128, DC * D], BF16) for i in range(2)]
        ebr_s = sb("ebr_s", [1, EL * D], BF16)
        ebr = ebr_s
        ones1 = sb("ones1", [1, 128], BF16)
        eye = sb("eye_s", [128, 128], BF16)
        Yev = sb("Yev", [32, EL * D], F32)
        lnscr = {k: sb("ln_" + k, [128, 1], F32)[:] for k in ("mu", "var", "s2", "rstd")}
        lnscr["xsq"] = sb("ln_xsq", [128, D], F32)[:]
        lnscr["eps"] = sb("ln_eps", [128, 1], F32)[:]
        S.do("vector", lambda e: e.memset(lnscr["eps"], EPS))

        tr_ps = [st.enter_context(nc.psum_tensor(f"tr{i}", [128, 1024], BF16))
                 for i in range(2)]
        Y_ps = [st.enter_context(nc.psum_tensor(f"Y{i}", [32, D], F32))
                for i in range(2)]

        t_ones = S.do("vector", lambda e: e.memset(ones1[:], 1.0), sig="dve")
        t_eye = S.dma("sync", eye[:], eye_d, sig="ld")
        for src in range(NCORE):
            for half in range(2):
                S.dma("sync", xp[:, (src * 2 + half) * D:(src * 2 + half + 1) * D],
                      Xp8_d[src, half * 128:(half + 1) * 128, :], sig="ldx")
        t_x = S.tok("ldx")
        for n in range(EL):
            S.dma("gpsimd", ebr[:, n * D:(n + 1) * D], eb_d[n:n + 1, :], sig="ldb")
        t_eb = S.tok("ldb")
        ew_ready = [None] * EL
        for n in range(2):
            for k in range(DC):
                S.dma("gpsimd", ew[n][:, k * D:(k + 1) * D],
                      ew_d[n, k * 128:(k + 1) * 128, :], sig="ldw")
            ew_ready[n] = S.tok("ldw")

        ln_free = None
        t_tanh = [None, None]
        ev_done = []
        for half in range(2):
            hs = slice(half * D, (half + 1) * D)
            S.do("vector", lambda e, hs=hs: e.tensor_add(
                out=xs[:, hs], in0=xp[:, hs],
                in1=xp[:, 2 * D + hs.start: 2 * D + hs.stop]), after=[t_x])
            for src in range(2, NCORE):
                S.do("vector", lambda e, hs=hs, src=src: e.tensor_add(
                    out=xs[:, hs], in0=xs[:, hs],
                    in1=xp[:, src * 2 * D + hs.start: src * 2 * D + hs.stop]))
            S.do("vector", lambda e, hs=hs: e.tensor_scalar(
                out=xs[:, hs], in0=xs[:, hs], scalar1=-33000.0, scalar2=65000.0,
                op0=ALU.max, op1=ALU.min))
            t_h, t_act = emit_ln(S, xs[:, hs], xn[:, hs], lnscr,
                                 [ln_free])
            t_t = S.do("scalar", lambda e, hs=hs: e.activation(
                out=xb[:, hs], in_=xn[:, hs], func=AF.Tanh), after=[t_h], sig="act")
            ln_free = t_t
            t_tanh[half] = t_t
            for k in range(DC):
                ps = tr_ps[k % 2]
                after = [t_t, t_eye]
                if len(ev_done) >= 2:
                    after.append(ev_done[-2])
                t_tr = S.do("tensor", lambda e, hs=hs, k=k, ps=ps: e.transpose(
                    ps[:, 0:128], xb[:, hs.start + k * 128: hs.start + (k + 1) * 128],
                    eye[:]), after=after, sig="pe")
                ev = S.do("vector", lambda e, half=half, k=k, ps=ps: e.tensor_copy(
                    out=XT[:, (half * DC + k) * 128:(half * DC + k + 1) * 128],
                    in_=ps[:, 0:128]), after=[t_tr], sig="dve")
                ev_done.append(ev)
        t_XT = S.tok("dve")

        y_free = [None, None]
        for n in range(EL):
            half, j = divmod(n, 4)
            ps = Y_ps[n % 2]
            base = (half * DC) * 128
            for k in range(DC):
                lhs = XT[:, (half * DC + k) * 128 + j * 32:
                         (half * DC + k) * 128 + (j + 1) * 32]
                for hh in range(2):
                    after = ()
                    if k == 0 and hh == 0:
                        after = [t_XT, ew_ready[n], y_free[n % 2], t_ones, t_eb]
                    S.do("tensor", lambda e, k=k, hh=hh, lhs=lhs, ps=ps, n=n: e.matmul(
                        ps[:, hh * 512:(hh + 1) * 512], lhs,
                        ew[n % 2][:, k * D + hh * 512: k * D + (hh + 1) * 512],
                        start=(k == 0), stop=False), after=after)
            t_pe = None
            for hh in range(2):
                t_pe = S.do("tensor", lambda e, n=n, hh=hh, ps=ps: e.matmul(
                    ps[:, hh * 512:(hh + 1) * 512], ones1[:, 0:32],
                    ebr[:, n * D + hh * 512: n * D + (hh + 1) * 512],
                    start=False, stop=True), sig="pe")
            t_ev = S.do("vector", lambda e, n=n, ps=ps: e.tensor_copy(
                out=Yev[:, n * D:(n + 1) * D], in_=ps[:]), after=[t_pe], sig="dve")
            y_free[n % 2] = t_ev
            if n + 2 < EL:
                for k in range(DC):
                    S.dma("gpsimd", ew[n % 2][:, k * D:(k + 1) * D],
                          ew_d[n + 2, k * 128:(k + 1) * 128, :],
                          after=[t_ev] if k == 0 else (), sig="ldw")
                ew_ready[n + 2] = S.tok("ldw")
            S.dma("sync", Y_d[n * PSL:(n + 1) * PSL, :],
                  Yev[0:PSL, n * D:(n + 1) * D], after=[t_ev], sig="dmaout")
        S.wait("sync", [S.tok("dmaout")])
        S.build(nc, st)
    return nc



def emit_ffn_D(S, nc, st, uid, C_d, Ytl_d, x_d, xout_d, tile_cb=None, psum_pp=2):
    sb = lambda name, shape, dt: st.enter_context(nc.sbuf_tensor(name, shape, dt))
    CT = sb(f"CT{uid}", [128, 16 * SLOTS], BF16)
    Ytl = sb(f"Ytl{uid}", [128, 16 * D], BF16)
    xt = [sb(f"xt{uid}{i}", [128, D], F32) for i in range(2)]
    x1t = [sb(f"x1t{uid}{i}", [128, D], F32) for i in range(2)]
    comb = [st.enter_context(nc.psum_tensor(f"comb{uid}{i}", [128, D], F32))
            for i in range(psum_pp)]
    for sc in range(16):
        S.dma("sync", CT[:, sc * SLOTS:(sc + 1) * SLOTS],
              C_d[0:TOK, sc * 128:(sc + 1) * 128], transpose=True, sig="ldct")
        S.dma("gpsimd", Ytl[:, sc * D:(sc + 1) * D],
              Ytl_d[sc * 128:(sc + 1) * 128, :], sig="ldyt")
    t_ct = S.tok("ldct")
    t_yt = S.tok("ldyt")
    x_free = [None, None]
    x1_free = [None, None]
    comb_free = [None] * psum_pp
    for m in range(NT):
        t_x = S.dma("sync", xt[m % 2][:], x_d[m * 128:(m + 1) * 128, :],
                    after=[x_free[m % 2]], sig="ldx" + uid)
        ps = comb[m % psum_pp]
        for sc in range(16):
            for hh in range(2):
                after = ()
                if sc == 0 and hh == 0:
                    after = [t_ct, t_yt, comb_free[m % psum_pp]]
                S.do("tensor", lambda e, m=m, sc=sc, hh=hh, ps=ps: e.matmul(
                    ps[:, hh * 512:(hh + 1) * 512],
                    CT[:, sc * SLOTS + m * 128: sc * SLOTS + (m + 1) * 128],
                    Ytl[:, sc * D + hh * 512: sc * D + (hh + 1) * 512],
                    start=(sc == 0), stop=(sc == 15)),
                    after=after, sig="pe" if (sc == 15 and hh == 1) else None)
        t_mm = S.tok("pe")
        t_x1 = S.do("vector", lambda e, m=m, ps=ps: e.scalar_tensor_tensor(
            out=x1t[m % 2][:], in0=ps[:], scalar=0.5, in1=xt[m % 2][:],
            op0=ALU.mult, op1=ALU.add),
            after=[t_mm, t_x, x_free[m % 2], x1_free[m % 2]])
        comb_free[m % psum_pp] = t_x1
        x_free[m % 2] = t_x1
        t_o = S.dma("sync", xout_d[m * 128:(m + 1) * 128, :], x1t[m % 2][:],
                    after=[t_x1], sig="dmaout")
        x1_free[m % 2] = t_o
        if tile_cb is not None:
            tile_cb(m, x1t[m % 2][:], t_x1)
    return S.tok("dmaout")


def build_L3():
    nc = bass.Bass("TRN2", target_bir_lowering=False)
    x_d = nc.dram_tensor("x", [TOK, D], F32, kind="ExternalInput").ap()
    C_d = nc.dram_tensor("C", [TOK, SLOTS], BF16, kind="ExternalInput").ap()
    Ytl_d = nc.dram_tensor("Ytl", [SLOTS, D], F32, kind="ExternalInput").ap()
    W_d = {w: nc.dram_tensor(w, [D, D], F32, kind="ExternalInput").ap()
           for w in ("WqT", "WkT", "WvT")}
    b_d = {w: nc.dram_tensor(w, [1, D], F32, kind="ExternalInput").ap()
           for w in ("bq", "bk", "bv")}
    cos_d = nc.dram_tensor("cos", [TOK, 32], F32, kind="ExternalInput").ap()
    sin_d = nc.dram_tensor("sin2", [TOK, 64], F32, kind="ExternalInput").ap()
    eye_d = nc.dram_tensor("eye", [128, 128], BF16, kind="ExternalInput").ap()
    eyef_d = nc.dram_tensor("eyef", [128, 128], F32, kind="ExternalInput").ap()
    x1_d = nc.dram_tensor("x1", [TOK, D], F32, kind="ExternalOutput").ap()
    q_d = nc.dram_tensor("q", [TOK, D], BF16, kind="ExternalOutput").ap()
    k_d = nc.dram_tensor("k", [TOK, D], BF16, kind="ExternalOutput").ap()
    v_d = nc.dram_tensor("v", [8, TOK, 130], BF16, kind="ExternalOutput").ap()
    nrm_d = nc.dram_tensor("norms", [NHEAD, 2], F32, kind="ExternalOutput").ap()
    bvscr_d = nc.dram_tensor("bvscr", [1, D], F32, kind="Internal").ap()

    S = Sched()
    with ExitStack() as st:
        sb = lambda name, shape, dt: st.enter_context(nc.sbuf_tensor(name, shape, dt))
        W = {w: sb(w + "_s", [128, DC * D], BF16) for w in ("WqT", "WkT", "WvT")}
        br = {w: sb(w + "_s", [1, D], BF16) for w in ("bq", "bk", "bv")}
        bvb = sb("bvb", [128, D], F32)
        ones1 = sb("ones1", [1, 128], BF16)
        eye = sb("eye_s", [128, 128], BF16)
        eyef = sb("eyef_s", [128, 128], F32)
        cos_sb = sb("cos_sb", [128, NT * 32], F32)
        sin_sb = sb("sin_sb", [128, NT * 64], F32)
        g_t = sb("g_t", [128, D], BF16)
        gT = sb("gT", [128, D], BF16)
        t1f = sb("t1f", [128, D], F32)
        t2f = sb("t2f", [128, D], F32)
        qk_bf = [sb(f"qkbf{i}", [128, D], BF16) for i in range(2)]
        v_t = [sb(f"vt{i}", [128, 16 * 65], BF16) for i in range(2)]
        sqs = sb("sqs", [128, D], F32)
        red16 = sb("red16", [128, NHEAD], F32)
        qn_run = sb("qn_run", [128, NHEAD], F32)
        kn_run = sb("kn_run", [128, NHEAD], F32)
        nrm_sb = sb("nrm_sb", [16, 2], F32)
        lnscr = {k2: sb("ln_" + k2, [128, 1], F32)[:] for k2 in ("mu", "var", "s2", "rstd")}
        lnscr["xsq"] = sb("ln_xsq", [128, D], F32)[:]
        lnscr["eps"] = sb("ln_eps", [128, 1], F32)[:]
        S.do("vector", lambda e: e.memset(lnscr["eps"], EPS))
        t_ones = S.do("vector", lambda e: e.memset(ones1[:], 1.0))
        S.do("vector", lambda e: e.memset(qn_run[:], 0.0))
        t_nrm0 = S.do("vector", lambda e: e.memset(kn_run[:], 0.0))

        t_eye = S.dma("sync", eye[:], eye_d, sig="ld")
        S.dma("sync", eyef[:], eyef_d, sig="ld")
        for m in range(NT):
            S.dma("sync", cos_sb[:, m * 32:(m + 1) * 32],
                  cos_d[m * 128:(m + 1) * 128, :], sig="ld")
            S.dma("sync", sin_sb[:, m * 64:(m + 1) * 64],
                  sin_d[m * 128:(m + 1) * 128, :], sig="ld")
        BIAS = {"WqT": "bq", "WkT": "bk", "WvT": "bv"}
        for w in ("WqT", "WkT", "WvT"):
            for kk in range(DC):
                S.dma("gpsimd", W[w][:, kk * D:(kk + 1) * D],
                      W_d[w][kk * 128:(kk + 1) * 128, :], sig="ldw")
            S.dma("gpsimd", br[BIAS[w]][:], b_d[BIAS[w]], sig="ldw")
        t_bv = S.dma("sync", bvscr_d, b_d["bv"], sig="ld")
        S.dma("sync", bvb[:], bvscr_d.broadcast_to([128, D]), after=[t_bv], sig="ld")
        t_ld = S.tok("ld")
        t_w = S.tok("ldw")

        tr_ps = [st.enter_context(nc.psum_tensor(f"trq{i}", [128, 1024], BF16))
                 for i in range(2)]
        qkv_ps = [st.enter_context(nc.psum_tensor(f"qkv{i}", [128, D], F32))
                  for i in range(2)]

        state = dict(trps_free=[None, None], qkv_free=[None, None],
                     qk_free=[None, None], vt_free=[None, None], ln_free=None,
                     gT_free=None)

        def qkv_tile(m, x1_ap, t_x1):
            t_g, t_act = emit_ln(S, x1_ap, g_t[:], lnscr,
                                 [t_x1, state["ln_free"]])
            state["ln_free"] = t_act
            ev_toks = []
            for kk in range(DC):
                ps = tr_ps[kk % 2]
                after = [t_g, t_eye] +                     ([ev_toks[kk - 2]] if kk >= 2 else [state["trps_free"][kk % 2]])
                if kk == 0:
                    after.append(state["gT_free"])
                t_tr = S.do("tensor", lambda e, kk=kk, ps=ps: e.transpose(
                    ps[:, 0:128], g_t[:, kk * 128:(kk + 1) * 128], eye[:]),
                    after=after, sig="pe")
                ev = S.do("vector", lambda e, kk=kk, ps=ps: e.tensor_copy(
                    out=gT[:, kk * 128:(kk + 1) * 128], in_=ps[:, 0:128]),
                    after=[t_tr])
                ev_toks.append(ev)
            state["trps_free"] = [ev_toks[DC - 2], ev_toks[DC - 1]]
            t_gT = ev_toks[-1]
            for pi, w in enumerate(("WqT", "WkT", "WvT")):
                ps = qkv_ps[pi % 2]
                for kk in range(DC):
                    for hh in range(2):
                        after = ()
                        if kk == 0 and hh == 0:
                            after = [t_gT, state["qkv_free"][pi % 2],
                                     t_w if m == 0 else None,
                                     t_ones if m == 0 else None]
                        S.do("tensor", lambda e, w=w, kk=kk, hh=hh, ps=ps: e.matmul(
                            ps[:, hh * 512:(hh + 1) * 512],
                            gT[:, kk * 128:(kk + 1) * 128],
                            W[w][:, kk * D + hh * 512: kk * D + (hh + 1) * 512],
                            start=(kk == 0), stop=False), after=after)
                t_p = None
                for hh in range(2):
                    t_p = S.do("tensor", lambda e, w=w, hh=hh, ps=ps: e.matmul(
                        ps[:, hh * 512:(hh + 1) * 512], ones1[:],
                        br[{"WqT": "bq", "WkT": "bk", "WvT": "bv"}[w]][:, hh * 512:(hh + 1) * 512],
                        start=False, stop=True), sig="pe")
                if w == "WvT":
                    vt = v_t[m % 2]
                    t_ms = S.do("vector", lambda e, vt=vt: e.memset(vt[:], 1.0),
                                after=[state["vt_free"][m % 2]])
                    t_v = S.do("vector", lambda e, vt=vt, ps=ps: e.tensor_copy(
                        out=vt[:].rearrange("p (h g) -> p h g", g=65)[:, :, 0:64],
                        in_=ps[:].rearrange("p (h g) -> p h g", g=64)),
                        after=[t_p, t_ms])
                    for hg in range(8):
                        S.dma("sync", v_d[hg, m * 128:(m + 1) * 128, :],
                              vt[:, hg * 130:(hg + 1) * 130], after=[t_v],
                              sig="dmaout")
                    state["vt_free"][m % 2] = t_v
                    state["qkv_free"][pi % 2] = t_v
                else:
                    qb = qk_bf[pi]
                    Pv = ps[:].rearrange("p (h s f) -> p h s f", h=NHEAD, s=2)
                    cosv = cos_sb[:, m * 32:(m + 1) * 32].unsqueeze(1).unsqueeze(2) \
                        .broadcast_to([128, NHEAD, 2, 32])
                    sinv = sin_sb[:, m * 64:(m + 1) * 64] \
                        .rearrange("p (s f) -> p s f", s=2).unsqueeze(1) \
                        .broadcast_to([128, NHEAD, 2, 32])
                    t1v = t1f[:].rearrange("p (h s f) -> p h s f", h=NHEAD, s=2)
                    t2v = t2f[:].rearrange("p (h s f) -> p h s f", h=NHEAD, s=2)
                    S.do("vector", lambda e, Pv=Pv, cosv=cosv, t1v=t1v: e.tensor_tensor(
                        out=t1v, in0=Pv, in1=cosv, op=ALU.mult), after=[t_p, t_ld])
                    S.do("vector", lambda e, Pv=Pv, sinv=sinv, t2v=t2v: e.tensor_tensor(
                        out=t2v[:, :, 0, :], in0=Pv[:, :, 1, :],
                        in1=sinv[:, :, 0, :], op=ALU.mult))
                    S.do("vector", lambda e, Pv=Pv, sinv=sinv, t2v=t2v: e.tensor_tensor(
                        out=t2v[:, :, 1, :], in0=Pv[:, :, 0, :],
                        in1=sinv[:, :, 1, :], op=ALU.mult))
                    t_qb = S.do("vector", lambda e, qb=qb: e.tensor_add(
                        out=qb[:], in0=t1f[:], in1=t2f[:]),
                        after=[state["qk_free"][pi]])
                    state["qkv_free"][pi % 2] = t_qb
                    t_sq = S.do("gpsimd", lambda e, qb=qb: e.tensor_tensor(
                        out=sqs[:], in0=qb[:], in1=qb[:], op=ALU.mult), after=[t_qb])
                    S.do("vector", lambda e: e.reduce_sum(
                        out=red16[:], in_=sqs[:].rearrange("p (h f) -> p h f", h=NHEAD),
                        axis=AX.X), after=[t_sq])
                    run = qn_run if w == "WqT" else kn_run
                    S.do("vector", lambda e, run=run: e.tensor_max(
                        out=run[:], in0=run[:], in1=red16[:]),
                        after=[t_nrm0] if m == 0 else ())
                    out_d = q_d if w == "WqT" else k_d
                    t_dq = S.dma("sync", out_d[m * 128:(m + 1) * 128, :], qb[:],
                                 after=[t_qb], sig="dmaout")
                    state["qk_free"][pi] = t_dq

        t_dout = emit_ffn_D(S, nc, st, "d", C_d, Ytl_d, x_d, x1_d,
                            tile_cb=qkv_tile, psum_pp=1)
        t_last = [S.tok("dve"), S.tok("pool"), S.tok("pe")]
        for i, run in enumerate((qn_run, kn_run)):
            t_tr = S.do("tensor", lambda e, run=run: e.transpose(
                qkv_ps[1][0:16, 0:128], run[:], eyef[:]),
                after=t_last + [S.tok("dve")], sig="pe")
            S.do("vector", lambda e, i=i: e.reduce_max(
                out=nrm_sb[:, i:i + 1], in_=qkv_ps[1][0:16, 0:128], axis=AX.X),
                after=[t_tr])
        t_n = S.tok("dve")
        S.dma("sync", nrm_d, nrm_sb[:], after=[t_n], sig="dmaout")
        S.wait("sync", [S.tok("dmaout")])
        S.build(nc, st)
    return nc


def build_L6():
    nc = bass.Bass("TRN2", target_bir_lowering=False)
    x_d = nc.dram_tensor("x", [TOK, D], F32, kind="ExternalInput").ap()
    C_d = nc.dram_tensor("C", [TOK, SLOTS], BF16, kind="ExternalInput").ap()
    Ytl_d = nc.dram_tensor("Ytl", [SLOTS, D], F32, kind="ExternalInput").ap()
    x3_d = nc.dram_tensor("x3", [TOK, D], F32, kind="ExternalOutput").ap()
    S = Sched()
    with ExitStack() as st:
        t_out = emit_ffn_D(S, nc, st, "f", C_d, Ytl_d, x_d, x3_d, psum_pp=2)
        S.wait("sync", [S.tok("dmaout")])
        S.build(nc, st)
    return nc



def build_L4():
    nc = bass.Bass("TRN2", target_bir_lowering=False)
    x1_d = nc.dram_tensor("x1", [TOK, D], F32, kind="ExternalInput").ap()
    q_d = nc.dram_tensor("q", [TOK, D], BF16, kind="ExternalInput").ap()
    K_d = nc.dram_tensor("K", [SEQ, D], BF16, kind="ExternalInput").ap()
    V_d = nc.dram_tensor("V", [8, SEQ, 130], BF16, kind="ExternalInput").ap()
    qn_d = nc.dram_tensor("qn", [1, NHEAD], F32, kind="ExternalInput").ap()
    kn4_d = nc.dram_tensor("kn4", [1, 4 * NHEAD], F32, kind="ExternalInput").ap()
    WoT_d = nc.dram_tensor("WoT", [D, D], F32, kind="ExternalInput").ap()
    bo_d = nc.dram_tensor("bo", [1, D], F32, kind="ExternalInput").ap()
    phiT_d = nc.dram_tensor("phiT", [D, SLOTS], F32, kind="ExternalInput").ap()
    phib_d = nc.dram_tensor("phib", [1, SLOTS], F32, kind="ExternalInput").ap()
    eye_d = nc.dram_tensor("eye", [128, 128], BF16, kind="ExternalInput").ap()
    x2_d = nc.dram_tensor("x2", [TOK, D], F32, kind="ExternalOutput").ap()
    Xp2_d = nc.dram_tensor("Xp2", [SLOTS, D], F32, kind="ExternalOutput").ap()
    C2_d = nc.dram_tensor("C2", [TOK, SLOTS], BF16, kind="ExternalOutput").ap()
    aTdbg_d = (nc.dram_tensor("aTdbg", [128, DC * TOK], BF16,
                              kind="ExternalOutput").ap() if L4_DEBUG else None)
    nmdbg_d = (nc.dram_tensor("nmdbg", [128, NHEAD], F32,
                              kind="ExternalOutput").ap() if L4_DEBUG else None)
    rscr_d = [nc.dram_tensor(f"rscr{i}", [1, TOK // 2], F32, kind="Internal").ap()
              for i in range(2)]

    S = Sched()
    with ExitStack() as st:
        sb = lambda name, shape, dt: st.enter_context(nc.sbuf_tensor(name, shape, dt))
        negmb = sb("negmb", [128, NHEAD], F32)
        aT = sb("aT", [128, DC * TOK], BF16)

        nrq = sb("nrq", [128, NHEAD], F32)
        nrk = sb("nrk", [128, 4 * NHEAD], F32)
        t_qn = S.dma("sync", nrq[:], qn_d.broadcast_to([128, NHEAD]), sig="ld")
        t_kn = S.dma("sync", nrk[:], kn4_d.broadcast_to([128, 4 * NHEAD]), sig="ld")
        knv = nrk[:].rearrange("p (c h) -> p h c", c=4)
        S.do("vector", lambda e: e.reduce_max(
            out=negmb[:], in_=knv, axis=AX.X), after=[t_qn, t_kn])
        t_mul = S.do("vector", lambda e: e.tensor_mul(
            out=negmb[:], in0=negmb[:], in1=nrq[:]))
        S.do("scalar", lambda e: e.activation(
            out=negmb[:], in_=negmb[:], func=AF.Ln), after=[t_mul])
        t_ex = S.do("scalar", lambda e: e.activation(
            out=negmb[:], in_=negmb[:], func=AF.Exp, scale=0.5))
        t_nb = S.do("vector", lambda e: e.tensor_scalar_mul(
            out=negmb[:], in0=negmb[:], scalar1=-0.125), after=[t_ex])

        with ExitStack() as st2:
            sb2 = lambda name, shape, dt: st2.enter_context(nc.sbuf_tensor(name, shape, dt))
            kT = [sb2(f"kT{i}", [128, SEQ], BF16) for i in range(2)]
            qT = [sb2(f"qT{i}", [128, TOK // 2], BF16) for i in range(2)]
            vsb = [sb2(f"vsb{i}", [128, 64 * 130], BF16) for i in range(2)]
            E = [sb2(f"E{i}", [128, TOK // 2], BF16) for i in range(2)]
            rrow = [sb2(f"rrow{i}", [1, TOK // 2], F32) for i in range(2)]
            rb = [sb2(f"rb{i}", [64, TOK // 2], F32) for i in range(2)]
            s_ps = [st2.enter_context(nc.psum_tensor(f"sps{i}", [128, TOK // 2], F32))
                    for i in range(2)]
            av_ps = [st2.enter_context(nc.psum_tensor(f"av{i}", [65, TOK // 2], F32))
                     for i in range(2)]
            QW = TOK // 2

            buf_free = [None, None]
            av_free = [None, None]
            exp_done = [None, None]
            rrow_free = [None, None]
            rb_free = [None, None]
            prev_av_last = None
            di = 0
            seg_list = [(qt, hg) for qt in range(2) for hg in range(8)]
            for si, (qt, hg) in enumerate(seg_list):
                if True:
                    bi = si % 2
                    t_kT = S.dma("sync", kT[bi][:],
                                 K_d[0:SEQ, hg * 128:(hg + 1) * 128],
                                 after=[buf_free[bi]], transpose=True, sig="ldk")
                    t_qT = S.dma("sync", qT[bi][:],
                                 q_d[qt * QW:(qt + 1) * QW, hg * 128:(hg + 1) * 128],
                                 after=[buf_free[bi]], transpose=True, sig="ldk")
                    t_v = S.dma("gpsimd",
                                vsb[bi][:].rearrange("p (kb f) -> p kb f", f=130),
                                V_d[hg].rearrange("(kb p) f -> p kb f", p=128),
                                after=[buf_free[bi]], sig="ldv")
                    t_loads = [t_kT, t_qT, t_v, t_nb]
                    for h2 in range(2):
                        h = 2 * hg + h2
                        avp = di % 2
                        av = av_ps[avp]
                        exp_tok = [None] * 64
                        av_tok = [None] * 64

                        def emit_scores(kb, h2=h2, bi=bi, h=h):
                            sp = s_ps[kb % 2]
                            after0 = [exp_done[kb % 2]]
                            if kb < 2:
                                after0 = after0 + t_loads
                            S.do("tensor", lambda e, kb=kb, sp=sp, h2=h2, bi=bi: e.matmul(
                                sp[:, 0:512],
                                kT[bi][h2 * 64:(h2 + 1) * 64, kb * 128:(kb + 1) * 128],
                                qT[bi][h2 * 64:(h2 + 1) * 64, 0:512],
                                start=True, stop=True), after=after0)
                            t_s = S.do("tensor", lambda e, kb=kb, sp=sp, h2=h2, bi=bi: e.matmul(
                                sp[:, 512:1024],
                                kT[bi][h2 * 64:(h2 + 1) * 64, kb * 128:(kb + 1) * 128],
                                qT[bi][h2 * 64:(h2 + 1) * 64, 512:1024],
                                start=True, stop=True), sig="pe")
                            t_e = S.do("scalar", lambda e, kb=kb, sp=sp, h=h: e.activation(
                                out=E[kb % 2][:], in_=sp[:], func=AF.Exp,
                                scale=0.125, bias=negmb[:, h:h + 1]),
                                after=[t_s] + ([av_tok[kb - 2]] if kb >= 2 else
                                               [prev_av_last]), chain=False)
                            exp_done[kb % 2] = t_e
                            exp_tok[kb] = t_e

                        def emit_av(kb, h2=h2, bi=bi, av=av, avp=avp):
                            for hh in range(2):
                                after1 = []
                                if kb == 0 and hh == 0:
                                    after1 = [exp_tok[kb], av_free[avp]]
                                elif hh == 0:
                                    after1 = [exp_tok[kb]]
                                t = S.do("tensor", lambda e, kb=kb, hh=hh, av=av, h2=h2, bi=bi: e.matmul(
                                    av[:, hh * 512:(hh + 1) * 512],
                                    vsb[bi][:].rearrange("p (kb f) -> p kb f", f=130)
                                    [:, kb, h2 * 65:(h2 + 1) * 65],
                                    E[kb % 2][:, hh * 512:(hh + 1) * 512],
                                    start=(kb == 0), stop=(kb == 63)),
                                    after=after1,
                                    sig="pe" if hh == 1 else None)
                                if hh == 1:
                                    av_tok[kb] = t

                        emit_scores(0)
                        for kb in range(1, 64):
                            emit_scores(kb)
                            emit_av(kb - 1)
                        emit_av(63)
                        prev_av_last = av_tok[63]
                        t_av = S.tok("pe")
                        rp = di % 2
                        t_rc = S.do("vector", lambda e, av=av, rp=rp: e.reciprocal(
                            out=rrow[rp][:], in_=av[64:65, :]),
                            after=[t_av, rrow_free[rp]])
                        t_r1 = S.dma("sync", rscr_d[rp], rrow[rp][:], after=[t_rc],
                                     sig="ldr")
                        rrow_free[rp] = t_r1
                        t_r2 = S.dma("sync", rb[rp][:],
                                     rscr_d[rp].broadcast_to([64, QW]),
                                     after=[t_r1, rb_free[rp]], sig="ldr")
                        t_aT = S.do("vector", lambda e, av=av, rp=rp, h2=h2, hg=hg, qt=qt:
                                    e.tensor_tensor(
                                        out=aT[h2 * 64:(h2 + 1) * 64,
                                               hg * TOK + qt * QW: hg * TOK + (qt + 1) * QW],
                                        in0=av[0:64, :], in1=rb[rp][:], op=ALU.mult),
                                    after=[t_r2])
                        rb_free[rp] = t_aT
                        av_free[avp] = t_aT
                        di += 1
                    buf_free[bi] = S.tok("pe")
            t_attn = [S.tok("dve"), S.tok("pe")]
            if L4_DEBUG:
                S.dma("sync", aTdbg_d, aT[:], after=t_attn, sig="dmaout")
                S.dma("sync", nmdbg_d, negmb[:], after=t_attn, sig="dmaout")

        with ExitStack() as st3:
            sb3 = lambda name, shape, dt: st3.enter_context(nc.sbuf_tensor(name, shape, dt))
            WoT = sb3("WoT_s", [128, DC * D], BF16)
            bor = sb3("bor", [1, D], BF16)
            ones1o = sb3("ones1o", [1, 128], BF16)
            x1t = [sb3(f"x1t{i}", [128, D], F32) for i in range(2)]
            x2t = [sb3(f"x2t{i}", [128, D], F32) for i in range(2)]
            o_ps = [st3.enter_context(nc.psum_tensor(f"ops{i}", [128, D], F32))
                    for i in range(2)]
            attn_done = t_attn + [S.tok("act")]
            t_ones = S.do("vector", lambda e: e.memset(ones1o[:], 1.0),
                          after=attn_done)
            for kk in range(DC):
                S.dma("gpsimd", WoT[:, kk * D:(kk + 1) * D],
                      WoT_d[kk * 128:(kk + 1) * 128, :], after=attn_done, sig="ldw")
            S.dma("gpsimd", bor[:], bo_d, after=attn_done, sig="ldw")
            t_wo = S.tok("ldw")
            x1_free = [None, None]
            x2_free = [None, None]
            o_free = [None, None]
            x2_out = [None] * NT
            for m in range(NT):
                t_x1 = S.dma("sync", x1t[m % 2][:], x1_d[m * 128:(m + 1) * 128, :],
                             after=[x1_free[m % 2]], sig="ldx")
                ps = o_ps[m % 2]
                for kk in range(DC):
                    for hh in range(2):
                        after = ()
                        if kk == 0 and hh == 0:
                            after = t_attn + [t_wo, o_free[m % 2]] + \
                                ([t_ones] if m == 0 else [])
                        S.do("tensor", lambda e, m=m, kk=kk, hh=hh, ps=ps: e.matmul(
                            ps[:, hh * 512:(hh + 1) * 512],
                            aT[:, kk * TOK + m * 128: kk * TOK + (m + 1) * 128],
                            WoT[:, kk * D + hh * 512: kk * D + (hh + 1) * 512],
                            start=(kk == 0), stop=False), after=after)
                t_p = None
                for hh in range(2):
                    t_p = S.do("tensor", lambda e, hh=hh, ps=ps: e.matmul(
                        ps[:, hh * 512:(hh + 1) * 512], ones1o[:],
                        bor[:, hh * 512:(hh + 1) * 512], start=False, stop=True),
                        sig="pe")
                t_x2 = S.do("vector", lambda e, m=m, ps=ps: e.tensor_add(
                    out=x2t[m % 2][:], in0=ps[:], in1=x1t[m % 2][:]),
                    after=[t_p, t_x1, x2_free[m % 2]])
                o_free[m % 2] = t_x2
                x1_free[m % 2] = t_x2
                t_o = S.dma("sync", x2_d[m * 128:(m + 1) * 128, :], x2t[m % 2][:],
                            after=[t_x2], sig="dmaout")
                x2_free[m % 2] = t_o
                x2_out[m] = t_o

        oproj_done = [S.tok("pe"), S.tok("dve")]
        sb4 = lambda name, shape, dt: st.enter_context(nc.sbuf_tensor(name, shape, dt))
        xt = [sb4(f"xtf{i}", [128, D], F32) for i in range(2)]
        phiT = sb4("phiT_s", [128, DC * SLOTS], BF16)
        phib = sb4("phib_s", [1, SLOTS], BF16)
        ones1 = sb4("ones1f", [1, 128], BF16)
        eye = sb4("eye_s", [128, 128], BF16)
        lnscr = {k2: sb4("lnf_" + k2, [128, 1], F32)[:] for k2 in ("mu", "var", "s2", "rstd")}
        lnscr["xsq"] = sb4("lnf_xsq", [128, D], F32)[:]
        lnscr["eps"] = sb4("lnf_eps", [128, 1], F32)[:]
        S.do("vector", lambda e: e.memset(lnscr["eps"], EPS))
        b = dict(uid="f4", phiT=phiT[:], phib=phib[:], ones1=ones1[:], eye=eye[:],
                 lnscr=lnscr)
        b["t_ones"] = S.do("vector", lambda e: e.memset(ones1[:], 1.0))
        t_eye = S.dma("sync", eye[:], eye_d, after=oproj_done, sig="ld")
        for kk in range(DC):
            S.dma("gpsimd", phiT[:, kk * SLOTS:(kk + 1) * SLOTS],
                  phiT_d[kk * 128:(kk + 1) * 128, :], after=oproj_done, sig="ldw")
        S.dma("gpsimd", phib[:], phib_d, after=oproj_done, sig="ldw")
        t_w = S.tok("ldw")
        xf_free = [None, None]

        def get_x_tile(m):
            t_x = S.dma("sync", xt[m % 2][:], x2_d[m * 128:(m + 1) * 128, :],
                        after=[xf_free[m % 2], x2_out[m]] +
                        (oproj_done if m < 2 else []), sig="ldxf")
            return xt[m % 2][:], [t_x, t_w, t_eye]

        def x_done(m, t_h):
            xf_free[m % 2] = t_h
        b["x_done"] = x_done
        emit_ffn_AB(S, nc, st, b, get_x_tile, C2_d, Xp2_d)
        S.wait("sync", [S.tok("dmaout")])
        S.build(nc, st)
    return nc


def _prep_common(Wq, bq, Wk, bk, Wv, bv, Wo, bo, phi_w, phi_b, expert_w, expert_b):
    c = {}
    c["phiT"] = np.ascontiguousarray(phi_w.T)
    c["phib"] = np.ascontiguousarray(phi_b[None, :])
    c["WqT"] = np.ascontiguousarray(Wq.T)
    c["WkT"] = np.ascontiguousarray(Wk.T)
    c["WvT"] = np.ascontiguousarray(Wv.T)
    c["WoT"] = np.ascontiguousarray(Wo.T)
    c["bq"] = np.ascontiguousarray(bq[None, :])
    c["bk"] = np.ascontiguousarray(bk[None, :])
    c["bv"] = np.ascontiguousarray(bv[None, :])
    c["bo"] = np.ascontiguousarray(bo[None, :])
    c["ew"] = np.ascontiguousarray(expert_w)
    c["eb"] = np.ascontiguousarray(expert_b)
    c["eye"] = np.eye(128, dtype=ml_dtypes.bfloat16)
    inv = 10000.0 ** (-np.arange(0, DH, 2, dtype=np.float64) / DH)
    cos_l, sin_l = [], []
    for core in range(NCORE):
        pos = (core % 4) * TOK + np.arange(TOK, dtype=np.float64)
        fr = np.outer(pos, inv)
        cos_l.append(np.cos(fr).astype(np.float32))
        s = np.sin(fr).astype(np.float32)
        sin_l.append(np.ascontiguousarray(np.concatenate([-s, s], axis=1)))
    c["cos"] = cos_l
    c["sin2"] = sin_l
    return c


def run_expert_launch(Xp_list, c):
    nc2 = build_L2()
    in2 = []
    for cc in range(NCORE):
        xp8 = np.stack([Xp_list[src][cc * EL * PSL:(cc + 1) * EL * PSL]
                        for src in range(NCORE)])
        in2.append({"Xp8": np.ascontiguousarray(xp8),
                    "ew": np.ascontiguousarray(c["ew"][cc * EL:(cc + 1) * EL]),
                    "eb": np.ascontiguousarray(c["eb"][cc * EL:(cc + 1) * EL]),
                    "eye": c["eye"]})
    r2 = run_bass_kernel_spmd(nc2, in2, core_ids=list(range(NCORE)))
    return np.ascontiguousarray(
        np.concatenate([r2.results[cc]["Ytl"] for cc in range(NCORE)], axis=0))


def kernel(L0_attn_x, L0_attn_Wq, L0_attn_bq, L0_attn_Wk, L0_attn_bk, L0_attn_Wv,
           L0_attn_bv, L0_attn_Wo, L0_attn_bo, L0_ffn_phi_w, L0_ffn_phi_b,
           L0_ffn_expert_w, L0_ffn_expert_b):
    c = _prep_common(L0_attn_Wq, L0_attn_bq, L0_attn_Wk, L0_attn_bk, L0_attn_Wv,
                     L0_attn_bv, L0_attn_Wo, L0_attn_bo, L0_ffn_phi_w,
                     L0_ffn_phi_b, L0_ffn_expert_w, L0_ffn_expert_b)
    x0 = np.ascontiguousarray(np.asarray(L0_attn_x, dtype=np.float32)
                              .reshape(NCORE * TOK, D))
    eyef = np.eye(128, dtype=np.float32)
    cores = list(range(NCORE))

    nc1 = build_L1()
    in1 = [{"x": np.ascontiguousarray(x0[cc * TOK:(cc + 1) * TOK]),
            "phiT": c["phiT"], "phib": c["phib"], "eye": c["eye"]}
           for cc in cores]
    r1 = run_bass_kernel_spmd(nc1, in1, core_ids=cores)
    Xp1 = [r1.results[cc]["Xp"] for cc in cores]
    C1 = [r1.results[cc]["C"] for cc in cores]

    Ytl1 = run_expert_launch(Xp1, c)

    nc3 = build_L3()
    in3 = [{"x": in1[cc]["x"], "C": C1[cc], "Ytl": Ytl1,
            "WqT": c["WqT"], "WkT": c["WkT"], "WvT": c["WvT"],
            "bq": c["bq"], "bk": c["bk"], "bv": c["bv"],
            "cos": c["cos"][cc], "sin2": c["sin2"][cc],
            "eye": c["eye"], "eyef": eyef} for cc in cores]
    r3 = run_bass_kernel_spmd(nc3, in3, core_ids=cores)

    nc4 = build_L4()
    in4 = []
    for cc in cores:
        grp = range(4 * (cc // 4), 4 * (cc // 4) + 4)
        Kf = np.ascontiguousarray(
            np.concatenate([r3.results[g]["k"] for g in grp], axis=0))
        Vf = np.ascontiguousarray(
            np.concatenate([r3.results[g]["v"] for g in grp], axis=1))
        kn4 = np.ascontiguousarray(
            np.concatenate([r3.results[g]["norms"][:, 1] for g in grp])[None, :])
        qn = np.ascontiguousarray(r3.results[cc]["norms"][:, 0][None, :])
        in4.append({"x1": r3.results[cc]["x1"], "q": r3.results[cc]["q"],
                    "K": Kf, "V": Vf, "qn": qn, "kn4": kn4,
                    "WoT": c["WoT"], "bo": c["bo"],
                    "phiT": c["phiT"], "phib": c["phib"], "eye": c["eye"]})
    r4 = run_bass_kernel_spmd(nc4, in4, core_ids=cores)
    Xp2 = [r4.results[cc]["Xp2"] for cc in cores]
    C2 = [r4.results[cc]["C2"] for cc in cores]

    Ytl2 = run_expert_launch(Xp2, c)

    nc6 = build_L6()
    in6 = [{"x": r4.results[cc]["x2"], "C": C2[cc], "Ytl": Ytl2} for cc in cores]
    r6 = run_bass_kernel_spmd(nc6, in6, core_ids=cores)
    out = np.concatenate([r6.results[cc]["x3"] for cc in cores], axis=0)
    return out.reshape(2, SEQ, D)


# version 04
# speedup vs baseline: 1.4664x; 1.0712x over previous
import numpy as np
import ml_dtypes
from contextlib import ExitStack
import concourse.bass as bass
import concourse.mybir as mybir
from concourse.bass_utils import run_bass_kernel_spmd

F32 = mybir.dt.float32
BF16 = mybir.dt.bfloat16
AF = mybir.ActivationFunctionType
ALU = mybir.AluOpType
AX = mybir.AxisListType

NCORE = 8
TOK = 2048
NT = TOK // 128
D = 1024
DC = D // 128
SLOTS = 2048
NEXP = 64
PSL = 32
EL = NEXP // NCORE
NHEAD = 16
DH = 64
L4_DEBUG = False
SEQ = 8192
EPS = 1e-5


class Sched:

    def __init__(self):
        self.ops = []
        self.counts = {}

    AUTO_SIG = {"vector": "dve", "scalar": "act", "gpsimd": "pool"}

    def tok(self, sem):
        return (sem, self.counts.get(sem, 0))

    def do(self, eng, fn, after=(), sig=None, inc=1, chain=True):
        auto = self.AUTO_SIG.get(eng)
        if sig is None and auto is not None:
            sig = auto
        after = [t for t in after if t is not None]
        if chain and auto is not None and self.counts.get(auto, 0) > 0:
            after.append((auto, self.counts[auto]))
        after = tuple(after)
        for s, v in after:
            assert v <= self.counts.get(s, 0), (s, v, self.counts.get(s, 0))
        self.ops.append((eng, after, fn, sig or None, inc if sig else 0))
        if sig:
            self.counts[sig] = self.counts.get(sig, 0) + inc
            return (sig, self.counts[sig])
        return None

    def dma(self, eng, out, in_, after=(), sig="dma", **kw):
        return self.do(eng, lambda e: e.dma_start(out=out, in_=in_, **kw),
                       after=after, sig=sig, inc=16, chain=False)

    def wait(self, eng, after):
        self.do(eng, lambda e: None, after=after, sig=False, chain=False)

    def build(self, nc, stack):
        sems = {n: stack.enter_context(nc.semaphore("s_" + n)) for n in self.counts}
        per_eng = {}
        for eng, after, fn, sig, inc in self.ops:
            per_eng.setdefault(eng, []).append((after, fn, sig, inc))
        block = stack.enter_context(nc.Block())

        def mk(items):
            def body(e):
                waited = {}
                for after, fn, sig, inc in items:
                    for s, v in after:
                        if waited.get(s, 0) < v:
                            e.wait_ge(sems[s], v)
                            waited[s] = v
                    ins = fn(e)
                    if sig and ins is not None:
                        ins.then_inc(sems[sig], inc)
            return body
        for eng_name, items in per_eng.items():
            getattr(block, eng_name)(mk(items))


def emit_ln(S, x_ap, h_out_ap, scr, guard, out_dve_sig="dve"):
    mu, var, s2, rstd, xsq = scr["mu"], scr["var"], scr["s2"], scr["rstd"], scr["xsq"]
    S.do("vector", lambda e: e.reduce_sum(out=mu, in_=x_ap, axis=AX.X), after=guard)
    S.do("vector", lambda e: e.tensor_scalar_mul(out=mu, in0=mu, scalar1=1.0 / x_ap.shape[-1]))
    S.do("vector", lambda e: e.tensor_mul(out=xsq, in0=x_ap, in1=x_ap))
    S.do("vector", lambda e: e.reduce_sum(out=s2, in_=xsq, axis=AX.X))
    S.do("vector", lambda e: e.tensor_mul(out=var, in0=mu, in1=mu))
    t_v = S.do("vector", lambda e: e.scalar_tensor_tensor(
        out=var, in0=s2, scalar=1.0 / x_ap.shape[-1], in1=var,
        op0=ALU.mult, op1=ALU.subtract), sig="dve")
    S.do("scalar", lambda e: e.activation(out=s2, in_=var, func=AF.Ln, bias=scr["eps"]),
         after=[t_v])
    t_act = S.do("scalar", lambda e: e.activation(out=rstd, in_=s2, func=AF.Exp,
                                                  scale=-0.5), sig="act")
    t_h = S.do("vector", lambda e: e.tensor_scalar(
        out=h_out_ap, in0=x_ap, scalar1=mu, scalar2=rstd,
        op0=ALU.subtract, op1=ALU.mult), after=[t_act], sig=out_dve_sig)
    return t_h, t_act


def emit_ffn_A_tile(S, m, b, x_tile_ap, after_x):
    hm = b["h_all"][:, m * D:(m + 1) * D]
    t_h, t_ln_act = emit_ln(S, x_tile_ap, hm, b["lnscr"],
                            list(after_x) + [b["ln_free"]])
    b["ln_free"] = t_ln_act
    hT = b["hT"]
    ev_toks = []
    tr_toks = []
    for k in range(DC):
        ps = b["tr_ps"][k % 2]
        after = [t_h] + ([ev_toks[k - 2]] if k >= 2 else [b["trps_free"][k % 2]])
        t_tr = S.do("tensor", lambda e, k=k, ps=ps: e.transpose(
            ps[:, 0:128], b["h_all"][:, m * D + k * 128: m * D + (k + 1) * 128],
            b["eye"]), after=after, sig="pe")
        tr_toks.append(t_tr)
        ev = S.do("vector", lambda e, k=k, ps=ps: e.tensor_copy(
            out=hT[:, k * 128:(k + 1) * 128], in_=ps[:, 0:128]),
            after=[t_tr], sig="dve")
        ev_toks.append(ev)
    b["trps_free"] = [ev_toks[DC - 2], ev_toks[DC - 1]]
    lp = b["logits_ps"]
    for k in range(DC):
        for j in range(4):
            after = []
            if k == 0:
                after = [ev_toks[k], b["logits_free"]] + ([b["t_ones"]] if m == 0 else [])
            elif j == 0:
                after = [ev_toks[k]]
            S.do("tensor", lambda e, k=k, j=j: e.matmul(
                lp[:, j * 512:(j + 1) * 512],
                hT[:, k * 128:(k + 1) * 128],
                b["phiT"][:, k * SLOTS + j * 512: k * SLOTS + (j + 1) * 512],
                start=(k == 0), stop=False), after=after)
    t_mm = None
    for j in range(4):
        t_mm = S.do("tensor", lambda e, j=j: e.matmul(
            lp[:, j * 512:(j + 1) * 512], b["ones1"],
            b["phib"][:, j * 512:(j + 1) * 512], start=False, stop=True), sig="pe")
    t_negm = S.do("vector", lambda e: e.reduce_max(
        out=b["negm"], in_=lp[:], axis=AX.X, negate=True), after=[t_mm], sig="dve")
    E = b["E_t"][m % 2]
    t_E = S.do("scalar", lambda e: e.activation(
        out=E[:], in_=lp[:], func=AF.Exp, bias=b["negm"]),
        after=[t_negm] + b["E_free"][m % 2], sig="act")
    b["logits_free"] = t_E
    Ev = E[:].rearrange("p (n q) -> p n q", n=NEXP)
    Evt = E[:].rearrange("p (n q) -> p q n", n=NEXP)
    S.do("vector", lambda e: e.reduce_sum(out=b["sp"], in_=Ev, axis=AX.X), after=[t_E])
    S.do("vector", lambda e: e.reciprocal(out=b["rp"], in_=b["sp"]))
    S.do("vector", lambda e: e.reduce_sum(out=b["sn"], in_=Evt, axis=AX.X))
    t_rn = S.do("vector", lambda e: e.reciprocal(out=b["rn"], in_=b["sn"]), sig="dve")
    Dv = b["D_all"][:, m * SLOTS:(m + 1) * SLOTS].rearrange("p (n q) -> p n q", n=NEXP)
    t_D = S.do("vector", lambda e: e.tensor_tensor(
        out=Dv, in0=Ev, in1=b["rp"].unsqueeze(2).broadcast_to([128, NEXP, PSL]),
        op=ALU.mult), after=[t_rn], sig="dve")
    C = b["C_t"][m % 2]
    Cv = C[:].rearrange("p (n q) -> p n q", n=NEXP)
    t_C = S.do("gpsimd", lambda e: e.tensor_tensor(
        out=Cv, in0=Ev, in1=b["rn"].unsqueeze(1).broadcast_to([128, NEXP, PSL]),
        op=ALU.mult), after=[t_rn, b["C_free"][m % 2]], sig="pool")
    t_Cout = S.dma("sync", b["C_out"][m * 128:(m + 1) * 128, :], C[:],
                   after=[t_C], sig="dmaout")
    b["C_free"][m % 2] = t_Cout
    b["E_free"][m % 2] = [t_D, t_C]
    return t_h


def emit_ffn_AB(S, nc, st, b, get_x_tile, C_out, Xp_out):
    sb = lambda name, shape, dt: st.enter_context(nc.sbuf_tensor(name, shape, dt))
    uid = b["uid"]
    b["h_all"] = sb(f"h_all{uid}", [128, NT * D], BF16)[:]
    b["hT"] = sb(f"hT{uid}", [128, D], BF16)[:]
    b["D_all"] = sb(f"D_all{uid}", [128, NT * SLOTS], BF16)[:]
    b["E_t"] = [sb(f"E{uid}_{i}", [128, SLOTS], BF16) for i in range(2)]
    b["C_t"] = [sb(f"Ct{uid}_{i}", [128, SLOTS], BF16) for i in range(2)]
    b["negm"] = sb(f"negm{uid}", [128, 1], F32)[:]
    b["sp"] = sb(f"sp{uid}", [128, NEXP], F32)[:]
    b["rp"] = sb(f"rp{uid}", [128, NEXP], F32)[:]
    b["sn"] = sb(f"sn{uid}", [128, PSL], F32)[:]
    b["rn"] = sb(f"rn{uid}", [128, PSL], F32)[:]
    b["C_out"] = C_out
    b["C_free"] = [None, None]
    b["E_free"] = [[], []]
    b["logits_free"] = None
    b["ln_free"] = None
    b["trps_free"] = [None, None]

    ps_stack = ExitStack()
    b["tr_ps"] = [ps_stack.enter_context(nc.psum_tensor(f"tr{uid}_{i}", [128, 1024], BF16))
                  for i in range(2)]
    b["logits_ps"] = ps_stack.enter_context(nc.psum_tensor(f"lg{uid}", [128, SLOTS], F32))[:]

    for m in range(NT):
        x_ap, after_x = get_x_tile(m)
        t_h = emit_ffn_A_tile(S, m, b, x_ap, after_x)
        if "x_done" in b:
            b["x_done"](m, t_h)

    t_lastA = [S.tok("dve"), S.tok("pool")]
    ps_stack.close()

    Xps = [st.enter_context(nc.psum_tensor(f"Xps{uid}_{j}", [128, D], F32))
           for j in range(4)]
    Xev = [sb(f"Xev{uid}_{j}", [128, D], F32) for j in range(2)]
    bank_free = [None] * 4
    xev_dma = [None, None]
    for s in range(4):
        for m in range(NT):
            for j in range(4):
                slot0 = s * 512 + j * 128
                after = []
                if m == 0:
                    after = (t_lastA if s == 0 else []) + [bank_free[j]]
                for half in range(2):
                    S.do("tensor", lambda e, m=m, j=j, half=half, slot0=slot0: e.matmul(
                        Xps[j][:, half * 512:(half + 1) * 512],
                        b["D_all"][:, m * SLOTS + slot0: m * SLOTS + slot0 + 128],
                        b["h_all"][:, m * D + half * 512: m * D + (half + 1) * 512],
                        start=(m == 0), stop=(m == NT - 1)),
                        after=after if half == 0 else (),
                        sig="pe" if (m == NT - 1 and half == 1) else None)
        t_s = S.tok("pe")
        for j in range(4):
            t_ev = S.do("vector", lambda e, j=j: e.tensor_copy(
                out=Xev[j % 2][:], in_=Xps[j][:]), after=[t_s, xev_dma[j % 2]],
                sig="dve")
            bank_free[j] = t_ev
            xev_dma[j % 2] = S.dma(
                "sync", Xp_out[s * 512 + j * 128: s * 512 + (j + 1) * 128, :],
                Xev[j % 2][:], after=[t_ev], sig="dmaout")
    return S.tok("dmaout")


def build_L1():
    nc = bass.Bass("TRN2", target_bir_lowering=False)
    x_d = nc.dram_tensor("x", [TOK, D], F32, kind="ExternalInput").ap()
    phiT_d = nc.dram_tensor("phiT", [D, SLOTS], F32, kind="ExternalInput").ap()
    phib_d = nc.dram_tensor("phib", [1, SLOTS], F32, kind="ExternalInput").ap()
    eye_d = nc.dram_tensor("eye", [128, 128], BF16, kind="ExternalInput").ap()
    Xp_d = nc.dram_tensor("Xp", [SLOTS, D], F32, kind="ExternalOutput").ap()
    C_d = nc.dram_tensor("C", [TOK, SLOTS], BF16, kind="ExternalOutput").ap()

    S = Sched()
    with ExitStack() as st:
        sb = lambda name, shape, dt: st.enter_context(nc.sbuf_tensor(name, shape, dt))
        xt = [sb(f"xt{i}", [128, D], F32) for i in range(2)]
        phiT = sb("phiT_s", [128, DC * SLOTS], BF16)
        phib = sb("phib_s", [1, SLOTS], BF16)
        ones1 = sb("ones1", [1, 128], BF16)
        eye = sb("eye_s", [128, 128], BF16)
        lnscr = {k: sb("ln_" + k, [128, 1], F32)[:] for k in ("mu", "var", "s2", "rstd")}
        lnscr["xsq"] = sb("ln_xsq", [128, D], F32)[:]
        lnscr["eps"] = sb("ln_eps", [128, 1], F32)[:]
        S.do("vector", lambda e: e.memset(lnscr["eps"], EPS))

        b = dict(uid="a", phiT=phiT[:], phib=phib[:], ones1=ones1[:], eye=eye[:],
                 lnscr=lnscr)
        b["t_ones"] = S.do("vector", lambda e: e.memset(ones1[:], 1.0), sig="dve")
        t_eye = S.dma("sync", eye[:], eye_d, sig="ld")
        for k in range(DC):
            S.dma("gpsimd", phiT[:, k * SLOTS:(k + 1) * SLOTS],
                  phiT_d[k * 128:(k + 1) * 128, :], sig="ldw")
        S.dma("gpsimd", phib[:], phib_d, sig="ldw")
        t_w = S.tok("ldw")

        x_free = [None, None]

        def get_x_tile(m):
            t_x = S.dma("sync", xt[m % 2][:], x_d[m * 128:(m + 1) * 128, :],
                        after=[x_free[m % 2]], sig="ldx")
            return xt[m % 2][:], [t_x, t_w, t_eye]

        def x_done(m, t_h):
            x_free[m % 2] = t_h
        b["x_done"] = x_done

        t_out = emit_ffn_AB(S, nc, st, b, get_x_tile, C_d, Xp_d)
        S.wait("sync", [t_out, S.tok("dmaout")])
        S.build(nc, st)
    return nc


def build_L2():
    nc = bass.Bass("TRN2", target_bir_lowering=False)
    Xp8_d = nc.dram_tensor("Xp8", [NCORE, EL * PSL, D], F32, kind="ExternalInput").ap()
    ew_d = nc.dram_tensor("ew", [EL, D, D], F32, kind="ExternalInput").ap()
    eb_d = nc.dram_tensor("eb", [EL, D], F32, kind="ExternalInput").ap()
    eye_d = nc.dram_tensor("eye", [128, 128], BF16, kind="ExternalInput").ap()
    Y_d = nc.dram_tensor("Ytl", [EL * PSL, D], F32, kind="ExternalOutput").ap()

    S = Sched()
    with ExitStack() as st:
        sb = lambda name, shape, dt: st.enter_context(nc.sbuf_tensor(name, shape, dt))
        xp = sb("xp_s", [128, NCORE * 2 * D], F32)
        xs = sb("xs", [128, 2 * D], F32)
        xn = sb("xn", [128, 2 * D], F32)
        xb = sb("xb", [128, 2 * D], BF16)
        XT = sb("XT", [128, 2 * DC * 128], BF16)
        ew = [sb(f"ew{i}", [128, DC * D], BF16) for i in range(2)]
        ebr_s = sb("ebr_s", [1, EL * D], BF16)
        ebr = ebr_s
        ones1 = sb("ones1", [1, 128], BF16)
        eye = sb("eye_s", [128, 128], BF16)
        Yev = sb("Yev", [32, EL * D], F32)
        lnscr = {k: sb("ln_" + k, [128, 1], F32)[:] for k in ("mu", "var", "s2", "rstd")}
        lnscr["xsq"] = sb("ln_xsq", [128, D], F32)[:]
        lnscr["eps"] = sb("ln_eps", [128, 1], F32)[:]
        S.do("vector", lambda e: e.memset(lnscr["eps"], EPS))

        tr_ps = [st.enter_context(nc.psum_tensor(f"tr{i}", [128, 1024], BF16))
                 for i in range(2)]
        Y_ps = [st.enter_context(nc.psum_tensor(f"Y{i}", [32, D], F32))
                for i in range(2)]

        t_ones = S.do("vector", lambda e: e.memset(ones1[:], 1.0), sig="dve")
        t_eye = S.dma("sync", eye[:], eye_d, sig="ld")
        for src in range(NCORE):
            for half in range(2):
                S.dma("sync", xp[:, (src * 2 + half) * D:(src * 2 + half + 1) * D],
                      Xp8_d[src, half * 128:(half + 1) * 128, :], sig="ldx")
        t_x = S.tok("ldx")
        for n in range(EL):
            S.dma("gpsimd", ebr[:, n * D:(n + 1) * D], eb_d[n:n + 1, :], sig="ldb")
        t_eb = S.tok("ldb")
        ew_ready = [None] * EL
        for n in range(2):
            for k in range(DC):
                S.dma("gpsimd", ew[n][:, k * D:(k + 1) * D],
                      ew_d[n, k * 128:(k + 1) * 128, :], sig="ldw")
            ew_ready[n] = S.tok("ldw")

        ln_free = None
        t_tanh = [None, None]
        ev_done = []
        for half in range(2):
            hs = slice(half * D, (half + 1) * D)
            S.do("vector", lambda e, hs=hs: e.tensor_add(
                out=xs[:, hs], in0=xp[:, hs],
                in1=xp[:, 2 * D + hs.start: 2 * D + hs.stop]), after=[t_x])
            for src in range(2, NCORE):
                S.do("vector", lambda e, hs=hs, src=src: e.tensor_add(
                    out=xs[:, hs], in0=xs[:, hs],
                    in1=xp[:, src * 2 * D + hs.start: src * 2 * D + hs.stop]))
            S.do("vector", lambda e, hs=hs: e.tensor_scalar(
                out=xs[:, hs], in0=xs[:, hs], scalar1=-33000.0, scalar2=65000.0,
                op0=ALU.max, op1=ALU.min))
            t_h, t_act = emit_ln(S, xs[:, hs], xn[:, hs], lnscr,
                                 [ln_free])
            t_t = S.do("scalar", lambda e, hs=hs: e.activation(
                out=xb[:, hs], in_=xn[:, hs], func=AF.Tanh), after=[t_h], sig="act")
            ln_free = t_t
            t_tanh[half] = t_t
            for k in range(DC):
                ps = tr_ps[k % 2]
                after = [t_t, t_eye]
                if len(ev_done) >= 2:
                    after.append(ev_done[-2])
                t_tr = S.do("tensor", lambda e, hs=hs, k=k, ps=ps: e.transpose(
                    ps[:, 0:128], xb[:, hs.start + k * 128: hs.start + (k + 1) * 128],
                    eye[:]), after=after, sig="pe")
                ev = S.do("vector", lambda e, half=half, k=k, ps=ps: e.tensor_copy(
                    out=XT[:, (half * DC + k) * 128:(half * DC + k + 1) * 128],
                    in_=ps[:, 0:128]), after=[t_tr], sig="dve")
                ev_done.append(ev)
        t_XT = S.tok("dve")

        y_free = [None, None]
        for n in range(EL):
            half, j = divmod(n, 4)
            ps = Y_ps[n % 2]
            base = (half * DC) * 128
            for k in range(DC):
                lhs = XT[:, (half * DC + k) * 128 + j * 32:
                         (half * DC + k) * 128 + (j + 1) * 32]
                for hh in range(2):
                    after = ()
                    if k == 0 and hh == 0:
                        after = [t_XT, ew_ready[n], y_free[n % 2], t_ones, t_eb]
                    S.do("tensor", lambda e, k=k, hh=hh, lhs=lhs, ps=ps, n=n: e.matmul(
                        ps[:, hh * 512:(hh + 1) * 512], lhs,
                        ew[n % 2][:, k * D + hh * 512: k * D + (hh + 1) * 512],
                        start=(k == 0), stop=False), after=after)
            t_pe = None
            for hh in range(2):
                t_pe = S.do("tensor", lambda e, n=n, hh=hh, ps=ps: e.matmul(
                    ps[:, hh * 512:(hh + 1) * 512], ones1[:, 0:32],
                    ebr[:, n * D + hh * 512: n * D + (hh + 1) * 512],
                    start=False, stop=True), sig="pe")
            t_ev = S.do("vector", lambda e, n=n, ps=ps: e.tensor_copy(
                out=Yev[:, n * D:(n + 1) * D], in_=ps[:]), after=[t_pe], sig="dve")
            y_free[n % 2] = t_ev
            if n + 2 < EL:
                for k in range(DC):
                    S.dma("gpsimd", ew[n % 2][:, k * D:(k + 1) * D],
                          ew_d[n + 2, k * 128:(k + 1) * 128, :],
                          after=[t_ev] if k == 0 else (), sig="ldw")
                ew_ready[n + 2] = S.tok("ldw")
            S.dma("sync", Y_d[n * PSL:(n + 1) * PSL, :],
                  Yev[0:PSL, n * D:(n + 1) * D], after=[t_ev], sig="dmaout")
        S.wait("sync", [S.tok("dmaout")])
        S.build(nc, st)
    return nc



def emit_ffn_D(S, nc, st, uid, C_d, Ytl_d, x_d, xout_d, tile_cb=None, psum_pp=2):
    sb = lambda name, shape, dt: st.enter_context(nc.sbuf_tensor(name, shape, dt))
    CT = sb(f"CT{uid}", [128, 16 * SLOTS], BF16)
    Ytl = sb(f"Ytl{uid}", [128, 16 * D], BF16)
    xt = [sb(f"xt{uid}{i}", [128, D], F32) for i in range(2)]
    x1t = [sb(f"x1t{uid}{i}", [128, D], F32) for i in range(2)]
    comb = [st.enter_context(nc.psum_tensor(f"comb{uid}{i}", [128, D], F32))
            for i in range(psum_pp)]
    for sc in range(16):
        S.dma("sync", CT[:, sc * SLOTS:(sc + 1) * SLOTS],
              C_d[0:TOK, sc * 128:(sc + 1) * 128], transpose=True, sig="ldct")
        S.dma("gpsimd", Ytl[:, sc * D:(sc + 1) * D],
              Ytl_d[sc * 128:(sc + 1) * 128, :], sig="ldyt")
    t_ct = S.tok("ldct")
    t_yt = S.tok("ldyt")
    x_free = [None, None]
    x1_free = [None, None]
    comb_free = [None] * psum_pp
    for m in range(NT):
        t_x = S.dma("sync", xt[m % 2][:], x_d[m * 128:(m + 1) * 128, :],
                    after=[x_free[m % 2]], sig="ldx" + uid)
        ps = comb[m % psum_pp]
        for sc in range(16):
            for hh in range(2):
                after = ()
                if sc == 0 and hh == 0:
                    after = [t_ct, t_yt, comb_free[m % psum_pp]]
                S.do("tensor", lambda e, m=m, sc=sc, hh=hh, ps=ps: e.matmul(
                    ps[:, hh * 512:(hh + 1) * 512],
                    CT[:, sc * SLOTS + m * 128: sc * SLOTS + (m + 1) * 128],
                    Ytl[:, sc * D + hh * 512: sc * D + (hh + 1) * 512],
                    start=(sc == 0), stop=(sc == 15)),
                    after=after, sig="pe" if (sc == 15 and hh == 1) else None)
        t_mm = S.tok("pe")
        t_x1 = S.do("vector", lambda e, m=m, ps=ps: e.scalar_tensor_tensor(
            out=x1t[m % 2][:], in0=ps[:], scalar=0.5, in1=xt[m % 2][:],
            op0=ALU.mult, op1=ALU.add),
            after=[t_mm, t_x, x_free[m % 2], x1_free[m % 2]])
        comb_free[m % psum_pp] = t_x1
        x_free[m % 2] = t_x1
        t_o = S.dma("sync", xout_d[m * 128:(m + 1) * 128, :], x1t[m % 2][:],
                    after=[t_x1], sig="dmaout")
        x1_free[m % 2] = t_o
        if tile_cb is not None:
            tile_cb(m, x1t[m % 2][:], t_x1)
    return S.tok("dmaout")


def build_L3():
    nc = bass.Bass("TRN2", target_bir_lowering=False)
    x_d = nc.dram_tensor("x", [TOK, D], F32, kind="ExternalInput").ap()
    C_d = nc.dram_tensor("C", [TOK, SLOTS], BF16, kind="ExternalInput").ap()
    Ytl_d = nc.dram_tensor("Ytl", [SLOTS, D], F32, kind="ExternalInput").ap()
    W_d = {w: nc.dram_tensor(w, [D, D], F32, kind="ExternalInput").ap()
           for w in ("WqT", "WkT", "WvT")}
    b_d = {w: nc.dram_tensor(w, [1, D], F32, kind="ExternalInput").ap()
           for w in ("bq", "bk", "bv")}
    cos_d = nc.dram_tensor("cos", [TOK, 32], F32, kind="ExternalInput").ap()
    sin_d = nc.dram_tensor("sin2", [TOK, 64], F32, kind="ExternalInput").ap()
    eye_d = nc.dram_tensor("eye", [128, 128], BF16, kind="ExternalInput").ap()
    eyef_d = nc.dram_tensor("eyef", [128, 128], F32, kind="ExternalInput").ap()
    x1_d = nc.dram_tensor("x1", [TOK, D], F32, kind="ExternalOutput").ap()
    q_d = nc.dram_tensor("q", [TOK, D], BF16, kind="ExternalOutput").ap()
    k_d = nc.dram_tensor("k", [TOK, D], BF16, kind="ExternalOutput").ap()
    v_d = nc.dram_tensor("v", [8, TOK, 130], BF16, kind="ExternalOutput").ap()
    nrm_d = nc.dram_tensor("norms", [NHEAD, 2], F32, kind="ExternalOutput").ap()
    bvscr_d = nc.dram_tensor("bvscr", [1, D], F32, kind="Internal").ap()

    S = Sched()
    with ExitStack() as st:
        sb = lambda name, shape, dt: st.enter_context(nc.sbuf_tensor(name, shape, dt))
        W = {w: sb(w + "_s", [128, DC * D], BF16) for w in ("WqT", "WkT", "WvT")}
        br = {w: sb(w + "_s", [1, D], BF16) for w in ("bq", "bk", "bv")}
        bvb = sb("bvb", [128, D], F32)
        ones1 = sb("ones1", [1, 128], BF16)
        eye = sb("eye_s", [128, 128], BF16)
        eyef = sb("eyef_s", [128, 128], F32)
        cos_sb = sb("cos_sb", [128, NT * 32], F32)
        sin_sb = sb("sin_sb", [128, NT * 64], F32)
        g_t = sb("g_t", [128, D], BF16)
        gT = sb("gT", [128, D], BF16)
        t1f = sb("t1f", [128, D], F32)
        t2f = sb("t2f", [128, D], F32)
        qk_bf = [sb(f"qkbf{i}", [128, D], BF16) for i in range(2)]
        v_t = [sb(f"vt{i}", [128, 16 * 65], BF16) for i in range(2)]
        sqs = sb("sqs", [128, D], F32)
        red16 = sb("red16", [128, NHEAD], F32)
        qn_run = sb("qn_run", [128, NHEAD], F32)
        kn_run = sb("kn_run", [128, NHEAD], F32)
        nrm_sb = sb("nrm_sb", [16, 2], F32)
        lnscr = {k2: sb("ln_" + k2, [128, 1], F32)[:] for k2 in ("mu", "var", "s2", "rstd")}
        lnscr["xsq"] = sb("ln_xsq", [128, D], F32)[:]
        lnscr["eps"] = sb("ln_eps", [128, 1], F32)[:]
        S.do("vector", lambda e: e.memset(lnscr["eps"], EPS))
        t_ones = S.do("vector", lambda e: e.memset(ones1[:], 1.0))
        S.do("vector", lambda e: e.memset(qn_run[:], 0.0))
        t_nrm0 = S.do("vector", lambda e: e.memset(kn_run[:], 0.0))

        t_eye = S.dma("sync", eye[:], eye_d, sig="ld")
        S.dma("sync", eyef[:], eyef_d, sig="ld")
        for m in range(NT):
            S.dma("sync", cos_sb[:, m * 32:(m + 1) * 32],
                  cos_d[m * 128:(m + 1) * 128, :], sig="ld")
            S.dma("sync", sin_sb[:, m * 64:(m + 1) * 64],
                  sin_d[m * 128:(m + 1) * 128, :], sig="ld")
        BIAS = {"WqT": "bq", "WkT": "bk", "WvT": "bv"}
        for w in ("WqT", "WkT", "WvT"):
            for kk in range(DC):
                S.dma("gpsimd", W[w][:, kk * D:(kk + 1) * D],
                      W_d[w][kk * 128:(kk + 1) * 128, :], sig="ldw")
            S.dma("gpsimd", br[BIAS[w]][:], b_d[BIAS[w]], sig="ldw")
        t_bv = S.dma("sync", bvscr_d, b_d["bv"], sig="ld")
        S.dma("sync", bvb[:], bvscr_d.broadcast_to([128, D]), after=[t_bv], sig="ld")
        t_ld = S.tok("ld")
        t_w = S.tok("ldw")

        tr_ps = [st.enter_context(nc.psum_tensor(f"trq{i}", [128, 1024], BF16))
                 for i in range(2)]
        qkv_ps = [st.enter_context(nc.psum_tensor(f"qkv{i}", [128, D], F32))
                  for i in range(2)]

        state = dict(trps_free=[None, None], qkv_free=[None, None],
                     qk_free=[None, None], vt_free=[None, None], ln_free=None,
                     gT_free=None)

        def qkv_tile(m, x1_ap, t_x1):
            t_g, t_act = emit_ln(S, x1_ap, g_t[:], lnscr,
                                 [t_x1, state["ln_free"]])
            state["ln_free"] = t_act
            ev_toks = []
            for kk in range(DC):
                ps = tr_ps[kk % 2]
                after = [t_g, t_eye] +                     ([ev_toks[kk - 2]] if kk >= 2 else [state["trps_free"][kk % 2]])
                if kk == 0:
                    after.append(state["gT_free"])
                t_tr = S.do("tensor", lambda e, kk=kk, ps=ps: e.transpose(
                    ps[:, 0:128], g_t[:, kk * 128:(kk + 1) * 128], eye[:]),
                    after=after, sig="pe")
                ev = S.do("vector", lambda e, kk=kk, ps=ps: e.tensor_copy(
                    out=gT[:, kk * 128:(kk + 1) * 128], in_=ps[:, 0:128]),
                    after=[t_tr])
                ev_toks.append(ev)
            state["trps_free"] = [ev_toks[DC - 2], ev_toks[DC - 1]]
            t_gT = ev_toks[-1]
            for pi, w in enumerate(("WqT", "WkT", "WvT")):
                ps = qkv_ps[pi % 2]
                for kk in range(DC):
                    for hh in range(2):
                        after = ()
                        if kk == 0 and hh == 0:
                            after = [t_gT, state["qkv_free"][pi % 2],
                                     t_w if m == 0 else None,
                                     t_ones if m == 0 else None]
                        S.do("tensor", lambda e, w=w, kk=kk, hh=hh, ps=ps: e.matmul(
                            ps[:, hh * 512:(hh + 1) * 512],
                            gT[:, kk * 128:(kk + 1) * 128],
                            W[w][:, kk * D + hh * 512: kk * D + (hh + 1) * 512],
                            start=(kk == 0), stop=False), after=after)
                t_p = None
                for hh in range(2):
                    t_p = S.do("tensor", lambda e, w=w, hh=hh, ps=ps: e.matmul(
                        ps[:, hh * 512:(hh + 1) * 512], ones1[:],
                        br[{"WqT": "bq", "WkT": "bk", "WvT": "bv"}[w]][:, hh * 512:(hh + 1) * 512],
                        start=False, stop=True), sig="pe")
                if w == "WvT":
                    vt = v_t[m % 2]
                    t_ms = S.do("vector", lambda e, vt=vt: e.memset(vt[:], 1.0),
                                after=[state["vt_free"][m % 2]])
                    t_v = S.do("vector", lambda e, vt=vt, ps=ps: e.tensor_copy(
                        out=vt[:].rearrange("p (h g) -> p h g", g=65)[:, :, 0:64],
                        in_=ps[:].rearrange("p (h g) -> p h g", g=64)),
                        after=[t_p, t_ms])
                    for hg in range(8):
                        S.dma("sync", v_d[hg, m * 128:(m + 1) * 128, :],
                              vt[:, hg * 130:(hg + 1) * 130], after=[t_v],
                              sig="dmaout")
                    state["vt_free"][m % 2] = t_v
                    state["qkv_free"][pi % 2] = t_v
                else:
                    qb = qk_bf[pi]
                    Pv = ps[:].rearrange("p (h s f) -> p h s f", h=NHEAD, s=2)
                    cosv = cos_sb[:, m * 32:(m + 1) * 32].unsqueeze(1).unsqueeze(2) \
                        .broadcast_to([128, NHEAD, 2, 32])
                    sinv = sin_sb[:, m * 64:(m + 1) * 64] \
                        .rearrange("p (s f) -> p s f", s=2).unsqueeze(1) \
                        .broadcast_to([128, NHEAD, 2, 32])
                    t1v = t1f[:].rearrange("p (h s f) -> p h s f", h=NHEAD, s=2)
                    t2v = t2f[:].rearrange("p (h s f) -> p h s f", h=NHEAD, s=2)
                    S.do("vector", lambda e, Pv=Pv, cosv=cosv, t1v=t1v: e.tensor_tensor(
                        out=t1v, in0=Pv, in1=cosv, op=ALU.mult), after=[t_p, t_ld])
                    S.do("vector", lambda e, Pv=Pv, sinv=sinv, t2v=t2v: e.tensor_tensor(
                        out=t2v[:, :, 0, :], in0=Pv[:, :, 1, :],
                        in1=sinv[:, :, 0, :], op=ALU.mult))
                    S.do("vector", lambda e, Pv=Pv, sinv=sinv, t2v=t2v: e.tensor_tensor(
                        out=t2v[:, :, 1, :], in0=Pv[:, :, 0, :],
                        in1=sinv[:, :, 1, :], op=ALU.mult))
                    t_qb = S.do("vector", lambda e, qb=qb: e.tensor_add(
                        out=qb[:], in0=t1f[:], in1=t2f[:]),
                        after=[state["qk_free"][pi]])
                    state["qkv_free"][pi % 2] = t_qb
                    t_sq = S.do("gpsimd", lambda e, qb=qb: e.tensor_tensor(
                        out=sqs[:], in0=qb[:], in1=qb[:], op=ALU.mult), after=[t_qb])
                    S.do("vector", lambda e: e.reduce_sum(
                        out=red16[:], in_=sqs[:].rearrange("p (h f) -> p h f", h=NHEAD),
                        axis=AX.X), after=[t_sq])
                    run = qn_run if w == "WqT" else kn_run
                    S.do("vector", lambda e, run=run: e.tensor_max(
                        out=run[:], in0=run[:], in1=red16[:]),
                        after=[t_nrm0] if m == 0 else ())
                    out_d = q_d if w == "WqT" else k_d
                    t_dq = S.dma("sync", out_d[m * 128:(m + 1) * 128, :], qb[:],
                                 after=[t_qb], sig="dmaout")
                    state["qk_free"][pi] = t_dq

        t_dout = emit_ffn_D(S, nc, st, "d", C_d, Ytl_d, x_d, x1_d,
                            tile_cb=qkv_tile, psum_pp=1)
        t_last = [S.tok("dve"), S.tok("pool"), S.tok("pe")]
        for i, run in enumerate((qn_run, kn_run)):
            t_tr = S.do("tensor", lambda e, run=run: e.transpose(
                qkv_ps[1][0:16, 0:128], run[:], eyef[:]),
                after=t_last + [S.tok("dve")], sig="pe")
            S.do("vector", lambda e, i=i: e.reduce_max(
                out=nrm_sb[:, i:i + 1], in_=qkv_ps[1][0:16, 0:128], axis=AX.X),
                after=[t_tr])
        t_n = S.tok("dve")
        S.dma("sync", nrm_d, nrm_sb[:], after=[t_n], sig="dmaout")
        S.wait("sync", [S.tok("dmaout")])
        S.build(nc, st)
    return nc


def build_L6():
    nc = bass.Bass("TRN2", target_bir_lowering=False)
    x_d = nc.dram_tensor("x", [TOK, D], F32, kind="ExternalInput").ap()
    C_d = nc.dram_tensor("C", [TOK, SLOTS], BF16, kind="ExternalInput").ap()
    Ytl_d = nc.dram_tensor("Ytl", [SLOTS, D], F32, kind="ExternalInput").ap()
    x3_d = nc.dram_tensor("x3", [TOK, D], F32, kind="ExternalOutput").ap()
    S = Sched()
    with ExitStack() as st:
        t_out = emit_ffn_D(S, nc, st, "f", C_d, Ytl_d, x_d, x3_d, psum_pp=2)
        S.wait("sync", [S.tok("dmaout")])
        S.build(nc, st)
    return nc



def build_L4():
    nc = bass.Bass("TRN2", target_bir_lowering=False)
    x1_d = nc.dram_tensor("x1", [TOK, D], F32, kind="ExternalInput").ap()
    q_d = nc.dram_tensor("q", [TOK, D], BF16, kind="ExternalInput").ap()
    K_d = nc.dram_tensor("K", [SEQ, D], BF16, kind="ExternalInput").ap()
    V_d = nc.dram_tensor("V", [8, SEQ, 130], BF16, kind="ExternalInput").ap()
    qn_d = nc.dram_tensor("qn", [1, NHEAD], F32, kind="ExternalInput").ap()
    kn4_d = nc.dram_tensor("kn4", [1, 4 * NHEAD], F32, kind="ExternalInput").ap()
    WoT_d = nc.dram_tensor("WoT", [D, D], F32, kind="ExternalInput").ap()
    bo_d = nc.dram_tensor("bo", [1, D], F32, kind="ExternalInput").ap()
    phiT_d = nc.dram_tensor("phiT", [D, SLOTS], F32, kind="ExternalInput").ap()
    phib_d = nc.dram_tensor("phib", [1, SLOTS], F32, kind="ExternalInput").ap()
    eye_d = nc.dram_tensor("eye", [128, 128], BF16, kind="ExternalInput").ap()
    x2_d = nc.dram_tensor("x2", [TOK, D], F32, kind="ExternalOutput").ap()
    Xp2_d = nc.dram_tensor("Xp2", [SLOTS, D], F32, kind="ExternalOutput").ap()
    C2_d = nc.dram_tensor("C2", [TOK, SLOTS], BF16, kind="ExternalOutput").ap()
    aTdbg_d = (nc.dram_tensor("aTdbg", [128, DC * TOK], BF16,
                              kind="ExternalOutput").ap() if L4_DEBUG else None)
    nmdbg_d = (nc.dram_tensor("nmdbg", [128, NHEAD], F32,
                              kind="ExternalOutput").ap() if L4_DEBUG else None)
    rscr_d = [nc.dram_tensor(f"rscr{i}", [1, TOK // 2], F32, kind="Internal").ap()
              for i in range(2)]

    S = Sched()
    with ExitStack() as st:
        sb = lambda name, shape, dt: st.enter_context(nc.sbuf_tensor(name, shape, dt))
        negmb = sb("negmb", [128, NHEAD], F32)
        aT = sb("aT", [128, DC * TOK], BF16)

        nrq = sb("nrq", [128, NHEAD], F32)
        nrk = sb("nrk", [128, 4 * NHEAD], F32)
        t_qn = S.dma("sync", nrq[:], qn_d.broadcast_to([128, NHEAD]), sig="ld")
        t_kn = S.dma("sync", nrk[:], kn4_d.broadcast_to([128, 4 * NHEAD]), sig="ld")
        knv = nrk[:].rearrange("p (c h) -> p h c", c=4)
        S.do("vector", lambda e: e.reduce_max(
            out=negmb[:], in_=knv, axis=AX.X), after=[t_qn, t_kn])
        t_mul = S.do("vector", lambda e: e.tensor_mul(
            out=negmb[:], in0=negmb[:], in1=nrq[:]))
        S.do("scalar", lambda e: e.activation(
            out=negmb[:], in_=negmb[:], func=AF.Ln), after=[t_mul])
        t_ex = S.do("scalar", lambda e: e.activation(
            out=negmb[:], in_=negmb[:], func=AF.Exp, scale=0.5))
        t_nb = S.do("vector", lambda e: e.tensor_scalar_mul(
            out=negmb[:], in0=negmb[:], scalar1=-0.125), after=[t_ex])

        with ExitStack() as st2:
            sb2 = lambda name, shape, dt: st2.enter_context(nc.sbuf_tensor(name, shape, dt))
            kT = [sb2(f"kT{i}", [128, SEQ], BF16) for i in range(2)]
            qT = [sb2(f"qT{i}", [128, TOK // 2], BF16) for i in range(2)]
            vsb = [sb2(f"vsb{i}", [128, 64 * 130], BF16) for i in range(2)]
            E = [sb2(f"E{i}", [128, TOK // 2], BF16) for i in range(4)]
            rrow = [sb2(f"rrow{i}", [1, TOK // 2], F32) for i in range(2)]
            rb = [sb2(f"rb{i}", [64, TOK // 2], F32) for i in range(2)]
            s_ps = [st2.enter_context(nc.psum_tensor(f"sps{i}", [128, TOK // 2], F32))
                    for i in range(2)]
            av_ps = [st2.enter_context(nc.psum_tensor(f"av{i}", [65, TOK // 2], F32))
                     for i in range(2)]
            QW = TOK // 2

            buf_free = [None, None]
            av_free = [None, None]
            exp_done = [None, None]
            rrow_free = [None, None]
            rb_free = [None, None]
            di = 0
            seg_list = [(qt, hg) for qt in range(2) for hg in range(8)]
            for si, (qt, hg) in enumerate(seg_list):
                bi = si % 2
                t_kT = S.dma("sync", kT[bi][:],
                             K_d[0:SEQ, hg * 128:(hg + 1) * 128],
                             after=[buf_free[bi]], transpose=True, sig="ldk")
                t_qT = S.dma("sync", qT[bi][:],
                             q_d[qt * QW:(qt + 1) * QW, hg * 128:(hg + 1) * 128],
                             after=[buf_free[bi]], transpose=True, sig="ldk")
                t_v = S.dma("gpsimd",
                            vsb[bi][:].rearrange("p (kb f) -> p kb f", f=130),
                            V_d[hg].rearrange("(kb p) f -> p kb f", p=128),
                            after=[buf_free[bi]], sig="ldv")
                t_loads = [t_kT, t_qT, t_v, t_nb]
                for h2 in range(2):
                    h = 2 * hg + h2
                    avp = di % 2
                    av = av_ps[avp]
                    exp_tok = [None] * 64
                    av_tok = [None] * 64

                    def emit_scores(kb, h2=h2, bi=bi, h=h):
                        sp = s_ps[kb % 2]
                        after0 = [exp_done[kb % 2]]
                        if kb < 2:
                            after0 = after0 + t_loads
                        S.do("tensor", lambda e, kb=kb, sp=sp, h2=h2, bi=bi: e.matmul(
                            sp[:, 0:512],
                            kT[bi][h2 * 64:(h2 + 1) * 64, kb * 128:(kb + 1) * 128],
                            qT[bi][h2 * 64:(h2 + 1) * 64, 0:512],
                            start=True, stop=True), after=after0)
                        t_s = S.do("tensor", lambda e, kb=kb, sp=sp, h2=h2, bi=bi: e.matmul(
                            sp[:, 512:1024],
                            kT[bi][h2 * 64:(h2 + 1) * 64, kb * 128:(kb + 1) * 128],
                            qT[bi][h2 * 64:(h2 + 1) * 64, 512:1024],
                            start=True, stop=True), sig="pe")
                        t_e = S.do("scalar", lambda e, kb=kb, sp=sp, h=h: e.activation(
                            out=E[kb % 4][:], in_=sp[:], func=AF.Exp,
                            scale=0.125, bias=negmb[:, h:h + 1]),
                            after=[t_s], chain=False)
                        exp_done[kb % 2] = t_e
                        exp_tok[kb] = t_e

                    def emit_av(kb, h2=h2, bi=bi, av=av, avp=avp):
                        for hh in range(2):
                            after1 = []
                            if kb == 0 and hh == 0:
                                after1 = [exp_tok[kb], av_free[avp]]
                            elif hh == 0:
                                after1 = [exp_tok[kb]]
                            t = S.do("tensor", lambda e, kb=kb, hh=hh, av=av, h2=h2, bi=bi: e.matmul(
                                av[:, hh * 512:(hh + 1) * 512],
                                vsb[bi][:].rearrange("p (kb f) -> p kb f", f=130)
                                [:, kb, h2 * 65:(h2 + 1) * 65],
                                E[kb % 4][:, hh * 512:(hh + 1) * 512],
                                start=(kb == 0), stop=(kb == 63)),
                                after=after1,
                                sig="pe" if hh == 1 else None)
                            if hh == 1:
                                av_tok[kb] = t

                    emit_scores(0)
                    emit_scores(1)
                    for kb in range(2, 64):
                        emit_scores(kb)
                        emit_av(kb - 2)
                    emit_av(62)
                    emit_av(63)
                    t_av = S.tok("pe")
                    rp = di % 2
                    t_rc = S.do("vector", lambda e, av=av, rp=rp: e.reciprocal(
                        out=rrow[rp][:], in_=av[64:65, :]),
                        after=[t_av, rrow_free[rp]])
                    t_r1 = S.dma("sync", rscr_d[rp], rrow[rp][:], after=[t_rc],
                                 sig="ldr")
                    rrow_free[rp] = t_r1
                    t_r2 = S.dma("sync", rb[rp][:],
                                 rscr_d[rp].broadcast_to([64, QW]),
                                 after=[t_r1, rb_free[rp]], sig="ldr")
                    t_aT = S.do("vector", lambda e, av=av, rp=rp, h2=h2, hg=hg, qt=qt:
                                e.tensor_tensor(
                                    out=aT[h2 * 64:(h2 + 1) * 64,
                                           hg * TOK + qt * QW: hg * TOK + (qt + 1) * QW],
                                    in0=av[0:64, :], in1=rb[rp][:], op=ALU.mult),
                                after=[t_r2])
                    rb_free[rp] = t_aT
                    av_free[avp] = t_aT
                    di += 1
                buf_free[bi] = S.tok("pe")
            t_attn = [S.tok("dve"), S.tok("pe")]
            if L4_DEBUG:
                S.dma("sync", aTdbg_d, aT[:], after=t_attn, sig="dmaout")
                S.dma("sync", nmdbg_d, negmb[:], after=t_attn, sig="dmaout")

        with ExitStack() as st3:
            sb3 = lambda name, shape, dt: st3.enter_context(nc.sbuf_tensor(name, shape, dt))
            WoT = sb3("WoT_s", [128, DC * D], BF16)
            bor = sb3("bor", [1, D], BF16)
            ones1o = sb3("ones1o", [1, 128], BF16)
            x1t = [sb3(f"x1t{i}", [128, D], F32) for i in range(2)]
            x2t = [sb3(f"x2t{i}", [128, D], F32) for i in range(2)]
            o_ps = [st3.enter_context(nc.psum_tensor(f"ops{i}", [128, D], F32))
                    for i in range(2)]
            attn_done = t_attn + [S.tok("act")]
            t_ones = S.do("vector", lambda e: e.memset(ones1o[:], 1.0),
                          after=attn_done)
            for kk in range(DC):
                S.dma("gpsimd", WoT[:, kk * D:(kk + 1) * D],
                      WoT_d[kk * 128:(kk + 1) * 128, :], after=attn_done, sig="ldw")
            S.dma("gpsimd", bor[:], bo_d, after=attn_done, sig="ldw")
            t_wo = S.tok("ldw")
            x1_free = [None, None]
            x2_free = [None, None]
            o_free = [None, None]
            x2_out = [None] * NT
            for m in range(NT):
                t_x1 = S.dma("sync", x1t[m % 2][:], x1_d[m * 128:(m + 1) * 128, :],
                             after=[x1_free[m % 2]], sig="ldx")
                ps = o_ps[m % 2]
                for kk in range(DC):
                    for hh in range(2):
                        after = ()
                        if kk == 0 and hh == 0:
                            after = t_attn + [t_wo, o_free[m % 2]] + \
                                ([t_ones] if m == 0 else [])
                        S.do("tensor", lambda e, m=m, kk=kk, hh=hh, ps=ps: e.matmul(
                            ps[:, hh * 512:(hh + 1) * 512],
                            aT[:, kk * TOK + m * 128: kk * TOK + (m + 1) * 128],
                            WoT[:, kk * D + hh * 512: kk * D + (hh + 1) * 512],
                            start=(kk == 0), stop=False), after=after)
                t_p = None
                for hh in range(2):
                    t_p = S.do("tensor", lambda e, hh=hh, ps=ps: e.matmul(
                        ps[:, hh * 512:(hh + 1) * 512], ones1o[:],
                        bor[:, hh * 512:(hh + 1) * 512], start=False, stop=True),
                        sig="pe")
                t_x2 = S.do("vector", lambda e, m=m, ps=ps: e.tensor_add(
                    out=x2t[m % 2][:], in0=ps[:], in1=x1t[m % 2][:]),
                    after=[t_p, t_x1, x2_free[m % 2]])
                o_free[m % 2] = t_x2
                x1_free[m % 2] = t_x2
                t_o = S.dma("sync", x2_d[m * 128:(m + 1) * 128, :], x2t[m % 2][:],
                            after=[t_x2], sig="dmaout")
                x2_free[m % 2] = t_o
                x2_out[m] = t_o

        oproj_done = [S.tok("pe"), S.tok("dve")]
        sb4 = lambda name, shape, dt: st.enter_context(nc.sbuf_tensor(name, shape, dt))
        xt = [sb4(f"xtf{i}", [128, D], F32) for i in range(2)]
        phiT = sb4("phiT_s", [128, DC * SLOTS], BF16)
        phib = sb4("phib_s", [1, SLOTS], BF16)
        ones1 = sb4("ones1f", [1, 128], BF16)
        eye = sb4("eye_s", [128, 128], BF16)
        lnscr = {k2: sb4("lnf_" + k2, [128, 1], F32)[:] for k2 in ("mu", "var", "s2", "rstd")}
        lnscr["xsq"] = sb4("lnf_xsq", [128, D], F32)[:]
        lnscr["eps"] = sb4("lnf_eps", [128, 1], F32)[:]
        S.do("vector", lambda e: e.memset(lnscr["eps"], EPS))
        b = dict(uid="f4", phiT=phiT[:], phib=phib[:], ones1=ones1[:], eye=eye[:],
                 lnscr=lnscr)
        b["t_ones"] = S.do("vector", lambda e: e.memset(ones1[:], 1.0))
        t_eye = S.dma("sync", eye[:], eye_d, after=oproj_done, sig="ld")
        for kk in range(DC):
            S.dma("gpsimd", phiT[:, kk * SLOTS:(kk + 1) * SLOTS],
                  phiT_d[kk * 128:(kk + 1) * 128, :], after=oproj_done, sig="ldw")
        S.dma("gpsimd", phib[:], phib_d, after=oproj_done, sig="ldw")
        t_w = S.tok("ldw")
        xf_free = [None, None]

        def get_x_tile(m):
            t_x = S.dma("sync", xt[m % 2][:], x2_d[m * 128:(m + 1) * 128, :],
                        after=[xf_free[m % 2], x2_out[m]] +
                        (oproj_done if m < 2 else []), sig="ldxf")
            return xt[m % 2][:], [t_x, t_w, t_eye]

        def x_done(m, t_h):
            xf_free[m % 2] = t_h
        b["x_done"] = x_done
        emit_ffn_AB(S, nc, st, b, get_x_tile, C2_d, Xp2_d)
        S.wait("sync", [S.tok("dmaout")])
        S.build(nc, st)
    return nc


def _prep_common(Wq, bq, Wk, bk, Wv, bv, Wo, bo, phi_w, phi_b, expert_w, expert_b):
    c = {}
    c["phiT"] = np.ascontiguousarray(phi_w.T)
    c["phib"] = np.ascontiguousarray(phi_b[None, :])
    c["WqT"] = np.ascontiguousarray(Wq.T)
    c["WkT"] = np.ascontiguousarray(Wk.T)
    c["WvT"] = np.ascontiguousarray(Wv.T)
    c["WoT"] = np.ascontiguousarray(Wo.T)
    c["bq"] = np.ascontiguousarray(bq[None, :])
    c["bk"] = np.ascontiguousarray(bk[None, :])
    c["bv"] = np.ascontiguousarray(bv[None, :])
    c["bo"] = np.ascontiguousarray(bo[None, :])
    c["ew"] = np.ascontiguousarray(expert_w)
    c["eb"] = np.ascontiguousarray(expert_b)
    c["eye"] = np.eye(128, dtype=ml_dtypes.bfloat16)
    inv = 10000.0 ** (-np.arange(0, DH, 2, dtype=np.float64) / DH)
    cos_l, sin_l = [], []
    for core in range(NCORE):
        pos = (core % 4) * TOK + np.arange(TOK, dtype=np.float64)
        fr = np.outer(pos, inv)
        cos_l.append(np.cos(fr).astype(np.float32))
        s = np.sin(fr).astype(np.float32)
        sin_l.append(np.ascontiguousarray(np.concatenate([-s, s], axis=1)))
    c["cos"] = cos_l
    c["sin2"] = sin_l
    return c


def run_expert_launch(Xp_list, c):
    nc2 = build_L2()
    in2 = []
    for cc in range(NCORE):
        xp8 = np.stack([Xp_list[src][cc * EL * PSL:(cc + 1) * EL * PSL]
                        for src in range(NCORE)])
        in2.append({"Xp8": np.ascontiguousarray(xp8),
                    "ew": np.ascontiguousarray(c["ew"][cc * EL:(cc + 1) * EL]),
                    "eb": np.ascontiguousarray(c["eb"][cc * EL:(cc + 1) * EL]),
                    "eye": c["eye"]})
    r2 = run_bass_kernel_spmd(nc2, in2, core_ids=list(range(NCORE)))
    return np.ascontiguousarray(
        np.concatenate([r2.results[cc]["Ytl"] for cc in range(NCORE)], axis=0))


def kernel(L0_attn_x, L0_attn_Wq, L0_attn_bq, L0_attn_Wk, L0_attn_bk, L0_attn_Wv,
           L0_attn_bv, L0_attn_Wo, L0_attn_bo, L0_ffn_phi_w, L0_ffn_phi_b,
           L0_ffn_expert_w, L0_ffn_expert_b):
    c = _prep_common(L0_attn_Wq, L0_attn_bq, L0_attn_Wk, L0_attn_bk, L0_attn_Wv,
                     L0_attn_bv, L0_attn_Wo, L0_attn_bo, L0_ffn_phi_w,
                     L0_ffn_phi_b, L0_ffn_expert_w, L0_ffn_expert_b)
    x0 = np.ascontiguousarray(np.asarray(L0_attn_x, dtype=np.float32)
                              .reshape(NCORE * TOK, D))
    eyef = np.eye(128, dtype=np.float32)
    cores = list(range(NCORE))

    nc1 = build_L1()
    in1 = [{"x": np.ascontiguousarray(x0[cc * TOK:(cc + 1) * TOK]),
            "phiT": c["phiT"], "phib": c["phib"], "eye": c["eye"]}
           for cc in cores]
    r1 = run_bass_kernel_spmd(nc1, in1, core_ids=cores)
    Xp1 = [r1.results[cc]["Xp"] for cc in cores]
    C1 = [r1.results[cc]["C"] for cc in cores]

    Ytl1 = run_expert_launch(Xp1, c)

    nc3 = build_L3()
    in3 = [{"x": in1[cc]["x"], "C": C1[cc], "Ytl": Ytl1,
            "WqT": c["WqT"], "WkT": c["WkT"], "WvT": c["WvT"],
            "bq": c["bq"], "bk": c["bk"], "bv": c["bv"],
            "cos": c["cos"][cc], "sin2": c["sin2"][cc],
            "eye": c["eye"], "eyef": eyef} for cc in cores]
    r3 = run_bass_kernel_spmd(nc3, in3, core_ids=cores)

    nc4 = build_L4()
    in4 = []
    for cc in cores:
        grp = range(4 * (cc // 4), 4 * (cc // 4) + 4)
        Kf = np.ascontiguousarray(
            np.concatenate([r3.results[g]["k"] for g in grp], axis=0))
        Vf = np.ascontiguousarray(
            np.concatenate([r3.results[g]["v"] for g in grp], axis=1))
        kn4 = np.ascontiguousarray(
            np.concatenate([r3.results[g]["norms"][:, 1] for g in grp])[None, :])
        qn = np.ascontiguousarray(r3.results[cc]["norms"][:, 0][None, :])
        in4.append({"x1": r3.results[cc]["x1"], "q": r3.results[cc]["q"],
                    "K": Kf, "V": Vf, "qn": qn, "kn4": kn4,
                    "WoT": c["WoT"], "bo": c["bo"],
                    "phiT": c["phiT"], "phib": c["phib"], "eye": c["eye"]})
    r4 = run_bass_kernel_spmd(nc4, in4, core_ids=cores)
    Xp2 = [r4.results[cc]["Xp2"] for cc in cores]
    C2 = [r4.results[cc]["C2"] for cc in cores]

    Ytl2 = run_expert_launch(Xp2, c)

    nc6 = build_L6()
    in6 = [{"x": r4.results[cc]["x2"], "C": C2[cc], "Ytl": Ytl2} for cc in cores]
    r6 = run_bass_kernel_spmd(nc6, in6, core_ids=cores)
    out = np.concatenate([r6.results[cc]["x3"] for cc in cores], axis=0)
    return out.reshape(2, SEQ, D)


# version 05
# speedup vs baseline: 1.5194x; 1.0361x over previous
import numpy as np
import ml_dtypes
from contextlib import ExitStack
import concourse.bass as bass
import concourse.mybir as mybir
from concourse.bass_utils import run_bass_kernel_spmd

F32 = mybir.dt.float32
BF16 = mybir.dt.bfloat16
AF = mybir.ActivationFunctionType
ALU = mybir.AluOpType
AX = mybir.AxisListType

NCORE = 8
TOK = 2048
NT = TOK // 128
D = 1024
DC = D // 128
SLOTS = 2048
NEXP = 64
PSL = 32
EL = NEXP // NCORE
NHEAD = 16
DH = 64
L4_DEBUG = False
SEQ = 8192
EPS = 1e-5


class Sched:

    def __init__(self):
        self.ops = []
        self.counts = {}

    AUTO_SIG = {"vector": "dve", "scalar": "act", "gpsimd": "pool"}

    def tok(self, sem):
        return (sem, self.counts.get(sem, 0))

    def do(self, eng, fn, after=(), sig=None, inc=1, chain=True):
        auto = self.AUTO_SIG.get(eng)
        if sig is None and auto is not None:
            sig = auto
        after = [t for t in after if t is not None]
        if chain and auto is not None and self.counts.get(auto, 0) > 0:
            after.append((auto, self.counts[auto]))
        after = tuple(after)
        for s, v in after:
            assert v <= self.counts.get(s, 0), (s, v, self.counts.get(s, 0))
        self.ops.append((eng, after, fn, sig or None, inc if sig else 0))
        if sig:
            self.counts[sig] = self.counts.get(sig, 0) + inc
            return (sig, self.counts[sig])
        return None

    def dma(self, eng, out, in_, after=(), sig="dma", **kw):
        return self.do(eng, lambda e: e.dma_start(out=out, in_=in_, **kw),
                       after=after, sig=sig, inc=16, chain=False)

    def wait(self, eng, after):
        self.do(eng, lambda e: None, after=after, sig=False, chain=False)

    def build(self, nc, stack):
        sems = {n: stack.enter_context(nc.semaphore("s_" + n)) for n in self.counts}
        per_eng = {}
        for eng, after, fn, sig, inc in self.ops:
            per_eng.setdefault(eng, []).append((after, fn, sig, inc))
        block = stack.enter_context(nc.Block())

        def mk(items):
            def body(e):
                waited = {}
                for after, fn, sig, inc in items:
                    for s, v in after:
                        if waited.get(s, 0) < v:
                            e.wait_ge(sems[s], v)
                            waited[s] = v
                    ins = fn(e)
                    if sig and ins is not None:
                        ins.then_inc(sems[sig], inc)
            return body
        for eng_name, items in per_eng.items():
            getattr(block, eng_name)(mk(items))


def emit_ln(S, x_ap, h_out_ap, scr, guard, out_dve_sig="dve"):
    mu, var, s2, rstd, xsq = scr["mu"], scr["var"], scr["s2"], scr["rstd"], scr["xsq"]
    S.do("vector", lambda e: e.reduce_sum(out=mu, in_=x_ap, axis=AX.X), after=guard)
    S.do("vector", lambda e: e.tensor_scalar_mul(out=mu, in0=mu, scalar1=1.0 / x_ap.shape[-1]))
    S.do("vector", lambda e: e.tensor_mul(out=xsq, in0=x_ap, in1=x_ap))
    S.do("vector", lambda e: e.reduce_sum(out=s2, in_=xsq, axis=AX.X))
    S.do("vector", lambda e: e.tensor_mul(out=var, in0=mu, in1=mu))
    t_v = S.do("vector", lambda e: e.scalar_tensor_tensor(
        out=var, in0=s2, scalar=1.0 / x_ap.shape[-1], in1=var,
        op0=ALU.mult, op1=ALU.subtract), sig="dve")
    S.do("scalar", lambda e: e.activation(out=s2, in_=var, func=AF.Ln, bias=scr["eps"]),
         after=[t_v])
    t_act = S.do("scalar", lambda e: e.activation(out=rstd, in_=s2, func=AF.Exp,
                                                  scale=-0.5), sig="act")
    t_h = S.do("vector", lambda e: e.tensor_scalar(
        out=h_out_ap, in0=x_ap, scalar1=mu, scalar2=rstd,
        op0=ALU.subtract, op1=ALU.mult), after=[t_act], sig=out_dve_sig)
    return t_h, t_act


def emit_ffn_H1(S, m, b, x_tile_ap, after_x):
    hm = b["h_all"][:, m * D:(m + 1) * D]
    t_h, t_ln_act = emit_ln(S, x_tile_ap, hm, b["lnscr"],
                            list(after_x) + [b["ln_free"]])
    b["ln_free"] = t_ln_act
    t_hd = S.dma("sync", b["h_dram"][m * 128:(m + 1) * 128, :], hm, after=[t_h],
                 sig="hd" + b["uid"])
    hT = b["hT"][m % 2]
    tr_toks = []
    for k in range(DC):
        t_tr = S.dma("scalar", hT[:, k * 128:(k + 1) * 128],
                     b["h_dram"][m * 128:(m + 1) * 128, k * 128:(k + 1) * 128],
                     after=[t_hd, b["hT_free"][m % 2]], transpose=True,
                     sig="ht" + b["uid"])
        tr_toks.append(t_tr)
    lp = b["logits_ps"]
    t_lg = [None, None]
    for half in range(2):
        L = lp[2 * (m % 2) + half]
        for k in range(DC):
            after = []
            if k == 0:
                after = [tr_toks[-1], b["logits_free"][2 * (m % 2) + half]] + \
                    ([b["t_ones"]] if m < 2 else [])
            for j in range(2):
                S.do("tensor", lambda e, k=k, j=j, half=half, L=L: e.matmul(
                    L[:, j * 512:(j + 1) * 512],
                    b["hT"][m % 2][:, k * 128:(k + 1) * 128],
                    b["phiT"][:, k * SLOTS + half * 1024 + j * 512:
                              k * SLOTS + half * 1024 + (j + 1) * 512],
                    start=(k == 0), stop=False), after=after if j == 0 else ())
        for j in range(2):
            t_lg[half] = S.do("tensor", lambda e, j=j, half=half, L=L: e.matmul(
                L[:, j * 512:(j + 1) * 512], b["ones1"],
                b["phib"][:, half * 1024 + j * 512: half * 1024 + (j + 1) * 512],
                start=False, stop=True), sig="pe")
    b["hT_free"][m % 2] = t_lg[1]
    b["lg_tok"][m % 2] = t_lg
    return t_h


def emit_ffn_H2(S, m, b):
    p = m % 2
    lp = b["logits_ps"]
    t_lg = b["lg_tok"][p]
    negm, negm2 = b["negm"][p], b["negm2"][p]
    t_n0 = S.do("vector", lambda e: e.reduce_max(
        out=negm, in_=lp[2 * p][:], axis=AX.X, negate=True),
        after=[t_lg[0], b["negm_free"][p]])
    t_n1 = S.do("vector", lambda e: e.reduce_max(
        out=negm2, in_=lp[2 * p + 1][:], axis=AX.X, negate=True), after=[t_lg[1]])
    t_nm = S.do("vector", lambda e: e.tensor_tensor(
        out=negm, in0=negm, in1=negm2, op=ALU.min))
    E = b["E_t"][p]
    t_EA = S.do("scalar", lambda e: e.activation(
        out=E[:, 0:1024], in_=lp[2 * p][:], func=AF.Exp, bias=negm),
        after=[t_nm] + b["E_free"][p], chain=False)
    t_EB = S.do("scalar", lambda e: e.activation(
        out=E[:, 1024:2048], in_=lp[2 * p + 1][:], func=AF.Exp, bias=negm),
        after=[], chain=True)
    b["logits_free"][2 * p] = t_EA
    b["logits_free"][2 * p + 1] = t_EB
    b["negm_free"][p] = t_EB
    sp, rp, sn, rn = b["sp"][p], b["rp"][p], b["sn"][p], b["rn"][p]
    Ev = E[:].rearrange("p (n q) -> p n q", n=NEXP)
    Evt = E[:].rearrange("p (n q) -> p q n", n=NEXP)
    aft = [t_EB] + b["srr_free"][p]
    S.do("vector", lambda e: e.reduce_sum(out=sp, in_=Ev, axis=AX.X), after=aft)
    S.do("vector", lambda e: e.reciprocal(out=rp, in_=sp))
    S.do("vector", lambda e: e.reduce_sum(out=sn, in_=Evt, axis=AX.X))
    t_rn = S.do("vector", lambda e: e.reciprocal(out=rn, in_=sn))
    Dv = b["D_all"][:, m * SLOTS:(m + 1) * SLOTS].rearrange("p (n q) -> p n q", n=NEXP)
    t_D = S.do("gpsimd", lambda e: e.tensor_tensor(
        out=Dv, in0=Ev, in1=rp.unsqueeze(2).broadcast_to([128, NEXP, PSL]),
        op=ALU.mult), after=[t_rn], sig="pool")
    C = b["C_t"][p]
    Cv = C[:].rearrange("p (n q) -> p n q", n=NEXP)
    t_C = S.do("gpsimd", lambda e: e.tensor_tensor(
        out=Cv, in0=Ev, in1=rn.unsqueeze(1).broadcast_to([128, NEXP, PSL]),
        op=ALU.mult), after=[b["C_free"][p]], sig="pool")
    t_Cout = S.dma("sync", b["C_out"][m * 128:(m + 1) * 128, :], C[:],
                   after=[t_C], sig="dmaout")
    b["C_free"][p] = t_Cout
    b["E_free"][p] = [t_D, t_C]
    b["srr_free"][p] = [t_D, t_C]
    return t_D


def emit_ffn_AB(S, nc, st, b, get_x_tile, C_out, Xp_out):
    sb = lambda name, shape, dt: st.enter_context(nc.sbuf_tensor(name, shape, dt))
    uid = b["uid"]
    b["h_all"] = sb(f"h_all{uid}", [128, NT * D], BF16)[:]
    b["hT"] = [sb(f"hT{uid}{i}", [128, D], BF16) for i in range(2)]
    b["h_dram"] = nc.dram_tensor(f"h_dram{uid}", [TOK, D], BF16, kind="Internal").ap()
    b["D_all"] = sb(f"D_all{uid}", [128, NT * SLOTS], BF16)[:]
    b["E_t"] = [sb(f"E{uid}_{i}", [128, SLOTS], BF16) for i in range(2)]
    b["C_t"] = [sb(f"Ct{uid}_{i}", [128, SLOTS], BF16) for i in range(2)]
    b["negm"] = [sb(f"negm{uid}{i}", [128, 1], F32)[:] for i in range(2)]
    b["negm2"] = [sb(f"negm2{uid}{i}", [128, 1], F32)[:] for i in range(2)]
    b["sp"] = [sb(f"sp{uid}{i}", [128, NEXP], F32)[:] for i in range(2)]
    b["rp"] = [sb(f"rp{uid}{i}", [128, NEXP], F32)[:] for i in range(2)]
    b["sn"] = [sb(f"sn{uid}{i}", [128, PSL], F32)[:] for i in range(2)]
    b["rn"] = [sb(f"rn{uid}{i}", [128, PSL], F32)[:] for i in range(2)]
    b["C_out"] = C_out
    b["C_free"] = [None, None]
    b["E_free"] = [[], []]
    b["srr_free"] = [[], []]
    b["negm_free"] = [None, None]
    b["logits_free"] = [None] * 4
    b["ln_free"] = None
    b["hT_free"] = [None, None]
    b["lg_tok"] = [None, None]

    ps_stack = ExitStack()
    b["logits_ps"] = [ps_stack.enter_context(
        nc.psum_tensor(f"lg{uid}{i}", [128, 1024], F32)) for i in range(4)]

    xs = {}
    for m in range(2):
        x_ap, after_x = get_x_tile(m)
        t_h = emit_ffn_H1(S, m, b, x_ap, after_x)
        if "x_done" in b:
            b["x_done"](m, t_h)
    for m in range(NT):
        emit_ffn_H2(S, m, b)
        if m + 2 < NT:
            x_ap, after_x = get_x_tile(m + 2)
            t_h = emit_ffn_H1(S, m + 2, b, x_ap, after_x)
            if "x_done" in b:
                b["x_done"](m + 2, t_h)

    t_lastA = [S.tok("dve"), S.tok("pool")]
    ps_stack.close()

    Xps = [st.enter_context(nc.psum_tensor(f"Xps{uid}_{j}", [128, D], F32))
           for j in range(4)]
    Xev = [sb(f"Xev{uid}_{j}", [128, D], F32) for j in range(2)]
    bank_free = [None] * 4
    xev_dma = [None, None]
    for s in range(4):
        for m in range(NT):
            for j in range(4):
                slot0 = s * 512 + j * 128
                after = []
                if m == 0:
                    after = (t_lastA if s == 0 else []) + [bank_free[j]]
                for half in range(2):
                    S.do("tensor", lambda e, m=m, j=j, half=half, slot0=slot0: e.matmul(
                        Xps[j][:, half * 512:(half + 1) * 512],
                        b["D_all"][:, m * SLOTS + slot0: m * SLOTS + slot0 + 128],
                        b["h_all"][:, m * D + half * 512: m * D + (half + 1) * 512],
                        start=(m == 0), stop=(m == NT - 1)),
                        after=after if half == 0 else (),
                        sig="pe" if (m == NT - 1 and half == 1) else None)
        t_s = S.tok("pe")
        for j in range(4):
            t_ev = S.do("vector", lambda e, j=j: e.tensor_copy(
                out=Xev[j % 2][:], in_=Xps[j][:]), after=[t_s, xev_dma[j % 2]],
                sig="dve")
            bank_free[j] = t_ev
            xev_dma[j % 2] = S.dma(
                "sync", Xp_out[s * 512 + j * 128: s * 512 + (j + 1) * 128, :],
                Xev[j % 2][:], after=[t_ev], sig="dmaout")
    return S.tok("dmaout")


def build_L1():
    nc = bass.Bass("TRN2", target_bir_lowering=False)
    x_d = nc.dram_tensor("x", [TOK, D], F32, kind="ExternalInput").ap()
    phiT_d = nc.dram_tensor("phiT", [D, SLOTS], F32, kind="ExternalInput").ap()
    phib_d = nc.dram_tensor("phib", [1, SLOTS], F32, kind="ExternalInput").ap()
    eye_d = nc.dram_tensor("eye", [128, 128], BF16, kind="ExternalInput").ap()
    Xp_d = nc.dram_tensor("Xp", [SLOTS, D], F32, kind="ExternalOutput").ap()
    C_d = nc.dram_tensor("C", [TOK, SLOTS], BF16, kind="ExternalOutput").ap()

    S = Sched()
    with ExitStack() as st:
        sb = lambda name, shape, dt: st.enter_context(nc.sbuf_tensor(name, shape, dt))
        xt = [sb(f"xt{i}", [128, D], F32) for i in range(2)]
        phiT = sb("phiT_s", [128, DC * SLOTS], BF16)
        phib = sb("phib_s", [1, SLOTS], BF16)
        ones1 = sb("ones1", [1, 128], BF16)
        eye = sb("eye_s", [128, 128], BF16)
        lnscr = {k: sb("ln_" + k, [128, 1], F32)[:] for k in ("mu", "var", "s2", "rstd")}
        lnscr["xsq"] = sb("ln_xsq", [128, D], F32)[:]
        lnscr["eps"] = sb("ln_eps", [128, 1], F32)[:]
        S.do("vector", lambda e: e.memset(lnscr["eps"], EPS))

        b = dict(uid="a", phiT=phiT[:], phib=phib[:], ones1=ones1[:], eye=eye[:],
                 lnscr=lnscr)
        b["t_ones"] = S.do("vector", lambda e: e.memset(ones1[:], 1.0), sig="dve")
        t_eye = S.dma("sync", eye[:], eye_d, sig="ld")
        for k in range(DC):
            S.dma("gpsimd", phiT[:, k * SLOTS:(k + 1) * SLOTS],
                  phiT_d[k * 128:(k + 1) * 128, :], sig="ldw")
        S.dma("gpsimd", phib[:], phib_d, sig="ldw")
        t_w = S.tok("ldw")

        x_free = [None, None]

        def get_x_tile(m):
            t_x = S.dma("sync", xt[m % 2][:], x_d[m * 128:(m + 1) * 128, :],
                        after=[x_free[m % 2]], sig="ldx")
            return xt[m % 2][:], [t_x, t_w, t_eye]

        def x_done(m, t_h):
            x_free[m % 2] = t_h
        b["x_done"] = x_done

        t_out = emit_ffn_AB(S, nc, st, b, get_x_tile, C_d, Xp_d)
        S.wait("sync", [t_out, S.tok("dmaout")])
        S.build(nc, st)
    return nc


def build_L2():
    nc = bass.Bass("TRN2", target_bir_lowering=False)
    Xp8_d = nc.dram_tensor("Xp8", [NCORE, EL * PSL, D], F32, kind="ExternalInput").ap()
    ew_d = nc.dram_tensor("ew", [EL, D, D], F32, kind="ExternalInput").ap()
    eb_d = nc.dram_tensor("eb", [EL, D], F32, kind="ExternalInput").ap()
    eye_d = nc.dram_tensor("eye", [128, 128], BF16, kind="ExternalInput").ap()
    Y_d = nc.dram_tensor("Ytl", [EL * PSL, D], F32, kind="ExternalOutput").ap()

    S = Sched()
    with ExitStack() as st:
        sb = lambda name, shape, dt: st.enter_context(nc.sbuf_tensor(name, shape, dt))
        xp = sb("xp_s", [128, NCORE * 2 * D], F32)
        xs = sb("xs", [128, 2 * D], F32)
        xn = sb("xn", [128, 2 * D], F32)
        xb = sb("xb", [128, 2 * D], BF16)
        XT = sb("XT", [128, 2 * DC * 128], BF16)
        ew = [sb(f"ew{i}", [128, DC * D], BF16) for i in range(2)]
        ebr_s = sb("ebr_s", [1, EL * D], BF16)
        ebr = ebr_s
        ones1 = sb("ones1", [1, 128], BF16)
        eye = sb("eye_s", [128, 128], BF16)
        Yev = sb("Yev", [32, EL * D], F32)
        lnscr = {k: sb("ln_" + k, [128, 1], F32)[:] for k in ("mu", "var", "s2", "rstd")}
        lnscr["xsq"] = sb("ln_xsq", [128, D], F32)[:]
        lnscr["eps"] = sb("ln_eps", [128, 1], F32)[:]
        S.do("vector", lambda e: e.memset(lnscr["eps"], EPS))

        tr_ps = [st.enter_context(nc.psum_tensor(f"tr{i}", [128, 1024], BF16))
                 for i in range(2)]
        Y_ps = [st.enter_context(nc.psum_tensor(f"Y{i}", [32, D], F32))
                for i in range(2)]

        t_ones = S.do("vector", lambda e: e.memset(ones1[:], 1.0), sig="dve")
        t_eye = S.dma("sync", eye[:], eye_d, sig="ld")
        for src in range(NCORE):
            for half in range(2):
                S.dma("sync", xp[:, (src * 2 + half) * D:(src * 2 + half + 1) * D],
                      Xp8_d[src, half * 128:(half + 1) * 128, :], sig="ldx")
        t_x = S.tok("ldx")
        for n in range(EL):
            S.dma("gpsimd", ebr[:, n * D:(n + 1) * D], eb_d[n:n + 1, :], sig="ldb")
        t_eb = S.tok("ldb")
        ew_ready = [None] * EL
        for n in range(2):
            for k in range(DC):
                S.dma("gpsimd", ew[n][:, k * D:(k + 1) * D],
                      ew_d[n, k * 128:(k + 1) * 128, :], sig="ldw")
            ew_ready[n] = S.tok("ldw")

        ln_free = None
        t_tanh = [None, None]
        ev_done = []
        for half in range(2):
            hs = slice(half * D, (half + 1) * D)
            S.do("vector", lambda e, hs=hs: e.tensor_add(
                out=xs[:, hs], in0=xp[:, hs],
                in1=xp[:, 2 * D + hs.start: 2 * D + hs.stop]), after=[t_x])
            for src in range(2, NCORE):
                S.do("vector", lambda e, hs=hs, src=src: e.tensor_add(
                    out=xs[:, hs], in0=xs[:, hs],
                    in1=xp[:, src * 2 * D + hs.start: src * 2 * D + hs.stop]))
            S.do("vector", lambda e, hs=hs: e.tensor_scalar(
                out=xs[:, hs], in0=xs[:, hs], scalar1=-33000.0, scalar2=65000.0,
                op0=ALU.max, op1=ALU.min))
            t_h, t_act = emit_ln(S, xs[:, hs], xn[:, hs], lnscr,
                                 [ln_free])
            t_t = S.do("scalar", lambda e, hs=hs: e.activation(
                out=xb[:, hs], in_=xn[:, hs], func=AF.Tanh), after=[t_h], sig="act")
            ln_free = t_t
            t_tanh[half] = t_t
            for k in range(DC):
                ps = tr_ps[k % 2]
                after = [t_t, t_eye]
                if len(ev_done) >= 2:
                    after.append(ev_done[-2])
                t_tr = S.do("tensor", lambda e, hs=hs, k=k, ps=ps: e.transpose(
                    ps[:, 0:128], xb[:, hs.start + k * 128: hs.start + (k + 1) * 128],
                    eye[:]), after=after, sig="pe")
                ev = S.do("vector", lambda e, half=half, k=k, ps=ps: e.tensor_copy(
                    out=XT[:, (half * DC + k) * 128:(half * DC + k + 1) * 128],
                    in_=ps[:, 0:128]), after=[t_tr], sig="dve")
                ev_done.append(ev)
        t_XT = S.tok("dve")

        y_free = [None, None]
        for n in range(EL):
            half, j = divmod(n, 4)
            ps = Y_ps[n % 2]
            base = (half * DC) * 128
            for k in range(DC):
                lhs = XT[:, (half * DC + k) * 128 + j * 32:
                         (half * DC + k) * 128 + (j + 1) * 32]
                for hh in range(2):
                    after = ()
                    if k == 0 and hh == 0:
                        after = [t_XT, ew_ready[n], y_free[n % 2], t_ones, t_eb]
                    S.do("tensor", lambda e, k=k, hh=hh, lhs=lhs, ps=ps, n=n: e.matmul(
                        ps[:, hh * 512:(hh + 1) * 512], lhs,
                        ew[n % 2][:, k * D + hh * 512: k * D + (hh + 1) * 512],
                        start=(k == 0), stop=False), after=after)
            t_pe = None
            for hh in range(2):
                t_pe = S.do("tensor", lambda e, n=n, hh=hh, ps=ps: e.matmul(
                    ps[:, hh * 512:(hh + 1) * 512], ones1[:, 0:32],
                    ebr[:, n * D + hh * 512: n * D + (hh + 1) * 512],
                    start=False, stop=True), sig="pe")
            t_ev = S.do("vector", lambda e, n=n, ps=ps: e.tensor_copy(
                out=Yev[:, n * D:(n + 1) * D], in_=ps[:]), after=[t_pe], sig="dve")
            y_free[n % 2] = t_ev
            if n + 2 < EL:
                for k in range(DC):
                    S.dma("gpsimd", ew[n % 2][:, k * D:(k + 1) * D],
                          ew_d[n + 2, k * 128:(k + 1) * 128, :],
                          after=[t_ev] if k == 0 else (), sig="ldw")
                ew_ready[n + 2] = S.tok("ldw")
            S.dma("sync", Y_d[n * PSL:(n + 1) * PSL, :],
                  Yev[0:PSL, n * D:(n + 1) * D], after=[t_ev], sig="dmaout")
        S.wait("sync", [S.tok("dmaout")])
        S.build(nc, st)
    return nc



def emit_ffn_D(S, nc, st, uid, C_d, Ytl_d, x_d, xout_d, tile_cb=None, psum_pp=2):
    sb = lambda name, shape, dt: st.enter_context(nc.sbuf_tensor(name, shape, dt))
    CT = sb(f"CT{uid}", [128, 16 * SLOTS], BF16)
    Ytl = sb(f"Ytl{uid}", [128, 16 * D], BF16)
    xt = [sb(f"xt{uid}{i}", [128, D], F32) for i in range(2)]
    x1t = [sb(f"x1t{uid}{i}", [128, D], F32) for i in range(2)]
    comb = [st.enter_context(nc.psum_tensor(f"comb{uid}{i}", [128, D], F32))
            for i in range(psum_pp)]
    for sc in range(16):
        S.dma("sync", CT[:, sc * SLOTS:(sc + 1) * SLOTS],
              C_d[0:TOK, sc * 128:(sc + 1) * 128], transpose=True, sig="ldct")
        S.dma("gpsimd", Ytl[:, sc * D:(sc + 1) * D],
              Ytl_d[sc * 128:(sc + 1) * 128, :], sig="ldyt")
    t_ct = S.tok("ldct")
    t_yt = S.tok("ldyt")
    x_free = [None, None]
    x1_free = [None, None]
    comb_free = [None] * psum_pp
    for m in range(NT):
        t_x = S.dma("sync", xt[m % 2][:], x_d[m * 128:(m + 1) * 128, :],
                    after=[x_free[m % 2]], sig="ldx" + uid)
        ps = comb[m % psum_pp]
        for sc in range(16):
            for hh in range(2):
                after = ()
                if sc == 0 and hh == 0:
                    after = [t_ct, t_yt, comb_free[m % psum_pp]]
                S.do("tensor", lambda e, m=m, sc=sc, hh=hh, ps=ps: e.matmul(
                    ps[:, hh * 512:(hh + 1) * 512],
                    CT[:, sc * SLOTS + m * 128: sc * SLOTS + (m + 1) * 128],
                    Ytl[:, sc * D + hh * 512: sc * D + (hh + 1) * 512],
                    start=(sc == 0), stop=(sc == 15)),
                    after=after, sig="pe" if (sc == 15 and hh == 1) else None)
        t_mm = S.tok("pe")
        t_x1 = S.do("vector", lambda e, m=m, ps=ps: e.scalar_tensor_tensor(
            out=x1t[m % 2][:], in0=ps[:], scalar=0.5, in1=xt[m % 2][:],
            op0=ALU.mult, op1=ALU.add),
            after=[t_mm, t_x, x_free[m % 2], x1_free[m % 2]])
        comb_free[m % psum_pp] = t_x1
        x_free[m % 2] = t_x1
        t_o = S.dma("sync", xout_d[m * 128:(m + 1) * 128, :], x1t[m % 2][:],
                    after=[t_x1], sig="dmaout")
        x1_free[m % 2] = t_o
        if tile_cb is not None:
            tile_cb(m, x1t[m % 2][:], t_x1)
    return S.tok("dmaout")


def build_L3():
    nc = bass.Bass("TRN2", target_bir_lowering=False)
    x_d = nc.dram_tensor("x", [TOK, D], F32, kind="ExternalInput").ap()
    C_d = nc.dram_tensor("C", [TOK, SLOTS], BF16, kind="ExternalInput").ap()
    Ytl_d = nc.dram_tensor("Ytl", [SLOTS, D], F32, kind="ExternalInput").ap()
    W_d = {w: nc.dram_tensor(w, [D, D], F32, kind="ExternalInput").ap()
           for w in ("WqT", "WkT", "WvT")}
    b_d = {w: nc.dram_tensor(w, [1, D], F32, kind="ExternalInput").ap()
           for w in ("bq", "bk", "bv")}
    cos_d = nc.dram_tensor("cos", [TOK, 32], F32, kind="ExternalInput").ap()
    sin_d = nc.dram_tensor("sin2", [TOK, 64], F32, kind="ExternalInput").ap()
    eye_d = nc.dram_tensor("eye", [128, 128], BF16, kind="ExternalInput").ap()
    eyef_d = nc.dram_tensor("eyef", [128, 128], F32, kind="ExternalInput").ap()
    x1_d = nc.dram_tensor("x1", [TOK, D], F32, kind="ExternalOutput").ap()
    q_d = nc.dram_tensor("q", [TOK, D], BF16, kind="ExternalOutput").ap()
    k_d = nc.dram_tensor("k", [TOK, D], BF16, kind="ExternalOutput").ap()
    v_d = nc.dram_tensor("v", [8, TOK, 130], BF16, kind="ExternalOutput").ap()
    nrm_d = nc.dram_tensor("norms", [NHEAD, 2], F32, kind="ExternalOutput").ap()
    bvscr_d = nc.dram_tensor("bvscr", [1, D], F32, kind="Internal").ap()

    S = Sched()
    with ExitStack() as st:
        sb = lambda name, shape, dt: st.enter_context(nc.sbuf_tensor(name, shape, dt))
        W = {w: sb(w + "_s", [128, DC * D], BF16) for w in ("WqT", "WkT", "WvT")}
        br = {w: sb(w + "_s", [1, D], BF16) for w in ("bq", "bk", "bv")}
        bvb = sb("bvb", [128, D], F32)
        ones1 = sb("ones1", [1, 128], BF16)
        eye = sb("eye_s", [128, 128], BF16)
        eyef = sb("eyef_s", [128, 128], F32)
        cos_sb = sb("cos_sb", [128, NT * 32], F32)
        sin_sb = sb("sin_sb", [128, NT * 64], F32)
        g_t = sb("g_t", [128, D], BF16)
        gT = sb("gT", [128, D], BF16)
        t1f = sb("t1f", [128, D], F32)
        t2f = sb("t2f", [128, D], F32)
        qk_bf = [sb(f"qkbf{i}", [128, D], BF16) for i in range(2)]
        v_t = [sb(f"vt{i}", [128, 16 * 65], BF16) for i in range(2)]
        sqs = sb("sqs", [128, D], F32)
        red16 = sb("red16", [128, NHEAD], F32)
        qn_run = sb("qn_run", [128, NHEAD], F32)
        kn_run = sb("kn_run", [128, NHEAD], F32)
        nrm_sb = sb("nrm_sb", [16, 2], F32)
        lnscr = {k2: sb("ln_" + k2, [128, 1], F32)[:] for k2 in ("mu", "var", "s2", "rstd")}
        lnscr["xsq"] = sb("ln_xsq", [128, D], F32)[:]
        lnscr["eps"] = sb("ln_eps", [128, 1], F32)[:]
        S.do("vector", lambda e: e.memset(lnscr["eps"], EPS))
        t_ones = S.do("vector", lambda e: e.memset(ones1[:], 1.0))
        S.do("vector", lambda e: e.memset(qn_run[:], 0.0))
        t_nrm0 = S.do("vector", lambda e: e.memset(kn_run[:], 0.0))

        t_eye = S.dma("sync", eye[:], eye_d, sig="ld")
        S.dma("sync", eyef[:], eyef_d, sig="ld")
        for m in range(NT):
            S.dma("sync", cos_sb[:, m * 32:(m + 1) * 32],
                  cos_d[m * 128:(m + 1) * 128, :], sig="ld")
            S.dma("sync", sin_sb[:, m * 64:(m + 1) * 64],
                  sin_d[m * 128:(m + 1) * 128, :], sig="ld")
        BIAS = {"WqT": "bq", "WkT": "bk", "WvT": "bv"}
        for w in ("WqT", "WkT", "WvT"):
            for kk in range(DC):
                S.dma("gpsimd", W[w][:, kk * D:(kk + 1) * D],
                      W_d[w][kk * 128:(kk + 1) * 128, :], sig="ldw")
            S.dma("gpsimd", br[BIAS[w]][:], b_d[BIAS[w]], sig="ldw")
        t_bv = S.dma("sync", bvscr_d, b_d["bv"], sig="ld")
        S.dma("sync", bvb[:], bvscr_d.broadcast_to([128, D]), after=[t_bv], sig="ld")
        t_ld = S.tok("ld")
        t_w = S.tok("ldw")

        tr_ps = [st.enter_context(nc.psum_tensor(f"trq{i}", [128, 1024], BF16))
                 for i in range(2)]
        qkv_ps = [st.enter_context(nc.psum_tensor(f"qkv{i}", [128, D], F32))
                  for i in range(2)]

        state = dict(trps_free=[None, None], qkv_free=[None, None],
                     qk_free=[None, None], vt_free=[None, None], ln_free=None,
                     gT_free=None)

        def qkv_tile(m, x1_ap, t_x1):
            t_g, t_act = emit_ln(S, x1_ap, g_t[:], lnscr,
                                 [t_x1, state["ln_free"]])
            state["ln_free"] = t_act
            ev_toks = []
            for kk in range(DC):
                ps = tr_ps[kk % 2]
                after = [t_g, t_eye] +                     ([ev_toks[kk - 2]] if kk >= 2 else [state["trps_free"][kk % 2]])
                if kk == 0:
                    after.append(state["gT_free"])
                t_tr = S.do("tensor", lambda e, kk=kk, ps=ps: e.transpose(
                    ps[:, 0:128], g_t[:, kk * 128:(kk + 1) * 128], eye[:]),
                    after=after, sig="pe")
                ev = S.do("vector", lambda e, kk=kk, ps=ps: e.tensor_copy(
                    out=gT[:, kk * 128:(kk + 1) * 128], in_=ps[:, 0:128]),
                    after=[t_tr])
                ev_toks.append(ev)
            state["trps_free"] = [ev_toks[DC - 2], ev_toks[DC - 1]]
            t_gT = ev_toks[-1]
            for pi, w in enumerate(("WqT", "WkT", "WvT")):
                ps = qkv_ps[pi % 2]
                for kk in range(DC):
                    for hh in range(2):
                        after = ()
                        if kk == 0 and hh == 0:
                            after = [t_gT, state["qkv_free"][pi % 2],
                                     t_w if m == 0 else None,
                                     t_ones if m == 0 else None]
                        S.do("tensor", lambda e, w=w, kk=kk, hh=hh, ps=ps: e.matmul(
                            ps[:, hh * 512:(hh + 1) * 512],
                            gT[:, kk * 128:(kk + 1) * 128],
                            W[w][:, kk * D + hh * 512: kk * D + (hh + 1) * 512],
                            start=(kk == 0), stop=False), after=after)
                t_p = None
                for hh in range(2):
                    t_p = S.do("tensor", lambda e, w=w, hh=hh, ps=ps: e.matmul(
                        ps[:, hh * 512:(hh + 1) * 512], ones1[:],
                        br[{"WqT": "bq", "WkT": "bk", "WvT": "bv"}[w]][:, hh * 512:(hh + 1) * 512],
                        start=False, stop=True), sig="pe")
                if w == "WvT":
                    vt = v_t[m % 2]
                    t_ms = S.do("vector", lambda e, vt=vt: e.memset(vt[:], 1.0),
                                after=[state["vt_free"][m % 2]])
                    t_v = S.do("vector", lambda e, vt=vt, ps=ps: e.tensor_copy(
                        out=vt[:].rearrange("p (h g) -> p h g", g=65)[:, :, 0:64],
                        in_=ps[:].rearrange("p (h g) -> p h g", g=64)),
                        after=[t_p, t_ms])
                    for hg in range(8):
                        S.dma("sync", v_d[hg, m * 128:(m + 1) * 128, :],
                              vt[:, hg * 130:(hg + 1) * 130], after=[t_v],
                              sig="dmaout")
                    state["vt_free"][m % 2] = t_v
                    state["qkv_free"][pi % 2] = t_v
                else:
                    qb = qk_bf[pi]
                    Pv = ps[:].rearrange("p (h s f) -> p h s f", h=NHEAD, s=2)
                    cosv = cos_sb[:, m * 32:(m + 1) * 32].unsqueeze(1).unsqueeze(2) \
                        .broadcast_to([128, NHEAD, 2, 32])
                    sinv = sin_sb[:, m * 64:(m + 1) * 64] \
                        .rearrange("p (s f) -> p s f", s=2).unsqueeze(1) \
                        .broadcast_to([128, NHEAD, 2, 32])
                    t1v = t1f[:].rearrange("p (h s f) -> p h s f", h=NHEAD, s=2)
                    t2v = t2f[:].rearrange("p (h s f) -> p h s f", h=NHEAD, s=2)
                    S.do("vector", lambda e, Pv=Pv, cosv=cosv, t1v=t1v: e.tensor_tensor(
                        out=t1v, in0=Pv, in1=cosv, op=ALU.mult), after=[t_p, t_ld])
                    S.do("vector", lambda e, Pv=Pv, sinv=sinv, t2v=t2v: e.tensor_tensor(
                        out=t2v[:, :, 0, :], in0=Pv[:, :, 1, :],
                        in1=sinv[:, :, 0, :], op=ALU.mult))
                    S.do("vector", lambda e, Pv=Pv, sinv=sinv, t2v=t2v: e.tensor_tensor(
                        out=t2v[:, :, 1, :], in0=Pv[:, :, 0, :],
                        in1=sinv[:, :, 1, :], op=ALU.mult))
                    t_qb = S.do("vector", lambda e, qb=qb: e.tensor_add(
                        out=qb[:], in0=t1f[:], in1=t2f[:]),
                        after=[state["qk_free"][pi]])
                    state["qkv_free"][pi % 2] = t_qb
                    t_sq = S.do("gpsimd", lambda e, qb=qb: e.tensor_tensor(
                        out=sqs[:], in0=qb[:], in1=qb[:], op=ALU.mult), after=[t_qb])
                    S.do("vector", lambda e: e.reduce_sum(
                        out=red16[:], in_=sqs[:].rearrange("p (h f) -> p h f", h=NHEAD),
                        axis=AX.X), after=[t_sq])
                    run = qn_run if w == "WqT" else kn_run
                    S.do("vector", lambda e, run=run: e.tensor_max(
                        out=run[:], in0=run[:], in1=red16[:]),
                        after=[t_nrm0] if m == 0 else ())
                    out_d = q_d if w == "WqT" else k_d
                    t_dq = S.dma("sync", out_d[m * 128:(m + 1) * 128, :], qb[:],
                                 after=[t_qb], sig="dmaout")
                    state["qk_free"][pi] = t_dq

        t_dout = emit_ffn_D(S, nc, st, "d", C_d, Ytl_d, x_d, x1_d,
                            tile_cb=qkv_tile, psum_pp=1)
        t_last = [S.tok("dve"), S.tok("pool"), S.tok("pe")]
        for i, run in enumerate((qn_run, kn_run)):
            t_tr = S.do("tensor", lambda e, run=run: e.transpose(
                qkv_ps[1][0:16, 0:128], run[:], eyef[:]),
                after=t_last + [S.tok("dve")], sig="pe")
            S.do("vector", lambda e, i=i: e.reduce_max(
                out=nrm_sb[:, i:i + 1], in_=qkv_ps[1][0:16, 0:128], axis=AX.X),
                after=[t_tr])
        t_n = S.tok("dve")
        S.dma("sync", nrm_d, nrm_sb[:], after=[t_n], sig="dmaout")
        S.wait("sync", [S.tok("dmaout")])
        S.build(nc, st)
    return nc


def build_L6():
    nc = bass.Bass("TRN2", target_bir_lowering=False)
    x_d = nc.dram_tensor("x", [TOK, D], F32, kind="ExternalInput").ap()
    C_d = nc.dram_tensor("C", [TOK, SLOTS], BF16, kind="ExternalInput").ap()
    Ytl_d = nc.dram_tensor("Ytl", [SLOTS, D], F32, kind="ExternalInput").ap()
    x3_d = nc.dram_tensor("x3", [TOK, D], F32, kind="ExternalOutput").ap()
    S = Sched()
    with ExitStack() as st:
        t_out = emit_ffn_D(S, nc, st, "f", C_d, Ytl_d, x_d, x3_d, psum_pp=2)
        S.wait("sync", [S.tok("dmaout")])
        S.build(nc, st)
    return nc



def build_L4():
    nc = bass.Bass("TRN2", target_bir_lowering=False)
    x1_d = nc.dram_tensor("x1", [TOK, D], F32, kind="ExternalInput").ap()
    q_d = nc.dram_tensor("q", [TOK, D], BF16, kind="ExternalInput").ap()
    K_d = nc.dram_tensor("K", [SEQ, D], BF16, kind="ExternalInput").ap()
    V_d = nc.dram_tensor("V", [8, SEQ, 130], BF16, kind="ExternalInput").ap()
    qn_d = nc.dram_tensor("qn", [1, NHEAD], F32, kind="ExternalInput").ap()
    kn4_d = nc.dram_tensor("kn4", [1, 4 * NHEAD], F32, kind="ExternalInput").ap()
    WoT_d = nc.dram_tensor("WoT", [D, D], F32, kind="ExternalInput").ap()
    bo_d = nc.dram_tensor("bo", [1, D], F32, kind="ExternalInput").ap()
    phiT_d = nc.dram_tensor("phiT", [D, SLOTS], F32, kind="ExternalInput").ap()
    phib_d = nc.dram_tensor("phib", [1, SLOTS], F32, kind="ExternalInput").ap()
    eye_d = nc.dram_tensor("eye", [128, 128], BF16, kind="ExternalInput").ap()
    x2_d = nc.dram_tensor("x2", [TOK, D], F32, kind="ExternalOutput").ap()
    Xp2_d = nc.dram_tensor("Xp2", [SLOTS, D], F32, kind="ExternalOutput").ap()
    C2_d = nc.dram_tensor("C2", [TOK, SLOTS], BF16, kind="ExternalOutput").ap()
    aTdbg_d = (nc.dram_tensor("aTdbg", [128, DC * TOK], BF16,
                              kind="ExternalOutput").ap() if L4_DEBUG else None)
    nmdbg_d = (nc.dram_tensor("nmdbg", [128, NHEAD], F32,
                              kind="ExternalOutput").ap() if L4_DEBUG else None)
    rscr_d = [nc.dram_tensor(f"rscr{i}", [1, TOK // 2], F32, kind="Internal").ap()
              for i in range(2)]

    S = Sched()
    with ExitStack() as st:
        sb = lambda name, shape, dt: st.enter_context(nc.sbuf_tensor(name, shape, dt))
        negmb = sb("negmb", [128, NHEAD], F32)
        aT = sb("aT", [128, DC * TOK], BF16)

        nrq = sb("nrq", [128, NHEAD], F32)
        nrk = sb("nrk", [128, 4 * NHEAD], F32)
        t_qn = S.dma("sync", nrq[:], qn_d.broadcast_to([128, NHEAD]), sig="ld")
        t_kn = S.dma("sync", nrk[:], kn4_d.broadcast_to([128, 4 * NHEAD]), sig="ld")
        knv = nrk[:].rearrange("p (c h) -> p h c", c=4)
        S.do("vector", lambda e: e.reduce_max(
            out=negmb[:], in_=knv, axis=AX.X), after=[t_qn, t_kn])
        t_mul = S.do("vector", lambda e: e.tensor_mul(
            out=negmb[:], in0=negmb[:], in1=nrq[:]))
        S.do("scalar", lambda e: e.activation(
            out=negmb[:], in_=negmb[:], func=AF.Ln), after=[t_mul])
        t_ex = S.do("scalar", lambda e: e.activation(
            out=negmb[:], in_=negmb[:], func=AF.Exp, scale=0.5))
        t_nb = S.do("vector", lambda e: e.tensor_scalar_mul(
            out=negmb[:], in0=negmb[:], scalar1=-0.125), after=[t_ex])

        with ExitStack() as st2:
            sb2 = lambda name, shape, dt: st2.enter_context(nc.sbuf_tensor(name, shape, dt))
            kT = [sb2(f"kT{i}", [128, SEQ], BF16) for i in range(2)]
            qT = [sb2(f"qT{i}", [128, TOK // 2], BF16) for i in range(2)]
            vsb = [sb2(f"vsb{i}", [128, 64 * 130], BF16) for i in range(2)]
            E = [sb2(f"E{i}", [128, TOK // 2], BF16) for i in range(4)]
            rrow = [sb2(f"rrow{i}", [1, TOK // 2], F32) for i in range(2)]
            rb = [sb2(f"rb{i}", [64, TOK // 2], F32) for i in range(2)]
            s_ps = [st2.enter_context(nc.psum_tensor(f"sps{i}", [128, TOK // 2], F32))
                    for i in range(2)]
            av_ps = [st2.enter_context(nc.psum_tensor(f"av{i}", [65, TOK // 2], F32))
                     for i in range(2)]
            QW = TOK // 2

            buf_free = [None, None]
            av_free = [None, None]
            exp_done = [None, None]
            rrow_free = [None, None]
            rb_free = [None, None]
            di = 0
            seg_list = [(qt, hg) for qt in range(2) for hg in range(8)]
            for si, (qt, hg) in enumerate(seg_list):
                bi = si % 2
                t_kT = S.dma("sync", kT[bi][:],
                             K_d[0:SEQ, hg * 128:(hg + 1) * 128],
                             after=[buf_free[bi]], transpose=True, sig="ldk")
                t_qT = S.dma("sync", qT[bi][:],
                             q_d[qt * QW:(qt + 1) * QW, hg * 128:(hg + 1) * 128],
                             after=[buf_free[bi]], transpose=True, sig="ldk")
                t_v = S.dma("gpsimd",
                            vsb[bi][:].rearrange("p (kb f) -> p kb f", f=130),
                            V_d[hg].rearrange("(kb p) f -> p kb f", p=128),
                            after=[buf_free[bi]], sig="ldv")
                t_loads = [t_kT, t_qT, t_v, t_nb]
                for h2 in range(2):
                    h = 2 * hg + h2
                    avp = di % 2
                    av = av_ps[avp]
                    exp_tok = [None] * 64
                    av_tok = [None] * 64

                    def emit_scores(kb, h2=h2, bi=bi, h=h):
                        sp = s_ps[kb % 2]
                        after0 = [exp_done[kb % 2]]
                        if kb < 2:
                            after0 = after0 + t_loads
                        S.do("tensor", lambda e, kb=kb, sp=sp, h2=h2, bi=bi: e.matmul(
                            sp[:, 0:512],
                            kT[bi][h2 * 64:(h2 + 1) * 64, kb * 128:(kb + 1) * 128],
                            qT[bi][h2 * 64:(h2 + 1) * 64, 0:512],
                            start=True, stop=True), after=after0)
                        t_s = S.do("tensor", lambda e, kb=kb, sp=sp, h2=h2, bi=bi: e.matmul(
                            sp[:, 512:1024],
                            kT[bi][h2 * 64:(h2 + 1) * 64, kb * 128:(kb + 1) * 128],
                            qT[bi][h2 * 64:(h2 + 1) * 64, 512:1024],
                            start=True, stop=True), sig="pe")
                        t_e = S.do("scalar", lambda e, kb=kb, sp=sp, h=h: e.activation(
                            out=E[kb % 4][:], in_=sp[:], func=AF.Exp,
                            scale=0.125, bias=negmb[:, h:h + 1]),
                            after=[t_s], chain=False)
                        exp_done[kb % 2] = t_e
                        exp_tok[kb] = t_e

                    def emit_av(kb, h2=h2, bi=bi, av=av, avp=avp):
                        for hh in range(2):
                            after1 = []
                            if kb == 0 and hh == 0:
                                after1 = [exp_tok[kb], av_free[avp]]
                            elif hh == 0:
                                after1 = [exp_tok[kb]]
                            t = S.do("tensor", lambda e, kb=kb, hh=hh, av=av, h2=h2, bi=bi: e.matmul(
                                av[:, hh * 512:(hh + 1) * 512],
                                vsb[bi][:].rearrange("p (kb f) -> p kb f", f=130)
                                [:, kb, h2 * 65:(h2 + 1) * 65],
                                E[kb % 4][:, hh * 512:(hh + 1) * 512],
                                start=(kb == 0), stop=(kb == 63)),
                                after=after1,
                                sig="pe" if hh == 1 else None)
                            if hh == 1:
                                av_tok[kb] = t

                    emit_scores(0)
                    emit_scores(1)
                    for kb in range(2, 64):
                        emit_scores(kb)
                        emit_av(kb - 2)
                    emit_av(62)
                    emit_av(63)
                    t_av = S.tok("pe")
                    rp = di % 2
                    t_rc = S.do("vector", lambda e, av=av, rp=rp: e.reciprocal(
                        out=rrow[rp][:], in_=av[64:65, :]),
                        after=[t_av, rrow_free[rp]])
                    t_r1 = S.dma("sync", rscr_d[rp], rrow[rp][:], after=[t_rc],
                                 sig="ldr")
                    rrow_free[rp] = t_r1
                    t_r2 = S.dma("sync", rb[rp][:],
                                 rscr_d[rp].broadcast_to([64, QW]),
                                 after=[t_r1, rb_free[rp]], sig="ldr")
                    t_aT = S.do("vector", lambda e, av=av, rp=rp, h2=h2, hg=hg, qt=qt:
                                e.tensor_tensor(
                                    out=aT[h2 * 64:(h2 + 1) * 64,
                                           hg * TOK + qt * QW: hg * TOK + (qt + 1) * QW],
                                    in0=av[0:64, :], in1=rb[rp][:], op=ALU.mult),
                                after=[t_r2])
                    rb_free[rp] = t_aT
                    av_free[avp] = t_aT
                    di += 1
                buf_free[bi] = S.tok("pe")
            t_attn = [S.tok("dve"), S.tok("pe")]
            if L4_DEBUG:
                S.dma("sync", aTdbg_d, aT[:], after=t_attn, sig="dmaout")
                S.dma("sync", nmdbg_d, negmb[:], after=t_attn, sig="dmaout")

        with ExitStack() as st3:
            sb3 = lambda name, shape, dt: st3.enter_context(nc.sbuf_tensor(name, shape, dt))
            WoT = sb3("WoT_s", [128, DC * D], BF16)
            bor = sb3("bor", [1, D], BF16)
            ones1o = sb3("ones1o", [1, 128], BF16)
            x1t = [sb3(f"x1t{i}", [128, D], F32) for i in range(2)]
            x2t = [sb3(f"x2t{i}", [128, D], F32) for i in range(2)]
            o_ps = [st3.enter_context(nc.psum_tensor(f"ops{i}", [128, D], F32))
                    for i in range(2)]
            attn_done = t_attn + [S.tok("act")]
            t_ones = S.do("vector", lambda e: e.memset(ones1o[:], 1.0),
                          after=attn_done)
            for kk in range(DC):
                S.dma("gpsimd", WoT[:, kk * D:(kk + 1) * D],
                      WoT_d[kk * 128:(kk + 1) * 128, :], after=attn_done, sig="ldw")
            S.dma("gpsimd", bor[:], bo_d, after=attn_done, sig="ldw")
            t_wo = S.tok("ldw")
            x1_free = [None, None]
            x2_free = [None, None]
            o_free = [None, None]
            x2_out = [None] * NT
            for m in range(NT):
                t_x1 = S.dma("sync", x1t[m % 2][:], x1_d[m * 128:(m + 1) * 128, :],
                             after=[x1_free[m % 2]], sig="ldx")
                ps = o_ps[m % 2]
                for kk in range(DC):
                    for hh in range(2):
                        after = ()
                        if kk == 0 and hh == 0:
                            after = t_attn + [t_wo, o_free[m % 2]] + \
                                ([t_ones] if m == 0 else [])
                        S.do("tensor", lambda e, m=m, kk=kk, hh=hh, ps=ps: e.matmul(
                            ps[:, hh * 512:(hh + 1) * 512],
                            aT[:, kk * TOK + m * 128: kk * TOK + (m + 1) * 128],
                            WoT[:, kk * D + hh * 512: kk * D + (hh + 1) * 512],
                            start=(kk == 0), stop=False), after=after)
                t_p = None
                for hh in range(2):
                    t_p = S.do("tensor", lambda e, hh=hh, ps=ps: e.matmul(
                        ps[:, hh * 512:(hh + 1) * 512], ones1o[:],
                        bor[:, hh * 512:(hh + 1) * 512], start=False, stop=True),
                        sig="pe")
                t_x2 = S.do("vector", lambda e, m=m, ps=ps: e.tensor_add(
                    out=x2t[m % 2][:], in0=ps[:], in1=x1t[m % 2][:]),
                    after=[t_p, t_x1, x2_free[m % 2]])
                o_free[m % 2] = t_x2
                x1_free[m % 2] = t_x2
                t_o = S.dma("sync", x2_d[m * 128:(m + 1) * 128, :], x2t[m % 2][:],
                            after=[t_x2], sig="dmaout")
                x2_free[m % 2] = t_o
                x2_out[m] = t_o

        oproj_done = [S.tok("pe"), S.tok("dve")]
        sb4 = lambda name, shape, dt: st.enter_context(nc.sbuf_tensor(name, shape, dt))
        xt = [sb4(f"xtf{i}", [128, D], F32) for i in range(2)]
        phiT = sb4("phiT_s", [128, DC * SLOTS], BF16)
        phib = sb4("phib_s", [1, SLOTS], BF16)
        ones1 = sb4("ones1f", [1, 128], BF16)
        eye = sb4("eye_s", [128, 128], BF16)
        lnscr = {k2: sb4("lnf_" + k2, [128, 1], F32)[:] for k2 in ("mu", "var", "s2", "rstd")}
        lnscr["xsq"] = sb4("lnf_xsq", [128, D], F32)[:]
        lnscr["eps"] = sb4("lnf_eps", [128, 1], F32)[:]
        S.do("vector", lambda e: e.memset(lnscr["eps"], EPS))
        b = dict(uid="f4", phiT=phiT[:], phib=phib[:], ones1=ones1[:], eye=eye[:],
                 lnscr=lnscr)
        b["t_ones"] = S.do("vector", lambda e: e.memset(ones1[:], 1.0))
        t_eye = S.dma("sync", eye[:], eye_d, after=oproj_done, sig="ld")
        for kk in range(DC):
            S.dma("gpsimd", phiT[:, kk * SLOTS:(kk + 1) * SLOTS],
                  phiT_d[kk * 128:(kk + 1) * 128, :], after=oproj_done, sig="ldw")
        S.dma("gpsimd", phib[:], phib_d, after=oproj_done, sig="ldw")
        t_w = S.tok("ldw")
        xf_free = [None, None]

        def get_x_tile(m):
            t_x = S.dma("sync", xt[m % 2][:], x2_d[m * 128:(m + 1) * 128, :],
                        after=[xf_free[m % 2], x2_out[m]] +
                        (oproj_done if m < 2 else []), sig="ldxf")
            return xt[m % 2][:], [t_x, t_w, t_eye]

        def x_done(m, t_h):
            xf_free[m % 2] = t_h
        b["x_done"] = x_done
        emit_ffn_AB(S, nc, st, b, get_x_tile, C2_d, Xp2_d)
        S.wait("sync", [S.tok("dmaout")])
        S.build(nc, st)
    return nc


def _prep_common(Wq, bq, Wk, bk, Wv, bv, Wo, bo, phi_w, phi_b, expert_w, expert_b):
    c = {}
    c["phiT"] = np.ascontiguousarray(phi_w.T)
    c["phib"] = np.ascontiguousarray(phi_b[None, :])
    c["WqT"] = np.ascontiguousarray(Wq.T)
    c["WkT"] = np.ascontiguousarray(Wk.T)
    c["WvT"] = np.ascontiguousarray(Wv.T)
    c["WoT"] = np.ascontiguousarray(Wo.T)
    c["bq"] = np.ascontiguousarray(bq[None, :])
    c["bk"] = np.ascontiguousarray(bk[None, :])
    c["bv"] = np.ascontiguousarray(bv[None, :])
    c["bo"] = np.ascontiguousarray(bo[None, :])
    c["ew"] = np.ascontiguousarray(expert_w)
    c["eb"] = np.ascontiguousarray(expert_b)
    c["eye"] = np.eye(128, dtype=ml_dtypes.bfloat16)
    inv = 10000.0 ** (-np.arange(0, DH, 2, dtype=np.float64) / DH)
    cos_l, sin_l = [], []
    for core in range(NCORE):
        pos = (core % 4) * TOK + np.arange(TOK, dtype=np.float64)
        fr = np.outer(pos, inv)
        cos_l.append(np.cos(fr).astype(np.float32))
        s = np.sin(fr).astype(np.float32)
        sin_l.append(np.ascontiguousarray(np.concatenate([-s, s], axis=1)))
    c["cos"] = cos_l
    c["sin2"] = sin_l
    return c


def run_expert_launch(Xp_list, c):
    nc2 = build_L2()
    in2 = []
    for cc in range(NCORE):
        xp8 = np.stack([Xp_list[src][cc * EL * PSL:(cc + 1) * EL * PSL]
                        for src in range(NCORE)])
        in2.append({"Xp8": np.ascontiguousarray(xp8),
                    "ew": np.ascontiguousarray(c["ew"][cc * EL:(cc + 1) * EL]),
                    "eb": np.ascontiguousarray(c["eb"][cc * EL:(cc + 1) * EL]),
                    "eye": c["eye"]})
    r2 = run_bass_kernel_spmd(nc2, in2, core_ids=list(range(NCORE)))
    return np.ascontiguousarray(
        np.concatenate([r2.results[cc]["Ytl"] for cc in range(NCORE)], axis=0))


def kernel(L0_attn_x, L0_attn_Wq, L0_attn_bq, L0_attn_Wk, L0_attn_bk, L0_attn_Wv,
           L0_attn_bv, L0_attn_Wo, L0_attn_bo, L0_ffn_phi_w, L0_ffn_phi_b,
           L0_ffn_expert_w, L0_ffn_expert_b):
    c = _prep_common(L0_attn_Wq, L0_attn_bq, L0_attn_Wk, L0_attn_bk, L0_attn_Wv,
                     L0_attn_bv, L0_attn_Wo, L0_attn_bo, L0_ffn_phi_w,
                     L0_ffn_phi_b, L0_ffn_expert_w, L0_ffn_expert_b)
    x0 = np.ascontiguousarray(np.asarray(L0_attn_x, dtype=np.float32)
                              .reshape(NCORE * TOK, D))
    eyef = np.eye(128, dtype=np.float32)
    cores = list(range(NCORE))

    nc1 = build_L1()
    in1 = [{"x": np.ascontiguousarray(x0[cc * TOK:(cc + 1) * TOK]),
            "phiT": c["phiT"], "phib": c["phib"], "eye": c["eye"]}
           for cc in cores]
    r1 = run_bass_kernel_spmd(nc1, in1, core_ids=cores)
    Xp1 = [r1.results[cc]["Xp"] for cc in cores]
    C1 = [r1.results[cc]["C"] for cc in cores]

    Ytl1 = run_expert_launch(Xp1, c)

    nc3 = build_L3()
    in3 = [{"x": in1[cc]["x"], "C": C1[cc], "Ytl": Ytl1,
            "WqT": c["WqT"], "WkT": c["WkT"], "WvT": c["WvT"],
            "bq": c["bq"], "bk": c["bk"], "bv": c["bv"],
            "cos": c["cos"][cc], "sin2": c["sin2"][cc],
            "eye": c["eye"], "eyef": eyef} for cc in cores]
    r3 = run_bass_kernel_spmd(nc3, in3, core_ids=cores)

    nc4 = build_L4()
    in4 = []
    for cc in cores:
        grp = range(4 * (cc // 4), 4 * (cc // 4) + 4)
        Kf = np.ascontiguousarray(
            np.concatenate([r3.results[g]["k"] for g in grp], axis=0))
        Vf = np.ascontiguousarray(
            np.concatenate([r3.results[g]["v"] for g in grp], axis=1))
        kn4 = np.ascontiguousarray(
            np.concatenate([r3.results[g]["norms"][:, 1] for g in grp])[None, :])
        qn = np.ascontiguousarray(r3.results[cc]["norms"][:, 0][None, :])
        in4.append({"x1": r3.results[cc]["x1"], "q": r3.results[cc]["q"],
                    "K": Kf, "V": Vf, "qn": qn, "kn4": kn4,
                    "WoT": c["WoT"], "bo": c["bo"],
                    "phiT": c["phiT"], "phib": c["phib"], "eye": c["eye"]})
    r4 = run_bass_kernel_spmd(nc4, in4, core_ids=cores)
    Xp2 = [r4.results[cc]["Xp2"] for cc in cores]
    C2 = [r4.results[cc]["C2"] for cc in cores]

    Ytl2 = run_expert_launch(Xp2, c)

    nc6 = build_L6()
    in6 = [{"x": r4.results[cc]["x2"], "C": C2[cc], "Ytl": Ytl2} for cc in cores]
    r6 = run_bass_kernel_spmd(nc6, in6, core_ids=cores)
    out = np.concatenate([r6.results[cc]["x3"] for cc in cores], axis=0)
    return out.reshape(2, SEQ, D)


# version 08 (final)
# speedup vs baseline: 1.6329x; 1.0747x over previous
import numpy as np
import ml_dtypes
from contextlib import ExitStack
import concourse.bass as bass
import concourse.mybir as mybir
from concourse.bass_utils import run_bass_kernel_spmd

F32 = mybir.dt.float32
BF16 = mybir.dt.bfloat16
AF = mybir.ActivationFunctionType
ALU = mybir.AluOpType
AX = mybir.AxisListType

NCORE = 8
TOK = 2048
NT = TOK // 128
D = 1024
DC = D // 128
SLOTS = 2048
NEXP = 64
PSL = 32
EL = NEXP // NCORE
NHEAD = 16
DH = 64
L4_DEBUG = False
SEQ = 8192
EPS = 1e-5


class Sched:

    def __init__(self):
        self.ops = []
        self.counts = {}

    AUTO_SIG = {"vector": "dve", "scalar": "act", "gpsimd": "pool"}

    def tok(self, sem):
        return (sem, self.counts.get(sem, 0))

    def do(self, eng, fn, after=(), sig=None, inc=1, chain=True):
        auto = self.AUTO_SIG.get(eng)
        if sig is None and auto is not None:
            sig = auto
        after = [t for t in after if t is not None]
        if chain and auto is not None and self.counts.get(auto, 0) > 0:
            after.append((auto, self.counts[auto]))
        after = tuple(after)
        for s, v in after:
            assert v <= self.counts.get(s, 0), (s, v, self.counts.get(s, 0))
        self.ops.append((eng, after, fn, sig or None, inc if sig else 0))
        if sig:
            self.counts[sig] = self.counts.get(sig, 0) + inc
            return (sig, self.counts[sig])
        return None

    def dma(self, eng, out, in_, after=(), sig="dma", **kw):
        return self.do(eng, lambda e: e.dma_start(out=out, in_=in_, **kw),
                       after=after, sig=sig, inc=16, chain=False)

    def wait(self, eng, after):
        self.do(eng, lambda e: None, after=after, sig=False, chain=False)

    def build(self, nc, stack):
        sems = {n: stack.enter_context(nc.semaphore("s_" + n)) for n in self.counts}
        per_eng = {}
        for eng, after, fn, sig, inc in self.ops:
            per_eng.setdefault(eng, []).append((after, fn, sig, inc))
        block = stack.enter_context(nc.Block())

        def mk(items):
            def body(e):
                waited = {}
                for after, fn, sig, inc in items:
                    for s, v in after:
                        if waited.get(s, 0) < v:
                            e.wait_ge(sems[s], v)
                            waited[s] = v
                    ins = fn(e)
                    if sig and ins is not None:
                        ins.then_inc(sems[sig], inc)
            return body
        for eng_name, items in per_eng.items():
            getattr(block, eng_name)(mk(items))


def emit_ln_sq(S, x_ap, scr, guard):
    t_sq = S.do("scalar", lambda e: e.activation(
        out=scr["xsq"], in_=x_ap, func=AF.Square),
        after=list(guard) + [scr.get("xsq_free")], sig="act")
    scr["sq_tok"] = t_sq
    return t_sq


def emit_ln(S, x_ap, h_out_ap, scr, guard, out_dve_sig="dve", h_eng="vector"):
    mu, var, s2, rstd, xsq = scr["mu"], scr["var"], scr["s2"], scr["rstd"], scr["xsq"]
    t_sq = scr.pop("sq_tok", None)
    if t_sq is None:
        t_sq = emit_ln_sq(S, x_ap, scr, guard)
        scr.pop("sq_tok")
    S.do("vector", lambda e: e.reduce_sum(out=mu, in_=x_ap, axis=AX.X), after=guard)
    S.do("vector", lambda e: e.tensor_scalar_mul(out=mu, in0=mu, scalar1=1.0 / x_ap.shape[-1]))
    t_s2 = S.do("vector", lambda e: e.reduce_sum(out=s2, in_=xsq, axis=AX.X),
                after=[t_sq], sig="dve")
    scr["xsq_free"] = t_s2
    S.do("vector", lambda e: e.tensor_mul(out=var, in0=mu, in1=mu))
    t_v = S.do("vector", lambda e: e.scalar_tensor_tensor(
        out=var, in0=s2, scalar=1.0 / x_ap.shape[-1], in1=var,
        op0=ALU.mult, op1=ALU.subtract), sig="dve")
    S.do("scalar", lambda e: e.activation(out=s2, in_=var, func=AF.Ln, bias=scr["eps"]),
         after=[t_v])
    t_act = S.do("scalar", lambda e: e.activation(out=rstd, in_=s2, func=AF.Exp,
                                                  scale=-0.5), sig="act")
    t_h = S.do(h_eng, lambda e: e.tensor_scalar(
        out=h_out_ap, in0=x_ap, scalar1=mu, scalar2=rstd,
        op0=ALU.subtract, op1=ALU.mult), after=[t_act],
        sig=("pool" if h_eng == "gpsimd" else out_dve_sig))
    return t_h, t_act


def emit_ffn_H1(S, m, b):
    if m + 2 < NT:
        b["x_load"](m + 2)
    x_tile_ap, after_x = b["x_ap"](m)
    hm = b["h_all"][:, m * D:(m + 1) * D]
    t_h, t_ln_act = emit_ln(S, x_tile_ap, hm, b["lnscr"],
                            list(after_x) + [b["ln_free"]])
    b["ln_free"] = t_ln_act
    b["x_done"](m, t_h)
    t_hd = S.dma("scalar", b["h_dram"][m * 128:(m + 1) * 128, :], hm, after=[t_h],
                 sig="hd" + b["uid"])
    hT = b["hT"][m % 2]
    tr_toks = []
    for k in range(DC):
        t_tr = S.dma("sync", hT[:, k * 128:(k + 1) * 128],
                     b["h_dram"][m * 128:(m + 1) * 128, k * 128:(k + 1) * 128],
                     after=[t_hd, b["hT_free"][m % 2]], transpose=True,
                     sig="ht" + b["uid"])
        tr_toks.append(t_tr)
    lp = b["logits_ps"]
    t_lg = [None, None]
    for half in range(2):
        L = lp[2 * (m % 2) + half]
        for k in range(DC):
            after = [b["t_wk"][k]] if m < 2 else []
            if k == 0:
                after += [tr_toks[-1], b["logits_free"][2 * (m % 2) + half]] + \
                    ([b["t_ones"], b["t_wb"]] if m < 2 else [])
            for j in range(2):
                S.do("tensor", lambda e, k=k, j=j, half=half, L=L: e.matmul(
                    L[:, j * 512:(j + 1) * 512],
                    b["hT"][m % 2][:, k * 128:(k + 1) * 128],
                    b["phiT"][:, k * SLOTS + half * 1024 + j * 512:
                              k * SLOTS + half * 1024 + (j + 1) * 512],
                    start=(k == 0), stop=False), after=after if j == 0 else ())
        for j in range(2):
            t_lg[half] = S.do("tensor", lambda e, j=j, half=half, L=L: e.matmul(
                L[:, j * 512:(j + 1) * 512], b["ones1"],
                b["phib"][:, half * 1024 + j * 512: half * 1024 + (j + 1) * 512],
                start=False, stop=True), sig="pe")
    b["hT_free"][m % 2] = t_lg[1]
    b["lg_tok"][m % 2] = t_lg
    return t_h


def emit_ffn_H2a(S, m, b):
    p = m % 2
    lp = b["logits_ps"]
    t_lg = b["lg_tok"][p]
    negm, negm2 = b["negm"][p], b["negm2"][p]
    t_n0 = S.do("vector", lambda e: e.reduce_max(
        out=negm, in_=lp[2 * p][:], axis=AX.X, negate=True),
        after=[t_lg[0], b["negm_free"][p]])
    t_n1 = S.do("vector", lambda e: e.reduce_max(
        out=negm2, in_=lp[2 * p + 1][:], axis=AX.X, negate=True), after=[t_lg[1]])
    t_nm = S.do("vector", lambda e: e.tensor_scalar(
        out=negm, in0=negm, scalar1=negm2, scalar2=None, op0=ALU.min))
    E = b["E_t"][m % b["nring"]]
    t_EA = S.do("scalar", lambda e: e.activation(
        out=E[:, 0:1024], in_=lp[2 * p][:], func=AF.Exp, bias=negm),
        after=[t_nm] + b["E_free"][m % b["nring"]], chain=False)
    t_EB = S.do("scalar", lambda e: e.activation(
        out=E[:, 1024:2048], in_=lp[2 * p + 1][:], func=AF.Exp, bias=negm),
        after=[], chain=True)
    b["logits_free"][2 * p] = t_EA
    b["logits_free"][2 * p + 1] = t_EB
    b["negm_free"][p] = t_EB
    b["exp_tok"][m % b["nring"]] = t_EB


def emit_ffn_H2b(S, m, b):
    p = m % 2
    E = b["E_t"][m % b["nring"]]
    t_EB = b["exp_tok"][m % b["nring"]]
    sp, rp, sn, rn = b["sp"][p], b["rp"][p], b["sn"][p], b["rn"][p]
    Ev = E[:].rearrange("p (n q) -> p n q", n=NEXP)
    Evt = E[:].rearrange("p (n q) -> p q n", n=NEXP)
    aft = [t_EB] + b["srr_free"][p]
    S.do("vector", lambda e: e.reduce_sum(out=sp, in_=Ev, axis=AX.X), after=aft)
    t_rp = S.do("vector", lambda e: e.reciprocal(out=rp, in_=sp), sig="dve")
    S.do("vector", lambda e: e.reduce_sum(out=sn, in_=Evt, axis=AX.X))
    t_rn = S.do("vector", lambda e: e.reciprocal(out=rn, in_=sn), sig="dve")
    Dv = b["D_all"][:, m * SLOTS:(m + 1) * SLOTS].rearrange("p (n q) -> p n q", n=NEXP)
    t_D = S.do("gpsimd", lambda e: e.tensor_tensor(
        out=Dv, in0=Ev, in1=rp.unsqueeze(2).broadcast_to([128, NEXP, PSL]),
        op=ALU.mult), after=[t_rp], sig="pool")
    C = b["C_t"][m % b["nring"]]
    Cv = C[:].rearrange("p (n q) -> p n q", n=NEXP)
    t_C = S.do("gpsimd", lambda e: e.tensor_tensor(
        out=Cv, in0=Ev, in1=rn.unsqueeze(1).broadcast_to([128, NEXP, PSL]),
        op=ALU.mult), after=[t_rn, b["C_free"][m % b["nring"]]], sig="pool")
    t_Cout = S.dma("gpsimd", b["C_out"][m * 128:(m + 1) * 128, :], C[:],
                   after=[t_C], sig="dmaout")
    b["C_free"][m % b["nring"]] = t_Cout
    b["E_free"][m % b["nring"]] = [t_D, t_C]
    b["srr_free"][p] = [t_D, t_C]
    return t_D


def emit_ffn_AB(S, nc, st, b, C_out, Xp_out):
    sb = lambda name, shape, dt: st.enter_context(nc.sbuf_tensor(name, shape, dt))
    uid = b["uid"]
    b["h_all"] = sb(f"h_all{uid}", [128, NT * D], BF16)[:]
    b["hT"] = [sb(f"hT{uid}{i}", [128, D], BF16) for i in range(2)]
    b["h_dram"] = nc.dram_tensor(f"h_dram{uid}", [TOK, D], BF16, kind="Internal").ap()
    b["D_all"] = sb(f"D_all{uid}", [128, NT * SLOTS], BF16)[:]
    ne = b.setdefault("nring", 3)
    b["E_t"] = [sb(f"E{uid}_{i}", [128, SLOTS], BF16) for i in range(ne)]
    b["C_t"] = [sb(f"Ct{uid}_{i}", [128, SLOTS], BF16) for i in range(ne)]
    b["negm"] = [sb(f"negm{uid}{i}", [128, 1], F32)[:] for i in range(2)]
    b["negm2"] = [sb(f"negm2{uid}{i}", [128, 1], F32)[:] for i in range(2)]
    b["sp"] = [sb(f"sp{uid}{i}", [128, NEXP], F32)[:] for i in range(2)]
    b["rp"] = [sb(f"rp{uid}{i}", [128, NEXP], F32)[:] for i in range(2)]
    b["sn"] = [sb(f"sn{uid}{i}", [128, PSL], F32)[:] for i in range(2)]
    b["rn"] = [sb(f"rn{uid}{i}", [128, PSL], F32)[:] for i in range(2)]
    b["C_out"] = C_out
    b["C_free"] = [None] * ne
    b["E_free"] = [[] for _ in range(ne)]
    b["exp_tok"] = [None] * ne
    b["srr_free"] = [[], []]
    b["negm_free"] = [None, None]
    b["logits_free"] = [None] * 4
    b["ln_free"] = None
    b["hT_free"] = [None, None]
    b["lg_tok"] = [None, None]

    ps_stack = ExitStack()
    b["logits_ps"] = [ps_stack.enter_context(
        nc.psum_tensor(f"lg{uid}{i}", [128, 1024], F32)) for i in range(4)]

    b["x_load"](0)
    b["x_load"](1)
    for m in range(2):
        emit_ffn_H1(S, m, b)
    for m in range(NT):
        if m + 2 < NT:
            x_ap2, after_x2 = b["x_ap"](m + 2)
            emit_ln_sq(S, b["x_ap"](m + 2)[0], b["lnscr"], after_x2)
        emit_ffn_H2a(S, m, b)
        if m + 2 < NT:
            emit_ffn_H1(S, m + 2, b)
        emit_ffn_H2b(S, m, b)

    t_lastA = [S.tok("dve"), S.tok("pool")]
    ps_stack.close()

    Xps = [st.enter_context(nc.psum_tensor(f"Xps{uid}_{j}", [128, D], F32))
           for j in range(4)]
    nxev = b.get("nxev", 2)
    Xev = [sb(f"Xev{uid}_{j}", [128, D], F32) for j in range(nxev)]
    bank_free = [None] * 4
    xev_dma = [None] * nxev
    for s in range(4):
        for m in range(NT):
            for j in range(4):
                slot0 = s * 512 + j * 128
                after = []
                if m == 0:
                    after = (t_lastA if s == 0 else []) + [bank_free[j]]
                for half in range(2):
                    S.do("tensor", lambda e, m=m, j=j, half=half, slot0=slot0: e.matmul(
                        Xps[j][:, half * 512:(half + 1) * 512],
                        b["D_all"][:, m * SLOTS + slot0: m * SLOTS + slot0 + 128],
                        b["h_all"][:, m * D + half * 512: m * D + (half + 1) * 512],
                        start=(m == 0), stop=(m == NT - 1)),
                        after=after if half == 0 else (),
                        sig="pe" if (m == NT - 1 and half == 1) else None)
        t_s = S.tok("pe")
        for j in range(4):
            t_ev = S.do("vector", lambda e, j=j: e.tensor_copy(
                out=Xev[j % nxev][:], in_=Xps[j][:]), after=[t_s, xev_dma[j % nxev]],
                sig="dve")
            bank_free[j] = t_ev
            xev_dma[j % nxev] = S.dma(
                "sync", Xp_out[s * 512 + j * 128: s * 512 + (j + 1) * 128, :],
                Xev[j % nxev][:], after=[t_ev], sig="dmaout")
    return S.tok("dmaout")


def build_L1():
    nc = bass.Bass("TRN2", target_bir_lowering=False)
    x_d = nc.dram_tensor("x", [TOK, D], F32, kind="ExternalInput").ap()
    phiT_d = nc.dram_tensor("phiT", [D, SLOTS], F32, kind="ExternalInput").ap()
    phib_d = nc.dram_tensor("phib", [1, SLOTS], F32, kind="ExternalInput").ap()
    eye_d = nc.dram_tensor("eye", [128, 128], BF16, kind="ExternalInput").ap()
    Xp_d = nc.dram_tensor("Xp", [SLOTS, D], F32, kind="ExternalOutput").ap()
    C_d = nc.dram_tensor("C", [TOK, SLOTS], BF16, kind="ExternalOutput").ap()

    S = Sched()
    with ExitStack() as st:
        sb = lambda name, shape, dt: st.enter_context(nc.sbuf_tensor(name, shape, dt))
        xt = [sb(f"xt{i}", [128, D], F32) for i in range(4)]
        phiT = sb("phiT_s", [128, DC * SLOTS], BF16)
        phib = sb("phib_s", [1, SLOTS], BF16)
        ones1 = sb("ones1", [1, 128], BF16)
        eye = sb("eye_s", [128, 128], BF16)
        lnscr = {k: sb("ln_" + k, [128, 1], F32)[:] for k in ("mu", "var", "s2", "rstd")}
        lnscr["xsq"] = sb("ln_xsq", [128, D], F32)[:]
        lnscr["eps"] = sb("ln_eps", [128, 1], F32)[:]
        S.do("vector", lambda e: e.memset(lnscr["eps"], EPS))

        b = dict(uid="a", phiT=phiT[:], phib=phib[:], ones1=ones1[:], eye=eye[:],
                 lnscr=lnscr)
        b["t_ones"] = S.do("vector", lambda e: e.memset(ones1[:], 1.0), sig="dve")
        t_eye = S.dma("sync", eye[:], eye_d, sig="ld")
        b["t_wk"] = [S.dma("gpsimd", phiT[:, k * SLOTS:(k + 1) * SLOTS],
                           phiT_d[k * 128:(k + 1) * 128, :], sig="ldw")
                     for k in range(DC)]
        b["t_wb"] = S.dma("gpsimd", phib[:], phib_d, sig="ldw")

        x_free = [None] * 4
        x_tok = [None] * NT

        def x_load(m):
            x_tok[m] = S.dma("sync", xt[m % 4][:], x_d[m * 128:(m + 1) * 128, :],
                             after=[x_free[m % 4]], sig="ldx")
        b["x_load"] = x_load
        b["x_ap"] = lambda m: (xt[m % 4][:], [x_tok[m]])

        def x_done(m, t_h):
            x_free[m % 4] = t_h
        b["x_done"] = x_done

        t_out = emit_ffn_AB(S, nc, st, b, C_d, Xp_d)
        S.wait("sync", [t_out, S.tok("dmaout")])
        S.build(nc, st)
    return nc


def build_L2():
    nc = bass.Bass("TRN2", target_bir_lowering=False)
    Xp8_d = nc.dram_tensor("Xp8", [NCORE, EL * PSL, D], F32, kind="ExternalInput").ap()
    ew_d = nc.dram_tensor("ew", [EL, D, D], F32, kind="ExternalInput").ap()
    eb_d = nc.dram_tensor("eb", [EL, D], F32, kind="ExternalInput").ap()
    eye_d = nc.dram_tensor("eye", [128, 128], BF16, kind="ExternalInput").ap()
    Y_d = nc.dram_tensor("Ytl", [EL * PSL, D], F32, kind="ExternalOutput").ap()

    S = Sched()
    with ExitStack() as st:
        sb = lambda name, shape, dt: st.enter_context(nc.sbuf_tensor(name, shape, dt))
        xp = sb("xp_s", [128, NCORE * 2 * D], F32)
        xs = sb("xs", [128, 2 * D], F32)
        xn = sb("xn", [128, 2 * D], F32)
        xb = sb("xb", [128, 2 * D], BF16)
        XT = sb("XT", [128, 2 * DC * 128], BF16)
        ew = [sb(f"ew{i}", [128, DC * D], BF16) for i in range(2)]
        ebr_s = sb("ebr_s", [1, EL * D], BF16)
        ebr = ebr_s
        ones1 = sb("ones1", [1, 128], BF16)
        eye = sb("eye_s", [128, 128], BF16)
        Yev = sb("Yev", [32, EL * D], F32)
        lnscr = {k: sb("ln_" + k, [128, 1], F32)[:] for k in ("mu", "var", "s2", "rstd")}
        lnscr["xsq"] = sb("ln_xsq", [128, D], F32)[:]
        lnscr["eps"] = sb("ln_eps", [128, 1], F32)[:]
        S.do("vector", lambda e: e.memset(lnscr["eps"], EPS))

        tr_ps = [st.enter_context(nc.psum_tensor(f"tr{i}", [128, 1024], BF16))
                 for i in range(2)]
        Y_ps = [st.enter_context(nc.psum_tensor(f"Y{i}", [32, D], F32))
                for i in range(2)]

        t_ones = S.do("vector", lambda e: e.memset(ones1[:], 1.0), sig="dve")
        t_eye = S.dma("sync", eye[:], eye_d, sig="ld")
        for src in range(NCORE):
            for half in range(2):
                S.dma("sync", xp[:, (src * 2 + half) * D:(src * 2 + half + 1) * D],
                      Xp8_d[src, half * 128:(half + 1) * 128, :], sig="ldx")
        t_x = S.tok("ldx")
        for n in range(EL):
            S.dma("gpsimd", ebr[:, n * D:(n + 1) * D], eb_d[n:n + 1, :], sig="ldb")
        t_eb = S.tok("ldb")
        ew_ready = [None] * EL
        for n in range(2):
            for k in range(DC):
                S.dma("gpsimd", ew[n][:, k * D:(k + 1) * D],
                      ew_d[n, k * 128:(k + 1) * 128, :], sig="ldw")
            ew_ready[n] = S.tok("ldw")

        ln_free = None
        t_tanh = [None, None]
        ev_done = []
        for half in range(2):
            hs = slice(half * D, (half + 1) * D)
            S.do("vector", lambda e, hs=hs: e.tensor_add(
                out=xs[:, hs], in0=xp[:, hs],
                in1=xp[:, 2 * D + hs.start: 2 * D + hs.stop]), after=[t_x])
            for src in range(2, NCORE):
                S.do("vector", lambda e, hs=hs, src=src: e.tensor_add(
                    out=xs[:, hs], in0=xs[:, hs],
                    in1=xp[:, src * 2 * D + hs.start: src * 2 * D + hs.stop]))
            t_clip = S.do("vector", lambda e, hs=hs: e.tensor_scalar(
                out=xs[:, hs], in0=xs[:, hs], scalar1=-33000.0, scalar2=65000.0,
                op0=ALU.max, op1=ALU.min), sig="dve")
            t_h, t_act = emit_ln(S, xs[:, hs], xn[:, hs], lnscr,
                                 [t_clip, ln_free])
            t_t = S.do("scalar", lambda e, hs=hs: e.activation(
                out=xb[:, hs], in_=xn[:, hs], func=AF.Tanh), after=[t_h], sig="act")
            ln_free = t_t
            t_tanh[half] = t_t
            for k in range(DC):
                ps = tr_ps[k % 2]
                after = [t_t, t_eye]
                if len(ev_done) >= 2:
                    after.append(ev_done[-2])
                t_tr = S.do("tensor", lambda e, hs=hs, k=k, ps=ps: e.transpose(
                    ps[:, 0:128], xb[:, hs.start + k * 128: hs.start + (k + 1) * 128],
                    eye[:]), after=after, sig="pe")
                ev = S.do("vector", lambda e, half=half, k=k, ps=ps: e.tensor_copy(
                    out=XT[:, (half * DC + k) * 128:(half * DC + k + 1) * 128],
                    in_=ps[:, 0:128]), after=[t_tr], sig="dve")
                ev_done.append(ev)
        t_XT = S.tok("dve")

        y_free = [None, None]
        for n in range(EL):
            half, j = divmod(n, 4)
            ps = Y_ps[n % 2]
            base = (half * DC) * 128
            for k in range(DC):
                lhs = XT[:, (half * DC + k) * 128 + j * 32:
                         (half * DC + k) * 128 + (j + 1) * 32]
                for hh in range(2):
                    after = ()
                    if k == 0 and hh == 0:
                        after = [t_XT, ew_ready[n], y_free[n % 2], t_ones, t_eb]
                    S.do("tensor", lambda e, k=k, hh=hh, lhs=lhs, ps=ps, n=n: e.matmul(
                        ps[:, hh * 512:(hh + 1) * 512], lhs,
                        ew[n % 2][:, k * D + hh * 512: k * D + (hh + 1) * 512],
                        start=(k == 0), stop=False), after=after)
            t_pe = None
            for hh in range(2):
                t_pe = S.do("tensor", lambda e, n=n, hh=hh, ps=ps: e.matmul(
                    ps[:, hh * 512:(hh + 1) * 512], ones1[:, 0:32],
                    ebr[:, n * D + hh * 512: n * D + (hh + 1) * 512],
                    start=False, stop=True), sig="pe")
            t_ev = S.do("vector", lambda e, n=n, ps=ps: e.tensor_copy(
                out=Yev[:, n * D:(n + 1) * D], in_=ps[:]), after=[t_pe], sig="dve")
            y_free[n % 2] = t_ev
            if n + 2 < EL:
                for k in range(DC):
                    S.dma("gpsimd", ew[n % 2][:, k * D:(k + 1) * D],
                          ew_d[n + 2, k * 128:(k + 1) * 128, :],
                          after=[t_ev] if k == 0 else (), sig="ldw")
                ew_ready[n + 2] = S.tok("ldw")
            S.dma("sync", Y_d[n * PSL:(n + 1) * PSL, :],
                  Yev[0:PSL, n * D:(n + 1) * D], after=[t_ev], sig="dmaout")
        S.wait("sync", [S.tok("dmaout")])
        S.build(nc, st)
    return nc



def emit_ffn_D(S, nc, st, uid, C_d, Ytl_d, x_d, xout_d, tile_cb=None, psum_pp=2):
    sb = lambda name, shape, dt: st.enter_context(nc.sbuf_tensor(name, shape, dt))
    CT = sb(f"CT{uid}", [128, 16 * SLOTS], BF16)
    Ytl = sb(f"Ytl{uid}", [128, 16 * D], BF16)
    xt = [sb(f"xt{uid}{i}", [128, D], F32) for i in range(2)]
    x1t = [sb(f"x1t{uid}{i}", [128, D], F32) for i in range(2)]
    comb = [st.enter_context(nc.psum_tensor(f"comb{uid}{i}", [128, D], F32))
            for i in range(psum_pp)]
    for sc in range(16):
        S.dma("sync", CT[:, sc * SLOTS:(sc + 1) * SLOTS],
              C_d[0:TOK, sc * 128:(sc + 1) * 128], transpose=True, sig="ldct")
        S.dma("gpsimd", Ytl[:, sc * D:(sc + 1) * D],
              Ytl_d[sc * 128:(sc + 1) * 128, :], sig="ldyt")
    t_ct = S.tok("ldct")
    t_yt = S.tok("ldyt")
    x_free = [None, None]
    x1_free = [None, None]
    comb_free = [None] * psum_pp
    for m in range(NT):
        t_x = S.dma("sync", xt[m % 2][:], x_d[m * 128:(m + 1) * 128, :],
                    after=[x_free[m % 2]], sig="ldx" + uid)
        ps = comb[m % psum_pp]
        for sc in range(16):
            for hh in range(2):
                after = ()
                if sc == 0 and hh == 0:
                    after = [t_ct, t_yt, comb_free[m % psum_pp]]
                S.do("tensor", lambda e, m=m, sc=sc, hh=hh, ps=ps: e.matmul(
                    ps[:, hh * 512:(hh + 1) * 512],
                    CT[:, sc * SLOTS + m * 128: sc * SLOTS + (m + 1) * 128],
                    Ytl[:, sc * D + hh * 512: sc * D + (hh + 1) * 512],
                    start=(sc == 0), stop=(sc == 15)),
                    after=after, sig="pe" if (sc == 15 and hh == 1) else None)
        t_mm = S.tok("pe")
        t_x1 = S.do("vector", lambda e, m=m, ps=ps: e.scalar_tensor_tensor(
            out=x1t[m % 2][:], in0=ps[:], scalar=0.5, in1=xt[m % 2][:],
            op0=ALU.mult, op1=ALU.add),
            after=[t_mm, t_x, x_free[m % 2], x1_free[m % 2]])
        comb_free[m % psum_pp] = t_x1
        x_free[m % 2] = t_x1
        t_o = S.dma("sync", xout_d[m * 128:(m + 1) * 128, :], x1t[m % 2][:],
                    after=[t_x1], sig="dmaout")
        x1_free[m % 2] = t_o
        if tile_cb is not None:
            tile_cb(m, x1t[m % 2][:], t_x1)
    return S.tok("dmaout")


def build_L3():
    nc = bass.Bass("TRN2", target_bir_lowering=False)
    x_d = nc.dram_tensor("x", [TOK, D], F32, kind="ExternalInput").ap()
    C_d = nc.dram_tensor("C", [TOK, SLOTS], BF16, kind="ExternalInput").ap()
    Ytl_d = nc.dram_tensor("Ytl", [SLOTS, D], F32, kind="ExternalInput").ap()
    W_d = {w: nc.dram_tensor(w, [D, D], F32, kind="ExternalInput").ap()
           for w in ("WqT", "WkT", "WvT")}
    b_d = {w: nc.dram_tensor(w, [1, D], F32, kind="ExternalInput").ap()
           for w in ("bq", "bk", "bv")}
    cos_d = nc.dram_tensor("cos", [TOK, 32], F32, kind="ExternalInput").ap()
    sin_d = nc.dram_tensor("sin2", [TOK, 64], F32, kind="ExternalInput").ap()
    eye_d = nc.dram_tensor("eye", [128, 128], BF16, kind="ExternalInput").ap()
    eyef_d = nc.dram_tensor("eyef", [128, 128], F32, kind="ExternalInput").ap()
    x1_d = nc.dram_tensor("x1", [TOK, D], F32, kind="ExternalOutput").ap()
    q_d = nc.dram_tensor("q", [TOK, D], BF16, kind="ExternalOutput").ap()
    k_d = nc.dram_tensor("k", [TOK, D], BF16, kind="ExternalOutput").ap()
    v_d = nc.dram_tensor("v", [8, TOK, 130], BF16, kind="ExternalOutput").ap()
    nrm_d = nc.dram_tensor("norms", [NHEAD, 2], F32, kind="ExternalOutput").ap()
    bvscr_d = nc.dram_tensor("bvscr", [1, D], F32, kind="Internal").ap()

    S = Sched()
    with ExitStack() as st:
        sb = lambda name, shape, dt: st.enter_context(nc.sbuf_tensor(name, shape, dt))
        W = {w: sb(w + "_s", [128, DC * D], BF16) for w in ("WqT", "WkT", "WvT")}
        br = {w: sb(w + "_s", [1, D], BF16) for w in ("bq", "bk", "bv")}
        bvb = sb("bvb", [128, D], F32)
        ones1 = sb("ones1", [1, 128], BF16)
        eye = sb("eye_s", [128, 128], BF16)
        eyef = sb("eyef_s", [128, 128], F32)
        cos_sb = sb("cos_sb", [128, NT * 32], F32)
        sin_sb = sb("sin_sb", [128, NT * 64], F32)
        g_t = sb("g_t", [128, D], BF16)
        gT = sb("gT", [128, D], BF16)
        t1f = sb("t1f", [128, D], F32)
        t2f = sb("t2f", [128, D], F32)
        qk_bf = [sb(f"qkbf{i}", [128, D], BF16) for i in range(2)]
        v_t = [sb(f"vt{i}", [128, 16 * 65], BF16) for i in range(2)]
        sqs = sb("sqs", [128, D], F32)
        red16 = sb("red16", [128, NHEAD], F32)
        qn_run = sb("qn_run", [128, NHEAD], F32)
        kn_run = sb("kn_run", [128, NHEAD], F32)
        nrm_sb = sb("nrm_sb", [16, 2], F32)
        lnscr = {k2: sb("ln_" + k2, [128, 1], F32)[:] for k2 in ("mu", "var", "s2", "rstd")}
        lnscr["xsq"] = sb("ln_xsq", [128, D], F32)[:]
        lnscr["eps"] = sb("ln_eps", [128, 1], F32)[:]
        S.do("vector", lambda e: e.memset(lnscr["eps"], EPS))
        t_ones = S.do("vector", lambda e: e.memset(ones1[:], 1.0))
        S.do("vector", lambda e: e.memset(qn_run[:], 0.0))
        t_nrm0 = S.do("vector", lambda e: e.memset(kn_run[:], 0.0))

        t_eye = S.dma("sync", eye[:], eye_d, sig="ld")
        S.dma("sync", eyef[:], eyef_d, sig="ld")
        for m in range(NT):
            S.dma("sync", cos_sb[:, m * 32:(m + 1) * 32],
                  cos_d[m * 128:(m + 1) * 128, :], sig="ld")
            S.dma("sync", sin_sb[:, m * 64:(m + 1) * 64],
                  sin_d[m * 128:(m + 1) * 128, :], sig="ld")
        BIAS = {"WqT": "bq", "WkT": "bk", "WvT": "bv"}
        for w in ("WqT", "WkT", "WvT"):
            for kk in range(DC):
                S.dma("gpsimd", W[w][:, kk * D:(kk + 1) * D],
                      W_d[w][kk * 128:(kk + 1) * 128, :], sig="ldw")
            S.dma("gpsimd", br[BIAS[w]][:], b_d[BIAS[w]], sig="ldw")
        t_bv = S.dma("sync", bvscr_d, b_d["bv"], sig="ld")
        S.dma("sync", bvb[:], bvscr_d.broadcast_to([128, D]), after=[t_bv], sig="ld")
        t_ld = S.tok("ld")
        t_w = S.tok("ldw")

        tr_ps = [st.enter_context(nc.psum_tensor(f"trq{i}", [128, 1024], BF16))
                 for i in range(2)]
        qkv_ps = [st.enter_context(nc.psum_tensor(f"qkv{i}", [128, D], F32))
                  for i in range(2)]

        state = dict(trps_free=[None, None], qkv_free=[None, None],
                     qk_free=[None, None], vt_free=[None, None], ln_free=None,
                     gT_free=None)

        def qkv_tile(m, x1_ap, t_x1):
            t_g, t_act = emit_ln(S, x1_ap, g_t[:], lnscr,
                                 [t_x1, state["ln_free"]])
            state["ln_free"] = t_act
            ev_toks = []
            for kk in range(DC):
                ps = tr_ps[kk % 2]
                after = [t_g, t_eye] +                     ([ev_toks[kk - 2]] if kk >= 2 else [state["trps_free"][kk % 2]])
                if kk == 0:
                    after.append(state["gT_free"])
                t_tr = S.do("tensor", lambda e, kk=kk, ps=ps: e.transpose(
                    ps[:, 0:128], g_t[:, kk * 128:(kk + 1) * 128], eye[:]),
                    after=after, sig="pe")
                ev = S.do("vector", lambda e, kk=kk, ps=ps: e.tensor_copy(
                    out=gT[:, kk * 128:(kk + 1) * 128], in_=ps[:, 0:128]),
                    after=[t_tr])
                ev_toks.append(ev)
            state["trps_free"] = [ev_toks[DC - 2], ev_toks[DC - 1]]
            t_gT = ev_toks[-1]
            for pi, w in enumerate(("WqT", "WkT", "WvT")):
                ps = qkv_ps[pi % 2]
                for kk in range(DC):
                    for hh in range(2):
                        after = ()
                        if kk == 0 and hh == 0:
                            after = [t_gT, state["qkv_free"][pi % 2],
                                     t_w if m == 0 else None,
                                     t_ones if m == 0 else None]
                        S.do("tensor", lambda e, w=w, kk=kk, hh=hh, ps=ps: e.matmul(
                            ps[:, hh * 512:(hh + 1) * 512],
                            gT[:, kk * 128:(kk + 1) * 128],
                            W[w][:, kk * D + hh * 512: kk * D + (hh + 1) * 512],
                            start=(kk == 0), stop=False), after=after)
                t_p = None
                for hh in range(2):
                    t_p = S.do("tensor", lambda e, w=w, hh=hh, ps=ps: e.matmul(
                        ps[:, hh * 512:(hh + 1) * 512], ones1[:],
                        br[{"WqT": "bq", "WkT": "bk", "WvT": "bv"}[w]][:, hh * 512:(hh + 1) * 512],
                        start=False, stop=True), sig="pe")
                if w == "WvT":
                    vt = v_t[m % 2]
                    t_ms = S.do("vector", lambda e, vt=vt: e.memset(vt[:], 1.0),
                                after=[state["vt_free"][m % 2]])
                    t_v = S.do("vector", lambda e, vt=vt, ps=ps: e.tensor_copy(
                        out=vt[:].rearrange("p (h g) -> p h g", g=65)[:, :, 0:64],
                        in_=ps[:].rearrange("p (h g) -> p h g", g=64)),
                        after=[t_p, t_ms])
                    for hg in range(8):
                        S.dma("sync", v_d[hg, m * 128:(m + 1) * 128, :],
                              vt[:, hg * 130:(hg + 1) * 130], after=[t_v],
                              sig="dmaout")
                    state["vt_free"][m % 2] = t_v
                    state["qkv_free"][pi % 2] = t_v
                else:
                    qb = qk_bf[pi]
                    Pv = ps[:].rearrange("p (h s f) -> p h s f", h=NHEAD, s=2)
                    cosv = cos_sb[:, m * 32:(m + 1) * 32].unsqueeze(1).unsqueeze(2) \
                        .broadcast_to([128, NHEAD, 2, 32])
                    sinv = sin_sb[:, m * 64:(m + 1) * 64] \
                        .rearrange("p (s f) -> p s f", s=2).unsqueeze(1) \
                        .broadcast_to([128, NHEAD, 2, 32])
                    t1v = t1f[:].rearrange("p (h s f) -> p h s f", h=NHEAD, s=2)
                    t2v = t2f[:].rearrange("p (h s f) -> p h s f", h=NHEAD, s=2)
                    S.do("vector", lambda e, Pv=Pv, cosv=cosv, t1v=t1v: e.tensor_tensor(
                        out=t1v, in0=Pv, in1=cosv, op=ALU.mult), after=[t_p, t_ld])
                    S.do("vector", lambda e, Pv=Pv, sinv=sinv, t2v=t2v: e.tensor_tensor(
                        out=t2v[:, :, 0, :], in0=Pv[:, :, 1, :],
                        in1=sinv[:, :, 0, :], op=ALU.mult))
                    S.do("vector", lambda e, Pv=Pv, sinv=sinv, t2v=t2v: e.tensor_tensor(
                        out=t2v[:, :, 1, :], in0=Pv[:, :, 0, :],
                        in1=sinv[:, :, 1, :], op=ALU.mult))
                    t_qb = S.do("vector", lambda e, qb=qb: e.tensor_add(
                        out=qb[:], in0=t1f[:], in1=t2f[:]),
                        after=[state["qk_free"][pi]])
                    state["qkv_free"][pi % 2] = t_qb
                    t_sq = S.do("gpsimd", lambda e, qb=qb: e.tensor_tensor(
                        out=sqs[:], in0=qb[:], in1=qb[:], op=ALU.mult), after=[t_qb])
                    S.do("vector", lambda e: e.reduce_sum(
                        out=red16[:], in_=sqs[:].rearrange("p (h f) -> p h f", h=NHEAD),
                        axis=AX.X), after=[t_sq])
                    run = qn_run if w == "WqT" else kn_run
                    S.do("vector", lambda e, run=run: e.tensor_max(
                        out=run[:], in0=run[:], in1=red16[:]),
                        after=[t_nrm0] if m == 0 else ())
                    out_d = q_d if w == "WqT" else k_d
                    t_dq = S.dma("sync", out_d[m * 128:(m + 1) * 128, :], qb[:],
                                 after=[t_qb], sig="dmaout")
                    state["qk_free"][pi] = t_dq

        t_dout = emit_ffn_D(S, nc, st, "d", C_d, Ytl_d, x_d, x1_d,
                            tile_cb=qkv_tile, psum_pp=1)
        t_last = [S.tok("dve"), S.tok("pool"), S.tok("pe")]
        for i, run in enumerate((qn_run, kn_run)):
            t_tr = S.do("tensor", lambda e, run=run: e.transpose(
                qkv_ps[1][0:16, 0:128], run[:], eyef[:]),
                after=t_last + [S.tok("dve")], sig="pe")
            S.do("vector", lambda e, i=i: e.reduce_max(
                out=nrm_sb[:, i:i + 1], in_=qkv_ps[1][0:16, 0:128], axis=AX.X),
                after=[t_tr])
        t_n = S.tok("dve")
        S.dma("sync", nrm_d, nrm_sb[:], after=[t_n], sig="dmaout")
        S.wait("sync", [S.tok("dmaout")])
        S.build(nc, st)
    return nc


def build_L6():
    nc = bass.Bass("TRN2", target_bir_lowering=False)
    x_d = nc.dram_tensor("x", [TOK, D], F32, kind="ExternalInput").ap()
    C_d = nc.dram_tensor("C", [TOK, SLOTS], BF16, kind="ExternalInput").ap()
    Ytl_d = nc.dram_tensor("Ytl", [SLOTS, D], F32, kind="ExternalInput").ap()
    x3_d = nc.dram_tensor("x3", [TOK, D], F32, kind="ExternalOutput").ap()
    S = Sched()
    with ExitStack() as st:
        t_out = emit_ffn_D(S, nc, st, "f", C_d, Ytl_d, x_d, x3_d, psum_pp=2)
        S.wait("sync", [S.tok("dmaout")])
        S.build(nc, st)
    return nc



def build_L4():
    nc = bass.Bass("TRN2", target_bir_lowering=False)
    x1_d = nc.dram_tensor("x1", [TOK, D], F32, kind="ExternalInput").ap()
    q_d = nc.dram_tensor("q", [TOK, D], BF16, kind="ExternalInput").ap()
    K_d = nc.dram_tensor("K", [SEQ, D], BF16, kind="ExternalInput").ap()
    V_d = nc.dram_tensor("V", [8, SEQ, 130], BF16, kind="ExternalInput").ap()
    qn_d = nc.dram_tensor("qn", [1, NHEAD], F32, kind="ExternalInput").ap()
    kn4_d = nc.dram_tensor("kn4", [1, 4 * NHEAD], F32, kind="ExternalInput").ap()
    WoT_d = nc.dram_tensor("WoT", [D, D], F32, kind="ExternalInput").ap()
    bo_d = nc.dram_tensor("bo", [1, D], F32, kind="ExternalInput").ap()
    phiT_d = nc.dram_tensor("phiT", [D, SLOTS], F32, kind="ExternalInput").ap()
    phib_d = nc.dram_tensor("phib", [1, SLOTS], F32, kind="ExternalInput").ap()
    eye_d = nc.dram_tensor("eye", [128, 128], BF16, kind="ExternalInput").ap()
    x2_d = nc.dram_tensor("x2", [TOK, D], F32, kind="ExternalOutput").ap()
    Xp2_d = nc.dram_tensor("Xp2", [SLOTS, D], F32, kind="ExternalOutput").ap()
    C2_d = nc.dram_tensor("C2", [TOK, SLOTS], BF16, kind="ExternalOutput").ap()
    aTdbg_d = (nc.dram_tensor("aTdbg", [128, DC * TOK], BF16,
                              kind="ExternalOutput").ap() if L4_DEBUG else None)
    nmdbg_d = (nc.dram_tensor("nmdbg", [128, NHEAD], F32,
                              kind="ExternalOutput").ap() if L4_DEBUG else None)
    rscr_d = [nc.dram_tensor(f"rscr{i}", [1, TOK // 2], F32, kind="Internal").ap()
              for i in range(2)]

    S = Sched()
    with ExitStack() as st:
        sb = lambda name, shape, dt: st.enter_context(nc.sbuf_tensor(name, shape, dt))
        negmb = sb("negmb", [128, NHEAD], F32)
        aT = sb("aT", [128, DC * TOK], BF16)

        nrq = sb("nrq", [128, NHEAD], F32)
        nrk = sb("nrk", [128, 4 * NHEAD], F32)
        t_qn = S.dma("sync", nrq[:], qn_d.broadcast_to([128, NHEAD]), sig="ld")
        t_kn = S.dma("sync", nrk[:], kn4_d.broadcast_to([128, 4 * NHEAD]), sig="ld")
        knv = nrk[:].rearrange("p (c h) -> p h c", c=4)
        S.do("vector", lambda e: e.reduce_max(
            out=negmb[:], in_=knv, axis=AX.X), after=[t_qn, t_kn])
        t_mul = S.do("vector", lambda e: e.tensor_mul(
            out=negmb[:], in0=negmb[:], in1=nrq[:]))
        S.do("scalar", lambda e: e.activation(
            out=negmb[:], in_=negmb[:], func=AF.Ln), after=[t_mul])
        t_ex = S.do("scalar", lambda e: e.activation(
            out=negmb[:], in_=negmb[:], func=AF.Exp, scale=0.5))
        t_nb = S.do("vector", lambda e: e.tensor_scalar_mul(
            out=negmb[:], in0=negmb[:], scalar1=-0.125), after=[t_ex])

        with ExitStack() as st2:
            sb2 = lambda name, shape, dt: st2.enter_context(nc.sbuf_tensor(name, shape, dt))
            kT = [sb2(f"kT{i}", [128, SEQ], BF16) for i in range(2)]
            qT = [sb2(f"qT{i}", [128, TOK // 2], BF16) for i in range(2)]
            vsb = [sb2(f"vsb{i}", [128, 64 * 130], BF16) for i in range(2)]
            E = [sb2(f"E{i}", [128, TOK // 2], BF16) for i in range(4)]
            rrow = [sb2(f"rrow{i}", [1, TOK // 2], F32) for i in range(2)]
            rb = [sb2(f"rb{i}", [64, TOK // 2], F32) for i in range(2)]
            s_ps = [st2.enter_context(nc.psum_tensor(f"sps{i}", [128, TOK // 2], F32))
                    for i in range(2)]
            av_ps = [st2.enter_context(nc.psum_tensor(f"av{i}", [65, TOK // 2], F32))
                     for i in range(2)]
            QW = TOK // 2

            buf_free = [None, None]
            av_free = [None, None]
            exp_done = [None, None]
            rrow_free = [None, None]
            rb_free = [None, None]
            di = 0
            seg_list = [(qt, hg) for qt in range(2) for hg in range(8)]
            for si, (qt, hg) in enumerate(seg_list):
                bi = si % 2
                t_kT = S.dma("sync", kT[bi][:],
                             K_d[0:SEQ, hg * 128:(hg + 1) * 128],
                             after=[buf_free[bi]], transpose=True, sig="ldk")
                t_qT = S.dma("sync", qT[bi][:],
                             q_d[qt * QW:(qt + 1) * QW, hg * 128:(hg + 1) * 128],
                             after=[buf_free[bi]], transpose=True, sig="ldk")
                t_v = S.dma("gpsimd",
                            vsb[bi][:].rearrange("p (kb f) -> p kb f", f=130),
                            V_d[hg].rearrange("(kb p) f -> p kb f", p=128),
                            after=[buf_free[bi]], sig="ldv")
                t_loads = [t_kT, t_qT, t_v, t_nb]
                for h2 in range(2):
                    h = 2 * hg + h2
                    avp = di % 2
                    av = av_ps[avp]
                    exp_tok = [None] * 64
                    av_tok = [None] * 64

                    def emit_scores(kb, h2=h2, bi=bi, h=h):
                        sp = s_ps[kb % 2]
                        after0 = [exp_done[kb % 2]]
                        if kb < 2:
                            after0 = after0 + t_loads
                        S.do("tensor", lambda e, kb=kb, sp=sp, h2=h2, bi=bi: e.matmul(
                            sp[:, 0:512],
                            kT[bi][h2 * 64:(h2 + 1) * 64, kb * 128:(kb + 1) * 128],
                            qT[bi][h2 * 64:(h2 + 1) * 64, 0:512],
                            start=True, stop=True), after=after0)
                        t_s = S.do("tensor", lambda e, kb=kb, sp=sp, h2=h2, bi=bi: e.matmul(
                            sp[:, 512:1024],
                            kT[bi][h2 * 64:(h2 + 1) * 64, kb * 128:(kb + 1) * 128],
                            qT[bi][h2 * 64:(h2 + 1) * 64, 512:1024],
                            start=True, stop=True), sig="pe")
                        t_e = S.do("scalar", lambda e, kb=kb, sp=sp, h=h: e.activation(
                            out=E[kb % 4][:], in_=sp[:], func=AF.Exp,
                            scale=0.125, bias=negmb[:, h:h + 1]),
                            after=[t_s], chain=False)
                        exp_done[kb % 2] = t_e
                        exp_tok[kb] = t_e

                    def emit_av(kb, h2=h2, bi=bi, av=av, avp=avp):
                        for hh in range(2):
                            after1 = []
                            if kb == 0 and hh == 0:
                                after1 = [exp_tok[kb], av_free[avp]]
                            elif hh == 0:
                                after1 = [exp_tok[kb]]
                            t = S.do("tensor", lambda e, kb=kb, hh=hh, av=av, h2=h2, bi=bi: e.matmul(
                                av[:, hh * 512:(hh + 1) * 512],
                                vsb[bi][:].rearrange("p (kb f) -> p kb f", f=130)
                                [:, kb, h2 * 65:(h2 + 1) * 65],
                                E[kb % 4][:, hh * 512:(hh + 1) * 512],
                                start=(kb == 0), stop=(kb == 63)),
                                after=after1,
                                sig="pe" if hh == 1 else None)
                            if hh == 1:
                                av_tok[kb] = t

                    emit_scores(0)
                    emit_scores(1)
                    for kb in range(2, 64):
                        emit_scores(kb)
                        emit_av(kb - 2)
                    emit_av(62)
                    emit_av(63)
                    t_av = S.tok("pe")
                    rp = di % 2
                    t_rc = S.do("vector", lambda e, av=av, rp=rp: e.reciprocal(
                        out=rrow[rp][:], in_=av[64:65, :]),
                        after=[t_av, rrow_free[rp]])
                    t_r1 = S.dma("sync", rscr_d[rp], rrow[rp][:], after=[t_rc],
                                 sig="ldr")
                    rrow_free[rp] = t_r1
                    t_r2 = S.dma("sync", rb[rp][:],
                                 rscr_d[rp].broadcast_to([64, QW]),
                                 after=[t_r1, rb_free[rp]], sig="ldr")
                    t_aT = S.do("vector", lambda e, av=av, rp=rp, h2=h2, hg=hg, qt=qt:
                                e.tensor_tensor(
                                    out=aT[h2 * 64:(h2 + 1) * 64,
                                           hg * TOK + qt * QW: hg * TOK + (qt + 1) * QW],
                                    in0=av[0:64, :], in1=rb[rp][:], op=ALU.mult),
                                after=[t_r2])
                    rb_free[rp] = t_aT
                    av_free[avp] = t_aT
                    di += 1
                buf_free[bi] = S.tok("pe")
            t_attn = [S.tok("dve"), S.tok("pe")]
            if L4_DEBUG:
                S.dma("sync", aTdbg_d, aT[:], after=t_attn, sig="dmaout")
                S.dma("sync", nmdbg_d, negmb[:], after=t_attn, sig="dmaout")

        with ExitStack() as st3:
            sb3 = lambda name, shape, dt: st3.enter_context(nc.sbuf_tensor(name, shape, dt))
            WoT = sb3("WoT_s", [128, DC * D], BF16)
            bor = sb3("bor", [1, D], BF16)
            ones1o = sb3("ones1o", [1, 128], BF16)
            x1t = [sb3(f"x1t{i}", [128, D], F32) for i in range(2)]
            x2t = [sb3(f"x2t{i}", [128, D], F32) for i in range(2)]
            o_ps = [st3.enter_context(nc.psum_tensor(f"ops{i}", [128, D], F32))
                    for i in range(2)]
            attn_done = t_attn + [S.tok("act")]
            t_ones = S.do("vector", lambda e: e.memset(ones1o[:], 1.0),
                          after=attn_done)
            for kk in range(DC):
                S.dma("gpsimd", WoT[:, kk * D:(kk + 1) * D],
                      WoT_d[kk * 128:(kk + 1) * 128, :], after=attn_done, sig="ldw")
            S.dma("gpsimd", bor[:], bo_d, after=attn_done, sig="ldw")
            t_wo = S.tok("ldw")
            x1_free = [None, None]
            x2_free = [None, None]
            o_free = [None, None]
            x2_out = [None] * NT
            for m in range(NT):
                t_x1 = S.dma("sync", x1t[m % 2][:], x1_d[m * 128:(m + 1) * 128, :],
                             after=[x1_free[m % 2]], sig="ldx")
                ps = o_ps[m % 2]
                for kk in range(DC):
                    for hh in range(2):
                        after = ()
                        if kk == 0 and hh == 0:
                            after = t_attn + [t_wo, o_free[m % 2]] + \
                                ([t_ones] if m == 0 else [])
                        S.do("tensor", lambda e, m=m, kk=kk, hh=hh, ps=ps: e.matmul(
                            ps[:, hh * 512:(hh + 1) * 512],
                            aT[:, kk * TOK + m * 128: kk * TOK + (m + 1) * 128],
                            WoT[:, kk * D + hh * 512: kk * D + (hh + 1) * 512],
                            start=(kk == 0), stop=False), after=after)
                t_p = None
                for hh in range(2):
                    t_p = S.do("tensor", lambda e, hh=hh, ps=ps: e.matmul(
                        ps[:, hh * 512:(hh + 1) * 512], ones1o[:],
                        bor[:, hh * 512:(hh + 1) * 512], start=False, stop=True),
                        sig="pe")
                t_x2 = S.do("vector", lambda e, m=m, ps=ps: e.tensor_add(
                    out=x2t[m % 2][:], in0=ps[:], in1=x1t[m % 2][:]),
                    after=[t_p, t_x1, x2_free[m % 2]])
                o_free[m % 2] = t_x2
                x1_free[m % 2] = t_x2
                t_o = S.dma("sync", x2_d[m * 128:(m + 1) * 128, :], x2t[m % 2][:],
                            after=[t_x2], sig="dmaout")
                x2_free[m % 2] = t_o
                x2_out[m] = t_o

        oproj_done = [S.tok("pe"), S.tok("dve")]
        sb4 = lambda name, shape, dt: st.enter_context(nc.sbuf_tensor(name, shape, dt))
        xt = [sb4(f"xtf{i}", [128, D], F32) for i in range(3)]
        phiT = sb4("phiT_s", [128, DC * SLOTS], BF16)
        phib = sb4("phib_s", [1, SLOTS], BF16)
        ones1 = sb4("ones1f", [1, 128], BF16)
        eye = sb4("eye_s", [128, 128], BF16)
        lnscr = {k2: sb4("lnf_" + k2, [128, 1], F32)[:] for k2 in ("mu", "var", "s2", "rstd")}
        lnscr["xsq"] = sb4("lnf_xsq", [128, D], F32)[:]
        lnscr["eps"] = sb4("lnf_eps", [128, 1], F32)[:]
        S.do("vector", lambda e: e.memset(lnscr["eps"], EPS))
        b = dict(uid="f4", phiT=phiT[:], phib=phib[:], ones1=ones1[:], eye=eye[:],
                 lnscr=lnscr, nring=2, nxev=1)
        b["t_ones"] = S.do("vector", lambda e: e.memset(ones1[:], 1.0))
        t_eye = S.dma("sync", eye[:], eye_d, after=oproj_done, sig="ld")
        b["t_wk"] = [S.dma("gpsimd", phiT[:, kk * SLOTS:(kk + 1) * SLOTS],
                           phiT_d[kk * 128:(kk + 1) * 128, :], after=oproj_done,
                           sig="ldw")
                     for kk in range(DC)]
        b["t_wb"] = S.dma("gpsimd", phib[:], phib_d, after=oproj_done, sig="ldw")
        xf_free = [None] * 3
        xf_tok = [None] * NT

        def x_load(m):
            xf_tok[m] = S.dma("sync", xt[m % 3][:], x2_d[m * 128:(m + 1) * 128, :],
                              after=[xf_free[m % 3], x2_out[m]] +
                              (oproj_done if m < 2 else []), sig="ldxf")
        b["x_load"] = x_load
        b["x_ap"] = lambda m: (xt[m % 3][:], [xf_tok[m]])

        def x_done(m, t_h):
            xf_free[m % 3] = t_h
        b["x_done"] = x_done
        emit_ffn_AB(S, nc, st, b, C2_d, Xp2_d)
        S.wait("sync", [S.tok("dmaout")])
        S.build(nc, st)
    return nc


def _prep_common(Wq, bq, Wk, bk, Wv, bv, Wo, bo, phi_w, phi_b, expert_w, expert_b):
    c = {}
    c["phiT"] = np.ascontiguousarray(phi_w.T)
    c["phib"] = np.ascontiguousarray(phi_b[None, :])
    c["WqT"] = np.ascontiguousarray(Wq.T)
    c["WkT"] = np.ascontiguousarray(Wk.T)
    c["WvT"] = np.ascontiguousarray(Wv.T)
    c["WoT"] = np.ascontiguousarray(Wo.T)
    c["bq"] = np.ascontiguousarray(bq[None, :])
    c["bk"] = np.ascontiguousarray(bk[None, :])
    c["bv"] = np.ascontiguousarray(bv[None, :])
    c["bo"] = np.ascontiguousarray(bo[None, :])
    c["ew"] = np.ascontiguousarray(expert_w)
    c["eb"] = np.ascontiguousarray(expert_b)
    c["eye"] = np.eye(128, dtype=ml_dtypes.bfloat16)
    inv = 10000.0 ** (-np.arange(0, DH, 2, dtype=np.float64) / DH)
    cos_l, sin_l = [], []
    for core in range(NCORE):
        pos = (core % 4) * TOK + np.arange(TOK, dtype=np.float64)
        fr = np.outer(pos, inv)
        cos_l.append(np.cos(fr).astype(np.float32))
        s = np.sin(fr).astype(np.float32)
        sin_l.append(np.ascontiguousarray(np.concatenate([-s, s], axis=1)))
    c["cos"] = cos_l
    c["sin2"] = sin_l
    return c


def run_expert_launch(Xp_list, c):
    nc2 = build_L2()
    in2 = []
    for cc in range(NCORE):
        xp8 = np.stack([Xp_list[src][cc * EL * PSL:(cc + 1) * EL * PSL]
                        for src in range(NCORE)])
        in2.append({"Xp8": np.ascontiguousarray(xp8),
                    "ew": np.ascontiguousarray(c["ew"][cc * EL:(cc + 1) * EL]),
                    "eb": np.ascontiguousarray(c["eb"][cc * EL:(cc + 1) * EL]),
                    "eye": c["eye"]})
    r2 = run_bass_kernel_spmd(nc2, in2, core_ids=list(range(NCORE)))
    return np.ascontiguousarray(
        np.concatenate([r2.results[cc]["Ytl"] for cc in range(NCORE)], axis=0))


def kernel(L0_attn_x, L0_attn_Wq, L0_attn_bq, L0_attn_Wk, L0_attn_bk, L0_attn_Wv,
           L0_attn_bv, L0_attn_Wo, L0_attn_bo, L0_ffn_phi_w, L0_ffn_phi_b,
           L0_ffn_expert_w, L0_ffn_expert_b):
    c = _prep_common(L0_attn_Wq, L0_attn_bq, L0_attn_Wk, L0_attn_bk, L0_attn_Wv,
                     L0_attn_bv, L0_attn_Wo, L0_attn_bo, L0_ffn_phi_w,
                     L0_ffn_phi_b, L0_ffn_expert_w, L0_ffn_expert_b)
    x0 = np.ascontiguousarray(np.asarray(L0_attn_x, dtype=np.float32)
                              .reshape(NCORE * TOK, D))
    eyef = np.eye(128, dtype=np.float32)
    cores = list(range(NCORE))

    nc1 = build_L1()
    in1 = [{"x": np.ascontiguousarray(x0[cc * TOK:(cc + 1) * TOK]),
            "phiT": c["phiT"], "phib": c["phib"], "eye": c["eye"]}
           for cc in cores]
    r1 = run_bass_kernel_spmd(nc1, in1, core_ids=cores)
    Xp1 = [r1.results[cc]["Xp"] for cc in cores]
    C1 = [r1.results[cc]["C"] for cc in cores]

    Ytl1 = run_expert_launch(Xp1, c)

    nc3 = build_L3()
    in3 = [{"x": in1[cc]["x"], "C": C1[cc], "Ytl": Ytl1,
            "WqT": c["WqT"], "WkT": c["WkT"], "WvT": c["WvT"],
            "bq": c["bq"], "bk": c["bk"], "bv": c["bv"],
            "cos": c["cos"][cc], "sin2": c["sin2"][cc],
            "eye": c["eye"], "eyef": eyef} for cc in cores]
    r3 = run_bass_kernel_spmd(nc3, in3, core_ids=cores)

    nc4 = build_L4()
    in4 = []
    for cc in cores:
        grp = range(4 * (cc // 4), 4 * (cc // 4) + 4)
        Kf = np.ascontiguousarray(
            np.concatenate([r3.results[g]["k"] for g in grp], axis=0))
        Vf = np.ascontiguousarray(
            np.concatenate([r3.results[g]["v"] for g in grp], axis=1))
        kn4 = np.ascontiguousarray(
            np.concatenate([r3.results[g]["norms"][:, 1] for g in grp])[None, :])
        qn = np.ascontiguousarray(r3.results[cc]["norms"][:, 0][None, :])
        in4.append({"x1": r3.results[cc]["x1"], "q": r3.results[cc]["q"],
                    "K": Kf, "V": Vf, "qn": qn, "kn4": kn4,
                    "WoT": c["WoT"], "bo": c["bo"],
                    "phiT": c["phiT"], "phib": c["phib"], "eye": c["eye"]})
    r4 = run_bass_kernel_spmd(nc4, in4, core_ids=cores)
    Xp2 = [r4.results[cc]["Xp2"] for cc in cores]
    C2 = [r4.results[cc]["C2"] for cc in cores]

    Ytl2 = run_expert_launch(Xp2, c)

    nc6 = build_L6()
    in6 = [{"x": r4.results[cc]["x2"], "C": C2[cc], "Ytl": Ytl2} for cc in cores]
    r6 = run_bass_kernel_spmd(nc6, in6, core_ids=cores)
    out = np.concatenate([r6.results[cc]["x3"] for cc in cores], axis=0)
    return out.reshape(2, SEQ, D)
